# version 04 (final)
# baseline (speedup 1.0000x reference)
import math
import contextlib
import numpy as np
import ml_dtypes

import concourse.bass as bass
import concourse.mybir as mybir
from concourse.bass_utils import run_bass_kernel_spmd


def _install_memtrim():
    import concourse.bass2jax as b2j
    if getattr(b2j, "_memtrim_installed", False):
        return
    orig = b2j.run_bass_via_pjrt

    def wrapped(*a, **k):
        r = orig(*a, **k)
        import gc as _gc
        import ctypes as _ct
        _gc.collect()
        try:
            _ct.CDLL("libc.so.6").malloc_trim(0)
        except Exception:
            pass
        return r

    b2j.run_bass_via_pjrt = wrapped
    b2j._memtrim_installed = True

BFNP = ml_dtypes.bfloat16
F32 = mybir.dt.float32
BF16 = mybir.dt.bfloat16
AF = mybir.ActivationFunctionType
ALU = mybir.AluOpType
AX = mybir.AxisListType

S = 2048
D = 2048
NT = S // 128
NC = D // 128
LN_EPS = 1e-5
ISQ = float(1.0 / math.sqrt(128.0))


class KB:

    KEY = {"ve": "cs", "se": "cs", "ge": "cs", "dl": "dma", "dg": "dma", "st": "dma",
           "pe": "pe", "cs": "cs", "dma": "dma"}

    def __init__(self):
        self.nc = bass.Bass("TRN2", target_bir_lowering=False)
        self.es = contextlib.ExitStack()
        self.sem = {}
        self.cnt = {}
        for name in ("pe", "cs", "dma"):
            self.sem[name] = self.es.enter_context(self.nc.semaphore(name))
            self.cnt[name] = 0
        self._uid = 0
        self._lastw = {e: {} for e in ("tensor", "vector", "scalar", "gpsimd", "sync")}

    def _nm(self, name):
        self._uid += 1
        return f"{name}_{self._uid}"

    def sb(self, name, shape, dt):
        return self.es.enter_context(self.nc.sbuf_tensor(self._nm(name), shape, dt))

    def ps(self, name, shape, dt=F32):
        return self.es.enter_context(self.nc.psum_tensor(self._nm(name), shape, dt))

    def scope(self):
        return contextlib.ExitStack()

    def sb_in(self, scope, name, shape, dt, side=None):
        kw = {"side": side} if side else {}
        return scope.enter_context(self.nc.sbuf_tensor(self._nm(name), shape, dt, **kw))

    def ps_in(self, scope, name, shape, dt=F32):
        return scope.enter_context(self.nc.psum_tensor(self._nm(name), shape, dt))

    def din(self, name, shape, dt):
        return self.nc.dram_tensor(name, shape, dt, kind="ExternalInput").ap()

    def dout(self, name, shape, dt):
        return self.nc.dram_tensor(name, shape, dt, kind="ExternalOutput").ap()

    def dtmp(self, name, shape, dt):
        return self.nc.dram_tensor(name, shape, dt, kind="Internal").ap()

    def wait(self, eng, *names):
        e = getattr(self.nc, eng)
        lw = self._lastw[eng]
        for n in set(self.KEY[n] for n in names):
            c = self.cnt[n]
            if c > 0 and lw.get(n, -1) < c:
                e.wait_ge(self.sem[n], c)
                lw[n] = c

    def _inc(self, instr, name, k):
        name = self.KEY[name]
        instr.then_inc(self.sem[name], k)
        self.cnt[name] += k

    def t(self, meth, *a, **k):
        instr = getattr(self.nc.tensor, meth)(*a, **k)
        self._inc(instr, "pe", 1)
        return instr

    def v(self, meth, *a, **k):
        nodrain = k.pop("nodrain", False)
        instr = getattr(self.nc.vector, meth)(*a, **k)
        self._inc(instr, "ve", 1)
        if not nodrain:
            self.nc.vector.drain()
        return instr

    def s(self, meth, *a, **k):
        nodrain = k.pop("nodrain", meth in ("copy", "mul"))
        instr = getattr(self.nc.scalar, meth)(*a, **k)
        self._inc(instr, "se", 1)
        if not nodrain:
            self.nc.scalar.drain()
        return instr

    def g(self, meth, *a, **k):
        instr = getattr(self.nc.gpsimd, meth)(*a, **k)
        self._inc(instr, "ge", 1)
        self.nc.gpsimd.drain()
        return instr

    def load(self, out, in_, queue="sync"):
        if queue == "sync":
            self._inc(self.nc.sync.dma_start(out=out, in_=in_), "dl", 16)
        else:
            self._inc(self.nc.gpsimd.dma_start(out=out, in_=in_), "dg", 16)

    def loadT(self, out, in_):
        self._inc(self.nc.sync.dma_start(out=out, in_=in_, transpose=True), "dl", 16)

    def store(self, out, in_, wait_on=("ve",)):
        self.wait("sync", *wait_on)
        self._inc(self.nc.sync.dma_start(out=out, in_=in_), "st", 16)

    def dump(self, name, tile, shape, dt, wait_on=("ve", "se", "pe")):
        ap = self.dout(name, shape, dt)
        self.store(ap, tile, wait_on=wait_on)

    def fence(self):
        for eng in ("tensor", "vector", "scalar", "gpsimd", "sync"):
            self.wait(eng, "pe", "cs", "dma")

    def finish(self):
        self.nc.sync.wait_ge(self.sem["dma"], self.cnt["dma"])

    def close(self):
        self.es.close()



def make_consts():
    c = {}
    c["IDENT_BF"] = np.eye(128, dtype=BFNP)
    c["IDENT_F32"] = np.eye(128, dtype=np.float32)
    c["I16"] = np.eye(16, dtype=BFNP)
    c["ONES_ROW"] = np.ones((1, 128), dtype=BFNP)
    c["ONES_COL"] = np.ones((128, 1), dtype=BFNP)
    iu = np.triu(np.ones((128, 128), dtype=np.float32), k=1)
    c["MASKA"] = (-1e10) * iu
    c["MASKB"] = (-1e30) * iu
    hd = 64
    inv = 1.0 / (10000.0 ** (np.arange(0, hd, 2, dtype=np.float32) / hd))
    t = np.arange(S, dtype=np.float32)
    fr = np.repeat(t[:, None] * inv[None, :], 2, axis=1)
    cos = np.cos(fr).T.astype(np.float32)
    sin = np.sin(fr).T.astype(np.float32)
    c["COS"] = np.concatenate([cos, cos], axis=0)
    c["SIN"] = np.concatenate([sin, sin], axis=0)
    P = np.zeros((128, 128), dtype=np.float32)
    for g in range(2):
        for i in range(hd // 2):
            P[g * hd + 2 * i, g * hd + 2 * i + 1] = -1.0
            P[g * hd + 2 * i + 1, g * hd + 2 * i] = 1.0
    c["PROT_T"] = P.T.copy().astype(BFNP)
    dec = np.exp(-np.arange(S, dtype=np.float32) / float(S))
    c["DECAY_ROW"] = dec.reshape(1, S).copy()
    c["DECAY_TM"] = dec.reshape(NT, 128).T.copy()
    c["DECAY_BC"] = np.tile(dec.reshape(1, S), (128, 1))
    c["ONES_ROW_F32"] = np.ones((1, 128), dtype=np.float32)
    c["ONES_COL_F32"] = np.ones((128, 1), dtype=np.float32)
    c["EPS8_COL"] = np.full((128, 1), 1e-8, dtype=np.float32)
    return c


CONST_SPECS = {
    "IDENT_BF": ([128, 128], BF16), "IDENT_F32": ([128, 128], F32),
    "I16": ([16, 16], BF16), "ONES_ROW": ([1, 128], BF16), "ONES_COL": ([128, 1], BF16),
    "MASKA": ([128, 128], F32), "MASKB": ([128, 128], F32),
    "COS": ([128, S], F32), "SIN": ([128, S], F32), "PROT_T": ([128, 128], BF16),
    "DECAY_ROW": ([1, S], F32), "DECAY_TM": ([128, NT], F32),
    "DECAY_BC": ([128, S], F32), "ONES_ROW_F32": ([1, 128], F32),
    "ONES_COL_F32": ([128, 1], F32), "EPS8_COL": ([128, 1], F32),
}


def emit_consts(kb, names):
    t = {}
    specs = {k: CONST_SPECS[k] for k in names}
    for name, (shape, dt) in specs.items():
        ap = kb.din(name, shape, dt)
        tile = kb.sb("c_" + name, shape, dt)
        kb.load(tile[:], ap)
        t[name] = tile
    for eng in ("tensor", "vector", "scalar", "gpsimd"):
        kb.wait(eng, "dl")
    return t


def emit_ln(kb, C, x_in, xn_fm, xtile, scr):
    nc = kb.nc
    mean, rstd, var = scr["mean"], scr["rstd"], scr["var"]
    sumc, sqc = scr["sumc"], scr["sqc"]
    xsq = scr["xsq"]
    xn_tm = scr["xn_tm"]
    pst = scr["pst"]
    for tt in range(NT):
        kb.wait("sync", "pe")
        kb.wait("sync", "ve", "se")
        kb.load(xtile[:], x_in[tt * 128:(tt + 1) * 128, :])
        kb.wait("vector", "dl")
        kb.v("tensor_reduce", out=sumc[:], in_=xtile[:], axis=AX.X, op=ALU.add)
        kb.wait("scalar", "dl")
        kb.wait("scalar", "ve")
        kb.s("activation", out=xsq[:], in_=xtile[:], func=AF.Square, accum_out=sqc[:])
        kb.wait("vector", "se")
        kb.v("tensor_scalar", out=mean[:], in0=sumc[:], scalar1=1.0 / D, scalar2=None, op0=ALU.mult)
        kb.v("tensor_tensor", out=var[:], in0=mean[:], in1=mean[:], op=ALU.mult)
        kb.v("scalar_tensor_tensor", out=var[:], in0=sqc[:], scalar=1.0 / D, in1=var[:], op0=ALU.mult, op1=ALU.subtract)
        kb.v("tensor_scalar", out=var[:], in0=var[:], scalar1=LN_EPS, scalar2=None, op0=ALU.add)
        kb.wait("scalar", "ve")
        kb.s("activation", out=rstd[:], in_=var[:], func=AF.Sqrt)
        kb.wait("vector", "se")
        kb.v("reciprocal", out=rstd[:], in_=rstd[:])
        kb.wait("vector", "pe")
        kb.v("tensor_scalar", out=xn_tm[:], in0=xtile[:], scalar1=mean[:], scalar2=rstd[:], op0=ALU.subtract, op1=ALU.mult)
        kb.wait("tensor", "ve")
        for c4 in range(4):
            kb.wait("tensor", "se")
            for j in range(4):
                c = c4 * 4 + j
                kb.t("transpose", out=pst[:, j * 128:(j + 1) * 128], in_=xn_tm[:, c * 128:(c + 1) * 128], identity=C["IDENT_BF"][:])
            kb.wait("scalar", "pe")
            for j in range(4):
                c = c4 * 4 + j
                kb.s("copy", out=xn_fm[:, c * S + tt * 128: c * S + (tt + 1) * 128], in_=pst[:, j * 128:(j + 1) * 128])


def ln_scratch(kb, scope, tag=""):
    scr = {
        "mean": kb.sb_in(scope, f"ln_mean{tag}", [128, 1], F32),
        "rstd": kb.sb_in(scope, f"ln_rstd{tag}", [128, 1], F32),
        "var": kb.sb_in(scope, f"ln_var{tag}", [128, 1], F32),
        "sumc": kb.sb_in(scope, f"ln_sum{tag}", [128, 1], F32),
        "sqc": kb.sb_in(scope, f"ln_sq{tag}", [128, 1], F32),
        "xsq": kb.sb_in(scope, f"ln_xsq{tag}", [128, D], F32),
        "xn_tm": kb.sb_in(scope, f"ln_xn_tm{tag}", [128, D], BF16),
        "pst": kb.ps_in(scope, f"ln_pst{tag}", [128, 512], BF16),
    }
    return scr



DEBUG = False


def emit_l0_attn_a(kb, C, x_in, x_out, W):
    nc = kb.nc
    with kb.scope() as sc:
        z_set = kb.sb_in(sc, "aa_z", [128, 16 * S], BF16, side="right")
        with kb.scope() as xnsc:
            xn_fm = kb.sb_in(xnsc, "aa_xn_fm", [128, NC * S], BF16)
            with kb.scope() as lnsc:
                xtile = kb.sb_in(lnsc, "aa_xtile", [128, D], F32)
                scr = ln_scratch(kb, lnsc, "aa")
                emit_ln(kb, C, x_in, xn_fm, xtile, scr)
            kb.fence()
            hsc = xnsc
            w_nat = kb.sb_in(hsc, "aa_wnat", [128, D], BF16)
            wT = kb.sb_in(hsc, "aa_wT", [128, D], BF16)
            q_h = kb.sb_in(hsc, "aa_qh", [128, S], BF16)
            k_h = kb.sb_in(hsc, "aa_kh", [128, S], BF16)
            v4 = kb.sb_in(hsc, "aa_v4", [128, 4 * NT * 128], BF16)
            w4T = kb.sb_in(hsc, "aa_w4T", [128, NC * 512], BF16)
            p_sb = kb.sb_in(hsc, "aa_p", [128, S], BF16)
            pt_sb = kb.sb_in(hsc, "aa_pt", [128, S], BF16)
            nmax = kb.sb_in(hsc, "aa_nmax", [128, 1], F32)
            bias = kb.sb_in(hsc, "aa_bias", [128, 1], F32)
            rs = kb.sb_in(hsc, "aa_rs", [128, 1], F32)
            rinv = kb.sb_in(hsc, "aa_rinv", [128, 1], F32)
            ps_s = kb.ps_in(hsc, "aa_ps_s", [128, S], F32)
            ps_tr = kb.ps_in(hsc, "aa_ps_tr", [128, 1024], BF16)
            ps_z = kb.ps_in(hsc, "aa_ps_z", [128, 512], F32)
            ps_v = kb.ps_in(hsc, "aa_ps_v", [128, 512], F32)

            def transpose_weight(src_hbm_head, dst_wT):
                kb.wait("gpsimd", "pe")
                kb.load(w_nat[:], src_hbm_head, queue="gpsimd")
                kb.wait("tensor", "dg")
                for c4 in range(4):
                    kb.wait("tensor", "se")
                    for j in range(4):
                        c = c4 * 4 + j
                        kb.t("transpose", out=ps_tr[:, j * 128:(j + 1) * 128], in_=w_nat[:, c * 128:(c + 1) * 128], identity=C["IDENT_BF"][:])
                    kb.wait("scalar", "pe")
                    for j in range(4):
                        c = c4 * 4 + j
                        kb.s("copy", out=dst_wT[:, c * 128:(c + 1) * 128], in_=ps_tr[:, j * 128:(j + 1) * 128])

            def proj_fm(dst, wTt):
                for n0 in range(0, S, 512):
                    kb.wait("tensor", "se", "ve")
                    for c in range(NC):
                        kb.t("matmul", out=ps_v[:, 0:512], lhsT=wTt[:, c * 128:(c + 1) * 128], rhs=xn_fm[:, c * S + n0: c * S + n0 + 512], start=(c == 0), stop=(c == NC - 1))
                    kb.wait("scalar", "pe")
                    kb.s("copy", out=dst[:, n0:n0 + 512], in_=ps_v[:, 0:512])

            for g4 in range(4):
                for j in range(4):
                    h = g4 * 4 + j
                    kb.wait("gpsimd", "pe")
                    kb.load(w_nat[:], W["W_V"][h, :, :], queue="gpsimd")
                    kb.wait("tensor", "dg")
                    for c4 in range(4):
                        kb.wait("tensor", "se")
                        for jj in range(4):
                            c = c4 * 4 + jj
                            kb.t("transpose", out=ps_tr[:, jj * 128:(jj + 1) * 128], in_=w_nat[:, c * 128:(c + 1) * 128], identity=C["IDENT_BF"][:])
                        kb.wait("scalar", "pe")
                        for jj in range(4):
                            c = c4 * 4 + jj
                            kb.s("copy", out=w4T[:, c * 512 + j * 128: c * 512 + (j + 1) * 128], in_=ps_tr[:, jj * 128:(jj + 1) * 128])
                for tt in range(NT):
                    kb.wait("tensor", "se", "ve")
                    for c in range(NC):
                        kb.t("matmul", out=ps_v[:, 0:512], lhsT=xn_fm[:, c * S + tt * 128: c * S + (tt + 1) * 128], rhs=w4T[:, c * 512:(c + 1) * 512], start=(c == 0), stop=(c == NC - 1))
                    kb.wait("scalar", "pe")
                    for j in range(4):
                        kb.s("copy", out=v4[:, (j * NT + tt) * 128:(j * NT + tt + 1) * 128], in_=ps_v[:, j * 128:(j + 1) * 128])

                for j in range(4):
                    h = g4 * 4 + j
                    transpose_weight(W["W_Q"][h, :, :], wT)
                    if DEBUG and h == 0:
                        kb.dump("dbg_wqT", wT[:], [128, D], BF16)
                    proj_fm(q_h, wT)
                    if DEBUG and h == 0:
                        kb.dump("dbg_qh", q_h[:], [128, S], BF16)
                    transpose_weight(W["W_K"][h, :, :], wT)
                    proj_fm(k_h, wT)
                    if DEBUG and h == 0:
                        kb.dump("dbg_kh", k_h[:], [128, S], BF16)
                        kb.dump("dbg_v40", v4[:, 0:NT * 128], [128, NT * 128], BF16)
                        kb.dump("dbg_xn0", xn_fm[:, 0:S], [128, S], BF16)
                    for qt in range(NT):
                        kmax = (qt + 1) * 128
                        kb.wait("tensor", "se", "ve")
                        for n0 in range(0, kmax, 512):
                            nn = min(512, kmax - n0)
                            kb.t("matmul", out=ps_s[:, n0:n0 + nn], lhsT=q_h[:, qt * 128:(qt + 1) * 128], rhs=k_h[:, n0:n0 + nn], start=True, stop=True)
                        kb.wait("vector", "pe")
                        kb.v("tensor_tensor", out=ps_s[:, qt * 128:kmax], in0=ps_s[:, qt * 128:kmax], in1=C["MASKA"][:], op=ALU.add)
                        kb.v("tensor_reduce", out=nmax[:], in_=ps_s[:, 0:kmax], axis=AX.X, op=ALU.max, negate=True)
                        kb.v("tensor_scalar", out=bias[:], in0=nmax[:], scalar1=ISQ, scalar2=None, op0=ALU.mult, nodrain=True)
                        kb.wait("scalar", "ve")
                        kb.s("activation", out=p_sb[:, 0:kmax], in_=ps_s[:, 0:kmax], func=AF.Exp, scale=ISQ, bias=bias[:], accum_out=rs[:])
                        kb.wait("vector", "se")
                        kb.v("reciprocal", out=rinv[:], in_=rs[:])
                        kb.v("tensor_scalar", out=p_sb[:, 0:kmax], in0=p_sb[:, 0:kmax], scalar1=rinv[:], scalar2=None, op0=ALU.mult, nodrain=True)
                        kb.wait("tensor", "ve")
                        for k4 in range(0, qt + 1, 4):
                            kjs = list(range(k4, min(k4 + 4, qt + 1)))
                            kb.wait("tensor", "se")
                            for idx, kc in enumerate(kjs):
                                kb.t("transpose", out=ps_tr[:, idx * 128:(idx + 1) * 128], in_=p_sb[:, kc * 128:(kc + 1) * 128], identity=C["IDENT_BF"][:])
                            kb.wait("scalar", "pe")
                            for idx, kc in enumerate(kjs):
                                kb.s("copy", out=pt_sb[:, kc * 128:(kc + 1) * 128], in_=ps_tr[:, idx * 128:(idx + 1) * 128])
                        kb.wait("tensor", "se")
                        for kc in range(qt + 1):
                            kb.t("matmul", out=ps_z[:, 0:128], lhsT=v4[:, (j * NT + kc) * 128:(j * NT + kc + 1) * 128], rhs=pt_sb[:, kc * 128:(kc + 1) * 128], start=(kc == 0), stop=(kc == qt))
                        kb.wait("scalar", "pe")
                        kb.s("copy", out=z_set[:, (h * NT + qt) * 128:(h * NT + qt + 1) * 128], in_=ps_z[:, 0:128])
                        if DEBUG and h == 0 and qt == 3:
                            kb.dump("dbg_p3", p_sb[:, 0:512], [128, 512], BF16)
                            kb.dump("dbg_rs", rs[:], [128, 1], F32)

        kb.fence()
        with kb.scope() as osc:
            woT = kb.sb_in(osc, "aa_woT", [128, 16 * S], BF16)
            w_nat2 = kb.sb_in(osc, "aa_wnat2", [128, D], BF16)
            xres = kb.sb_in(osc, "aa_xres", [128, D], F32)
            outb = kb.sb_in(osc, "aa_outb", [128, D], F32)
            ps_o = kb.ps_in(osc, "aa_ps_o", [128, S], F32)
            ps_t2 = kb.ps_in(osc, "aa_ps_t2", [128, 1024], BF16)
            for dch in range(NC):
                kb.wait("gpsimd", "pe")
                kb.load(w_nat2[:], W["W_O"][dch * 128:(dch + 1) * 128, :], queue="gpsimd")
                kb.wait("tensor", "dg")
                for c4 in range(4):
                    kb.wait("tensor", "se")
                    for jj in range(4):
                        f = c4 * 4 + jj
                        kb.t("transpose", out=ps_t2[:, jj * 128:(jj + 1) * 128], in_=w_nat2[:, f * 128:(f + 1) * 128], identity=C["IDENT_BF"][:])
                    kb.wait("scalar", "pe")
                    for jj in range(4):
                        f = c4 * 4 + jj
                        kb.s("copy", out=woT[:, f * S + dch * 128: f * S + (dch + 1) * 128], in_=ps_t2[:, jj * 128:(jj + 1) * 128])
            for tt in range(NT):
                kb.wait("sync", "ve")
                kb.load(xres[:], x_in[tt * 128:(tt + 1) * 128, :])
                kb.wait("tensor", "ve", "se")
                for h in range(16):
                    for n0 in range(0, S, 512):
                        kb.t("matmul", out=ps_o[:, n0:n0 + 512], lhsT=z_set[:, (h * NT + tt) * 128:(h * NT + tt + 1) * 128], rhs=woT[:, h * S + n0: h * S + n0 + 512], start=(h == 0), stop=(h == 15))
                kb.wait("vector", "pe", "dl")
                kb.v("tensor_tensor", out=outb[:], in0=ps_o[:, 0:S], in1=xres[:], op=ALU.add)
                kb.store(x_out[tt * 128:(tt + 1) * 128, :], outb[:], wait_on=("ve",))
                kb.wait("vector", "st")
                kb.wait("vector", "st")



def emit_l1_attn_b(kb, C, x_in, x_out, W):
    nc = kb.nc
    with kb.scope() as sc:
        att_set = kb.sb_in(sc, "lb_att", [128, 16 * S], BF16, side="right")
        with kb.scope() as xnsc:
            xn_fm = kb.sb_in(xnsc, "lb_xn_fm", [128, NC * S], BF16)
            with kb.scope() as lnsc:
                xtile = kb.sb_in(lnsc, "lb_xtile", [128, D], F32)
                scr = ln_scratch(kb, lnsc, "lb")
                emit_ln(kb, C, x_in, xn_fm, xtile, scr)
            kb.fence()
            hsc = xnsc
            b_rows = kb.sb_in(hsc, "lb_brows", [1, 3 * D], BF16)
            kb.load(b_rows[0:1, 0:D], W["bk"].rearrange("(one d) -> one d", one=1), queue="gpsimd")
            kb.load(b_rows[0:1, D:2 * D], W["bv"].rearrange("(one d) -> one d", one=1), queue="gpsimd")
            kb.load(b_rows[0:1, 2 * D:3 * D], W["bo"].rearrange("(one d) -> one d", one=1), queue="gpsimd")
            bq_row = kb.sb_in(hsc, "lb_bqrow", [1, D], BF16)
            kb.load(bq_row[:], W["bq"].rearrange("(one d) -> one d", one=1), queue="gpsimd")
            decay_bc = kb.sb_in(hsc, "lb_decbc", [128, S], F32)
            kb.load(decay_bc[:], W["DECAY_BC"], queue="sync")
            w_nat = kb.sb_in(hsc, "lb_wnat", [128, D], BF16)
            wT = kb.sb_in(hsc, "lb_wT", [128, D], BF16)
            wkT = kb.sb_in(hsc, "lb_wkT", [128, D], BF16)
            wvT = kb.sb_in(hsc, "lb_wvT", [128, D], BF16)
            q_h = kb.sb_in(hsc, "lb_qh", [128, S], BF16)
            k_tm = kb.sb_in(hsc, "lb_ktm", [128, S], BF16)
            v_tm = kb.sb_in(hsc, "lb_vtm", [128, S], BF16)
            kv_sb = kb.sb_in(hsc, "lb_kv", [128, 128], BF16)
            ksum = kb.sb_in(hsc, "lb_ksum", [128, 1], BF16)
            bq_col = kb.sb_in(hsc, "lb_bqcol", [128, 1], F32)
            zcol = kb.sb_in(hsc, "lb_zcol", [128, 1], F32)
            att_bf = kb.sb_in(hsc, "lb_attbf", [128, 128], BF16)
            ps_a = kb.ps_in(hsc, "lb_ps_a", [128, 512], F32)
            ps_q = kb.ps_in(hsc, "lb_ps_q", [128, 512], F32)
            ps_kv = kb.ps_in(hsc, "lb_ps_kv", [128, 512], F32)
            ps_tr = kb.ps_in(hsc, "lb_ps_tr", [128, 1024], BF16)

            def transpose_w(src_rows, dst):
                kb.wait("gpsimd", "pe")
                kb.load(w_nat[:], src_rows, queue="gpsimd")
                kb.wait("tensor", "dg")
                for c4 in range(4):
                    kb.wait("tensor", "se")
                    for j in range(4):
                        c = c4 * 4 + j
                        kb.t("transpose", out=ps_tr[:, j * 128:(j + 1) * 128], in_=w_nat[:, c * 128:(c + 1) * 128], identity=C["IDENT_BF"][:])
                    kb.wait("scalar", "pe")
                    for j in range(4):
                        c = c4 * 4 + j
                        kb.s("copy", out=dst[:, c * 128:(c + 1) * 128], in_=ps_tr[:, j * 128:(j + 1) * 128])

            for h in range(16):
                hs = slice(h * 128, (h + 1) * 128)
                transpose_w(W["Wk"][hs, :], wkT)
                transpose_w(W["Wv"][hs, :], wvT)
                transpose_w(W["Wq"][hs, :], wT)
                kb.wait("tensor", "ve")
                kb.t("matmul", out=ps_kv[:, 0:1], lhsT=bq_row[0:1, hs], rhs=C["ONES_COL"][0:1, 0:1], start=True, stop=True)
                kb.wait("vector", "pe")
                kb.v("tensor_copy", out=bq_col[:], in_=ps_kv[:, 0:1])
                for tt in range(NT):
                    kb.wait("tensor", "se", "ve")
                    for c in range(NC):
                        kb.t("matmul", out=ps_a[:, 0:128], lhsT=xn_fm[:, c * S + tt * 128: c * S + (tt + 1) * 128], rhs=wkT[:, c * 128:(c + 1) * 128], start=(c == 0), stop=False)
                    kb.t("matmul", out=ps_a[:, 0:128], lhsT=C["ONES_ROW"][:], rhs=b_rows[0:1, hs], start=False, stop=True)
                    kb.wait("vector", "pe")
                    kb.v("tensor_scalar", out=k_tm[:, tt * 128:(tt + 1) * 128], in0=ps_a[:, 0:128], scalar1=C["DECAY_TM"][:, tt:tt + 1], scalar2=0.0, op0=ALU.mult, op1=ALU.max)
                    kb.wait("tensor", "ve")
                    for c in range(NC):
                        kb.t("matmul", out=ps_a[:, 0:128], lhsT=xn_fm[:, c * S + tt * 128: c * S + (tt + 1) * 128], rhs=wvT[:, c * 128:(c + 1) * 128], start=(c == 0), stop=False)
                    kb.t("matmul", out=ps_a[:, 0:128], lhsT=C["ONES_ROW"][:], rhs=b_rows[0:1, D + h * 128:D + (h + 1) * 128], start=False, stop=True)
                    kb.wait("scalar", "pe")
                    kb.s("copy", out=v_tm[:, tt * 128:(tt + 1) * 128], in_=ps_a[:, 0:128])
                for n0 in range(0, S, 512):
                    kb.wait("tensor", "se", "ve")
                    for c in range(NC):
                        kb.t("matmul", out=ps_q[:, 0:512], lhsT=wT[:, c * 128:(c + 1) * 128], rhs=xn_fm[:, c * S + n0: c * S + n0 + 512], start=(c == 0), stop=(c == NC - 1))
                    kb.wait("vector", "pe")
                    kb.v("tensor_scalar", out=ps_q[:, 0:512], in0=ps_q[:, 0:512], scalar1=bq_col[:], scalar2=0.0, op0=ALU.add, op1=ALU.max)
                    kb.v("tensor_tensor", out=q_h[:, n0:n0 + 512], in0=ps_q[:, 0:512], in1=decay_bc[:, n0:n0 + 512], op=ALU.mult)
                kb.wait("tensor", "se", "ve")
                for tt in range(NT):
                    kb.t("matmul", out=ps_kv[:, 0:128], lhsT=k_tm[:, tt * 128:(tt + 1) * 128], rhs=v_tm[:, tt * 128:(tt + 1) * 128], start=(tt == 0), stop=(tt == NT - 1))
                for tt in range(NT):
                    kb.t("matmul", out=ps_kv[:, 128:129], lhsT=k_tm[:, tt * 128:(tt + 1) * 128], rhs=C["ONES_COL"][:], start=(tt == 0), stop=(tt == NT - 1))
                kb.wait("scalar", "pe")
                kb.s("copy", out=kv_sb[:], in_=ps_kv[:, 0:128])
                kb.s("copy", out=ksum[:], in_=ps_kv[:, 128:129])
                if DEBUG and h == 0:
                    kb.dump("dbg_ktm", k_tm[:], [128, S], BF16)
                    kb.dump("dbg_vtm", v_tm[:], [128, S], BF16)
                    kb.dump("dbg_qh", q_h[:], [128, S], BF16)
                    kb.dump("dbg_kv", kv_sb[:], [128, 128], BF16)
                    kb.dump("dbg_ksum", ksum[:], [128, 1], BF16)
                    kb.dump("dbg_xn0", xn_fm[:, 0:S], [128, S], BF16)
                for tt in range(NT):
                    kb.wait("tensor", "se", "ve")
                    kb.t("matmul", out=ps_a[:, 0:128], lhsT=q_h[:, tt * 128:(tt + 1) * 128], rhs=kv_sb[:], start=True, stop=True)
                    kb.t("matmul", out=ps_a[:, 128:129], lhsT=q_h[:, tt * 128:(tt + 1) * 128], rhs=ksum[:], start=True, stop=True)
                    kb.wait("vector", "pe")
                    kb.v("tensor_scalar", out=zcol[:], in0=ps_a[:, 128:129], scalar1=1e-6, scalar2=None, op0=ALU.add)
                    kb.v("reciprocal", out=zcol[:], in_=zcol[:])
                    kb.v("tensor_scalar", out=att_bf[:], in0=ps_a[:, 0:128], scalar1=zcol[:], scalar2=None, op0=ALU.mult)
                    kb.wait("tensor", "ve", "se")
                    kb.t("transpose", out=ps_tr[:, 0:128], in_=att_bf[:], identity=C["IDENT_BF"][:])
                    kb.wait("scalar", "pe")
                    kb.s("copy", out=att_set[:, (h * NT + tt) * 128:(h * NT + tt + 1) * 128], in_=ps_tr[:, 0:128])

        kb.fence()
        with kb.scope() as osc:
            woT = kb.sb_in(osc, "lb_woT", [128, 16 * S], BF16)
            w_nat2 = kb.sb_in(osc, "lb_wnat2", [128, D], BF16)
            bo_row = kb.sb_in(osc, "lb_borow", [1, D], BF16)
            kb.load(bo_row[:], W["bo"].rearrange("(one d) -> one d", one=1), queue="gpsimd")
            xres = kb.sb_in(osc, "lb_xres", [128, D], F32)
            outb = kb.sb_in(osc, "lb_outb", [128, D], F32)
            ps_o = kb.ps_in(osc, "lb_ps_o", [128, S], F32)
            ps_t2 = kb.ps_in(osc, "lb_ps_t2", [128, 1024], BF16)
            for dch in range(NC):
                kb.wait("gpsimd", "pe")
                kb.load(w_nat2[:], W["Wo"][dch * 128:(dch + 1) * 128, :], queue="gpsimd")
                kb.wait("tensor", "dg")
                for c4 in range(4):
                    kb.wait("tensor", "se")
                    for jj in range(4):
                        f = c4 * 4 + jj
                        kb.t("transpose", out=ps_t2[:, jj * 128:(jj + 1) * 128], in_=w_nat2[:, f * 128:(f + 1) * 128], identity=C["IDENT_BF"][:])
                    kb.wait("scalar", "pe")
                    for jj in range(4):
                        f = c4 * 4 + jj
                        kb.s("copy", out=woT[:, f * S + dch * 128: f * S + (dch + 1) * 128], in_=ps_t2[:, jj * 128:(jj + 1) * 128])
            for tt in range(NT):
                kb.wait("sync", "ve")
                kb.load(xres[:], x_in[tt * 128:(tt + 1) * 128, :])
                kb.wait("tensor", "ve", "se", "dg")
                for h in range(16):
                    for n0 in range(0, S, 512):
                        kb.t("matmul", out=ps_o[:, n0:n0 + 512], lhsT=att_set[:, (h * NT + tt) * 128:(h * NT + tt + 1) * 128], rhs=woT[:, h * S + n0: h * S + n0 + 512], start=(h == 0), stop=False)
                for n0 in range(0, S, 512):
                    kb.t("matmul", out=ps_o[:, n0:n0 + 512], lhsT=C["ONES_ROW"][:], rhs=bo_row[0:1, n0:n0 + 512], start=False, stop=True)
                kb.wait("vector", "pe", "dl")
                kb.v("tensor_tensor", out=outb[:], in0=ps_o[:, 0:S], in1=xres[:], op=ALU.add)
                kb.store(x_out[tt * 128:(tt + 1) * 128, :], outb[:], wait_on=("ve",))
                kb.wait("vector", "st")



LAMBDA_INIT = 0.8 - 0.6 * math.exp(-0.3 * 12)


def emit_l0_attn_b(kb, C, x_in, x_out, W):
    nc = kb.nc
    with kb.scope() as sc:
        att_set = kb.sb_in(sc, "ab_att", [128, 16 * S], BF16, side="right")
        with kb.scope() as xnsc:
            xn_fm = kb.sb_in(xnsc, "ab_xn_fm", [128, NC * S], BF16)
            with kb.scope() as lnsc:
                xtile = kb.sb_in(lnsc, "ab_xtile", [128, D], F32)
                scr = ln_scratch(kb, lnsc, "ab")
                emit_ln(kb, C, x_in, xn_fm, xtile, scr)
            kb.fence()
            hsc = xnsc
            lrow = kb.sb_in(hsc, "ab_lrow", [1, 4 * 64], F32)
            for i, nm in enumerate(["lambda_q1", "lambda_k1", "lambda_q2", "lambda_k2"]):
                kb.load(lrow[0:1, i * 64:(i + 1) * 64], W[nm].rearrange("(one d) -> one d", one=1), queue="sync")
            lscr = kb.sb_in(hsc, "ab_lscr", [1, 64], F32)
            lvals = kb.sb_in(hsc, "ab_lvals", [1, 4], F32)
            neglam_col = kb.sb_in(hsc, "ab_neglam", [128, 1], F32)
            kb.wait("vector", "dl")
            kb.v("tensor_tensor", out=lscr[:], in0=lrow[0:1, 0:64], in1=lrow[0:1, 64:128], op=ALU.mult)
            kb.v("tensor_reduce", out=lvals[0:1, 0:1], in_=lscr[:], axis=AX.X, op=ALU.add)
            kb.v("tensor_tensor", out=lscr[:], in0=lrow[0:1, 128:192], in1=lrow[0:1, 192:256], op=ALU.mult)
            kb.v("tensor_reduce", out=lvals[0:1, 1:2], in_=lscr[:], axis=AX.X, op=ALU.add)
            kb.wait("scalar", "ve")
            kb.s("activation", out=lvals[0:1, 2:3], in_=lvals[0:1, 0:1], func=AF.Exp)
            kb.s("activation", out=lvals[0:1, 3:4], in_=lvals[0:1, 1:2], func=AF.Exp)
            kb.wait("vector", "se")
            kb.v("tensor_tensor", out=lvals[0:1, 0:1], in0=lvals[0:1, 3:4], in1=lvals[0:1, 2:3], op=ALU.subtract)
            kb.v("tensor_scalar", out=lvals[0:1, 0:1], in0=lvals[0:1, 0:1], scalar1=LAMBDA_INIT, scalar2=None, op0=ALU.subtract)
            kb.wait("tensor", "ve")
            ps_sm = kb.ps_in(hsc, "ab_ps_sm", [128, 512], F32)
            kb.t("matmul", out=ps_sm[:, 0:1], lhsT=C["ONES_ROW_F32"][:], rhs=lvals[0:1, 0:1], start=True, stop=True)
            kb.wait("vector", "pe")
            kb.v("tensor_copy", out=neglam_col[:], in_=ps_sm[:, 0:1])
            lnrow = kb.sb_in(hsc, "ab_lnrow", [1, 256], F32)
            kb.load(lnrow[0:1, 0:128], W["ln_w"].rearrange("(one d) -> one d", one=1), queue="sync")
            kb.load(lnrow[0:1, 128:256], W["ln_b"].rearrange("(one d) -> one d", one=1), queue="sync")
            wbc = kb.sb_in(hsc, "ab_wbc", [128, 128], F32)
            bbc = kb.sb_in(hsc, "ab_bbc", [128, 128], F32)
            kb.wait("vector", "dl")
            kb.v("tensor_scalar", out=lnrow[:], in0=lnrow[:], scalar1=float(1.0 - LAMBDA_INIT), scalar2=None, op0=ALU.mult)
            kb.wait("tensor", "ve")
            kb.t("matmul", out=ps_sm[:, 0:128], lhsT=C["ONES_ROW_F32"][:], rhs=lnrow[0:1, 0:128], start=True, stop=True)
            kb.t("matmul", out=ps_sm[:, 128:256], lhsT=C["ONES_ROW_F32"][:], rhs=lnrow[0:1, 128:256], start=True, stop=True)
            kb.wait("vector", "pe")
            kb.v("tensor_copy", out=wbc[:], in_=ps_sm[:, 0:128])
            kb.v("tensor_copy", out=bbc[:], in_=ps_sm[:, 128:256])

            wcol = kb.sb_in(hsc, "ab_wcol", [128, D], BF16)
            q_h = kb.sb_in(hsc, "ab_qh", [128, S], BF16)
            k_h = kb.sb_in(hsc, "ab_kh", [128, S], BF16)
            v_fm = kb.sb_in(hsc, "ab_vfm", [128, S], BF16)
            v_tm = kb.sb_in(hsc, "ab_vtm", [128, S], BF16)
            raw = kb.sb_in(hsc, "ab_raw", [128, S], BF16)
            rope_t1 = kb.sb_in(hsc, "ab_rt1", [128, S], F32)
            p_sb = kb.sb_in(hsc, "ab_p", [128, S], BF16)
            pt_sb = kb.sb_in(hsc, "ab_pt", [128, S], BF16)
            z1_sb = kb.sb_in(hsc, "ab_z1", [128, 128], F32)
            cc = kb.sb_in(hsc, "ab_cc", [128, 128], F32)
            csq = kb.sb_in(hsc, "ab_csq", [128, 128], F32)
            nmax = kb.sb_in(hsc, "ab_nmax", [128, 1], F32)
            bias = kb.sb_in(hsc, "ab_bias", [128, 1], F32)
            rs = kb.sb_in(hsc, "ab_rs", [128, 1], F32)
            rinv = kb.sb_in(hsc, "ab_rinv", [128, 1], F32)
            ssq = kb.sb_in(hsc, "ab_ssq", [128, 1], F32)
            rstd = kb.sb_in(hsc, "ab_rstd", [128, 1], F32)
            ps_s = kb.ps_in(hsc, "ab_ps_s", [128, S], F32)
            ps_tr = kb.ps_in(hsc, "ab_ps_tr", [128, 1024], BF16)
            ps_z = kb.ps_in(hsc, "ab_ps_z", [128, 512], F32)

            def col_load(dst, W_ap, cout):
                src = W_ap[:, cout * 128:(cout + 1) * 128].rearrange("(c p) j -> p c j", p=128)
                kb.wait("gpsimd", "pe")
                kb.load(dst[:].rearrange("p (c j) -> p c j", j=128), src, queue="gpsimd")

            def proj_chunk(dst_raw, h):
                for n0 in range(0, S, 512):
                    kb.wait("tensor", "se", "ve", "dg")
                    for c in range(NC):
                        kb.t("matmul", out=ps_s[:, n0:n0 + 512], lhsT=wcol[:, c * 128:(c + 1) * 128], rhs=xn_fm[:, c * S + n0: c * S + n0 + 512], start=(c == 0), stop=(c == NC - 1))
                kb.wait("scalar", "pe")
                for n0 in range(0, S, 512):
                    kb.s("copy", out=dst_raw[:, n0:n0 + 512], in_=ps_s[:, n0:n0 + 512])

            def rope(dst, src):
                kb.wait("tensor", "se", "ve")
                for n0 in range(0, S, 512):
                    kb.t("matmul", out=ps_s[:, n0:n0 + 512], lhsT=C["PROT_T"][:], rhs=src[:, n0:n0 + 512], start=True, stop=True)
                kb.wait("vector", "pe")
                kb.v("tensor_tensor", out=rope_t1[:], in0=ps_s[:, 0:S], in1=C["SIN"][:], op=ALU.mult)
                kb.v("tensor_tensor", out=dst[:], in0=src[:], in1=C["COS"][:], op=ALU.mult)
                kb.v("tensor_tensor", out=dst[:], in0=dst[:], in1=rope_t1[:], op=ALU.add)

            for h in range(16):
                col_load(wcol, W["Wq"], h)
                proj_chunk(raw, h)
                rope(q_h, raw)
                col_load(wcol, W["Wk"], h)
                proj_chunk(raw, h)
                rope(k_h, raw)
                col_load(wcol, W["Wv"], h)
                proj_chunk(v_fm, h)
                for t4 in range(4):
                    kb.wait("tensor", "se", "ve")
                    for j in range(4):
                        tt = t4 * 4 + j
                        kb.t("transpose", out=ps_tr[:, j * 128:(j + 1) * 128], in_=v_fm[:, tt * 128:(tt + 1) * 128], identity=C["IDENT_BF"][:])
                    kb.wait("scalar", "pe")
                    for j in range(4):
                        tt = t4 * 4 + j
                        kb.s("copy", out=v_tm[:, tt * 128:(tt + 1) * 128], in_=ps_tr[:, j * 128:(j + 1) * 128])
                for qt in range(NT):
                    kmax = (qt + 1) * 128
                    for br in range(2):
                        pb = slice(64 * br, 64 * (br + 1))
                        kb.wait("tensor", "se", "ve")
                        for n0 in range(0, kmax, 512):
                            nn = min(512, kmax - n0)
                            kb.t("matmul", out=ps_s[:, n0:n0 + nn], lhsT=q_h[pb, qt * 128:(qt + 1) * 128], rhs=k_h[pb, n0:n0 + nn], start=True, stop=True)
                        kb.wait("vector", "pe")
                        kb.v("tensor_tensor", out=ps_s[:, qt * 128:kmax], in0=ps_s[:, qt * 128:kmax], in1=C["MASKB"][:], op=ALU.add)
                        kb.v("tensor_reduce", out=nmax[:], in_=ps_s[:, 0:kmax], axis=AX.X, op=ALU.max, negate=True)
                        kb.v("tensor_scalar", out=bias[:], in0=nmax[:], scalar1=0.125, scalar2=None, op0=ALU.mult, nodrain=True)
                        kb.wait("scalar", "ve")
                        kb.s("activation", out=p_sb[:, 0:kmax], in_=ps_s[:, 0:kmax], func=AF.Exp, scale=0.125, bias=bias[:], accum_out=rs[:])
                        kb.wait("vector", "se")
                        kb.v("reciprocal", out=rinv[:], in_=rs[:])
                        kb.v("tensor_scalar", out=p_sb[:, 0:kmax], in0=p_sb[:, 0:kmax], scalar1=rinv[:], scalar2=None, op0=ALU.mult, nodrain=True)
                        kb.wait("tensor", "ve")
                        for k4 in range(0, qt + 1, 4):
                            kjs = list(range(k4, min(k4 + 4, qt + 1)))
                            kb.wait("tensor", "se")
                            for idx, kc in enumerate(kjs):
                                kb.t("transpose", out=ps_tr[:, idx * 128:(idx + 1) * 128], in_=p_sb[:, kc * 128:(kc + 1) * 128], identity=C["IDENT_BF"][:])
                            kb.wait("scalar", "pe")
                            for idx, kc in enumerate(kjs):
                                kb.s("copy", out=pt_sb[:, kc * 128:(kc + 1) * 128], in_=ps_tr[:, idx * 128:(idx + 1) * 128])
                        kb.wait("tensor", "se", "ve")
                        for kc in range(qt + 1):
                            kb.t("matmul", out=ps_z[:, br * 128:(br + 1) * 128], lhsT=pt_sb[:, kc * 128:(kc + 1) * 128], rhs=v_tm[:, kc * 128:(kc + 1) * 128], start=(kc == 0), stop=(kc == qt))
                    kb.wait("vector", "pe")
                    kb.v("tensor_copy", out=z1_sb[:], in_=ps_z[:, 0:128])
                    kb.v("scalar_tensor_tensor", out=cc[:], in0=ps_z[:, 128:256], scalar=neglam_col[:], in1=z1_sb[:], op0=ALU.mult, op1=ALU.add)
                    kb.wait("scalar", "ve")
                    kb.s("activation", out=csq[:], in_=cc[:], func=AF.Square)
                    kb.wait("vector", "se")
                    kb.v("tensor_reduce", out=ssq[:], in_=csq[:], axis=AX.X, op=ALU.add)
                    kb.v("tensor_scalar", out=ssq[:], in0=ssq[:], scalar1=1.0 / 128.0, scalar2=1e-8, op0=ALU.mult, op1=ALU.add)
                    kb.wait("scalar", "ve")
                    kb.s("activation", out=rstd[:], in_=ssq[:], func=AF.Sqrt)
                    kb.wait("vector", "se")
                    kb.v("reciprocal", out=rstd[:], in_=rstd[:])
                    kb.v("scalar_tensor_tensor", out=cc[:], in0=cc[:], scalar=rstd[:], in1=wbc[:], op0=ALU.mult, op1=ALU.mult)
                    kb.v("tensor_tensor", out=cc[:], in0=cc[:], in1=bbc[:], op=ALU.add)
                    kb.v("tensor_copy", out=p_sb[:, 0:128], in_=cc[:])
                    kb.wait("tensor", "ve", "se")
                    kb.t("transpose", out=ps_tr[:, 0:128], in_=p_sb[:, 0:128], identity=C["IDENT_BF"][:])
                    kb.wait("scalar", "pe")
                    kb.s("copy", out=att_set[:, (h * NT + qt) * 128:(h * NT + qt + 1) * 128], in_=ps_tr[:, 0:128])

        kb.fence()
        with kb.scope() as osc:
            wo_nat = kb.sb_in(osc, "ab_wonat", [128, 16 * S], BF16)
            xres = kb.sb_in(osc, "ab_xres", [128, D], F32)
            outb = kb.sb_in(osc, "ab_outb", [128, D], F32)
            ps_o = kb.ps_in(osc, "ab_ps_o", [128, S], F32)
            for c in range(NC):
                kb.load(wo_nat[:, c * S:(c + 1) * S], W["Wo"][c * 128:(c + 1) * 128, :], queue="gpsimd")
            for tt in range(NT):
                kb.wait("sync", "ve")
                kb.load(xres[:], x_in[tt * 128:(tt + 1) * 128, :])
                kb.wait("tensor", "ve", "se", "dg")
                for c in range(16):
                    for n0 in range(0, S, 512):
                        kb.t("matmul", out=ps_o[:, n0:n0 + 512], lhsT=att_set[:, (c * NT + tt) * 128:(c * NT + tt + 1) * 128], rhs=wo_nat[:, c * S + n0: c * S + n0 + 512], start=(c == 0), stop=(c == 15))
                kb.wait("vector", "pe", "dl")
                kb.v("tensor_tensor", out=outb[:], in0=ps_o[:, 0:S], in1=xres[:], op=ALU.add)
                kb.store(x_out[tt * 128:(tt + 1) * 128, :], outb[:], wait_on=("ve",))
                kb.wait("vector", "st")



H1F = 8192
RMS_EPS_F32 = float(np.finfo(np.float32).eps)


def emit_ternary_quant(kb, C, w_src, n_rows, n_cols, w_dst):
    with kb.scope() as qsc:
        wb = kb.sb_in(qsc, "tq_wb", [128, 2048], BF16)
        gs = kb.sb_in(qsc, "tq_gs", [128, 32], F32)
        sc_ = kb.sb_in(qsc, "tq_sc", [128, 32], BF16)
        rc = kb.sb_in(qsc, "tq_rc", [128, 32], F32)
        t = kb.sb_in(qsc, "tq_t", [128, 2048], BF16)
        a = kb.sb_in(qsc, "tq_a", [128, 2048], BF16)
        b = kb.sb_in(qsc, "tq_b", [128, 2048], BF16)
        wq = kb.sb_in(qsc, "tq_wq", [128, 2048], BF16)
        for r0 in range(0, n_rows, 128):
            for c0 in range(0, n_cols, 2048):
                kb.wait("gpsimd", "ve")
                kb.load(wb[:], w_src[r0:r0 + 128, c0:c0 + 2048], queue="gpsimd")
                kb.wait("vector", "dg", "st")
                w3 = wb[:].rearrange("p (g r) -> p g r", r=64)
                kb.v("tensor_reduce", out=gs[:], in_=w3, axis=AX.X, op=ALU.add, apply_absolute_value=True)
                kb.v("tensor_scalar", out=sc_[:], in0=gs[:], scalar1=1.0 / 64, scalar2=1e-8, op0=ALU.mult, op1=ALU.max)
                kb.v("reciprocal", out=rc[:], in_=sc_[:])
                s3 = sc_[:].rearrange("p (g r) -> p g r", r=1)
                r3 = rc[:].rearrange("p (g r) -> p g r", r=1)
                w3b, s3b = bass.broadcast_tensor_aps(w3, s3)
                _, r3b = bass.broadcast_tensor_aps(w3, r3)
                kb.v("tensor_tensor", out=t[:].rearrange("p (g r) -> p g r", r=64), in0=w3b, in1=r3b, op=ALU.mult)
                kb.v("tensor_scalar", out=a[:], in0=t[:], scalar1=0.5, scalar2=None, op0=ALU.is_gt)
                kb.v("tensor_scalar", out=b[:], in0=t[:], scalar1=-0.5, scalar2=None, op0=ALU.is_lt)
                kb.v("tensor_tensor", out=wq[:], in0=a[:], in1=b[:], op=ALU.subtract)
                kb.v("tensor_tensor", out=wq[:].rearrange("p (g r) -> p g r", r=64), in0=wq[:].rearrange("p (g r) -> p g r", r=64), in1=s3b, op=ALU.mult)
                kb.store(w_dst[r0:r0 + 128, c0:c0 + 2048], wq[:], wait_on=("ve",))
    kb.fence()


def emit_l1_ffn(kb, C, x_in, x_out, W, quantize=True, scratch=None):
    nc = kb.nc
    if scratch is None:
        scratch = (kb.dtmp("l1f_gu", [2 * H1F, D], BF16), kb.dtmp("l1f_p", [D, H1F], BF16),
                   kb.dtmp("l1f_guT", [D, 2 * H1F], BF16), kb.dtmp("l1f_pT", [H1F, D], BF16))
    wq_gu, wq_p, wq_guT, wq_pT = scratch
    if quantize:
        emit_ternary_quant(kb, C, W["w_gate_up"], 2 * H1F, D, wq_gu)
        emit_ternary_quant(kb, C, W["w_proj"], D, H1F, wq_p)
        with kb.scope() as tsc:
            tstage = kb.sb_in(tsc, "l1f_tstage", [128, 2048], BF16)
            for c in range(NC):
                for p0 in range(0, 2 * H1F, 2048):
                    kb.wait("sync", "st")
                    kb.loadT(tstage[:], wq_gu[p0:p0 + 2048, c * 128:(c + 1) * 128])
                    kb.store(wq_guT[c * 128:(c + 1) * 128, p0:p0 + 2048], tstage[:], wait_on=("dl",))
            for hc in range(H1F // 128):
                kb.wait("sync", "st")
                kb.loadT(tstage[:], wq_p[0:D, hc * 128:(hc + 1) * 128])
                kb.store(wq_pT[hc * 128:(hc + 1) * 128, :], tstage[:], wait_on=("dl",))
        kb.fence()
    with kb.scope() as sc:
        xn_fm = kb.sb_in(sc, "l1f_xn", [128, NC * S], BF16, side="right")
        with kb.scope() as lnsc:
            xtile = kb.sb_in(lnsc, "l1f_xtile", [128, D], F32)
            scr = ln_scratch(kb, lnsc, "l1f")
            emit_ln(kb, C, x_in, xn_fm, xtile, scr)
        kb.fence()
        hstore = kb.sb_in(sc, "l1f_h", [128, 64 * 512], BF16)
        wt = kb.sb_in(sc, "l1f_wt", [128, NC * 512], BF16)
        sqt = kb.sb_in(sc, "l1f_sqt", [128, 512], BF16)
        rstd = kb.sb_in(sc, "l1f_rstd", [1, 512], F32)
        wp_g = kb.sb_in(sc, "l1f_wpg", [128, 8 * D], BF16)
        out_sb = kb.sb_in(sc, "l1f_out", [128, 2 * D], F32)
        xres = kb.sb_in(sc, "l1f_xres", [128, D], F32)
        ps_big = kb.ps_in(sc, "l1f_ps_big", [128, 2048], F32)
        ps_ss = kb.ps_in(sc, "l1f_ps_ss", [1, 512], F32)
        ps_bc = kb.ps_in(sc, "l1f_ps_bc", [128, 512], F32)
        for qq in range(4):
            q0 = qq * 512
            for mg in range(32):
                kb.wait("sync", "pe")
                kb.load(wt[:].rearrange("p (c j) -> p c j", j=512), wq_guT[:, mg * 512:(mg + 1) * 512].rearrange("(c p) j -> p c j", p=128))
                kb.wait("tensor", "dl", "se", "ve")
                for c in range(NC):
                    for j in range(4):
                        kb.t("matmul", out=ps_big[:, j * 512:(j + 1) * 512], lhsT=wt[:, c * 512 + j * 128: c * 512 + (j + 1) * 128], rhs=xn_fm[:, c * S + q0:c * S + q0 + 512], start=(c == 0), stop=(c == NC - 1))
                for j in range(4):
                    mc = mg * 4 + j
                    if mc < 64:
                        kb.wait("scalar", "pe")
                        kb.s("activation", out=hstore[:, mc * 512:(mc + 1) * 512], in_=ps_big[:, j * 512:(j + 1) * 512], func=AF.Silu, nodrain=True)
                    else:
                        hc = mc - 64
                        kb.wait("vector", "pe")
                        kb.v("tensor_tensor", out=hstore[:, hc * 512:(hc + 1) * 512], in0=hstore[:, hc * 512:(hc + 1) * 512], in1=ps_big[:, j * 512:(j + 1) * 512], op=ALU.mult)
                        kb.wait("scalar", "ve")
                        kb.s("activation", out=sqt[:], in_=hstore[:, hc * 512:(hc + 1) * 512], func=AF.Square, nodrain=True)
                        kb.wait("tensor", "se")
                        kb.t("matmul", out=ps_ss[:, 0:512], lhsT=C["ONES_COL"][:], rhs=sqt[:], start=(hc == 0), stop=(hc == 63))
            kb.wait("vector", "pe")
            kb.v("tensor_scalar", out=rstd[:], in0=ps_ss[:, 0:512], scalar1=1.0 / H1F, scalar2=RMS_EPS_F32, op0=ALU.mult, op1=ALU.add)
            kb.wait("scalar", "ve")
            kb.s("activation", out=rstd[:], in_=rstd[:], func=AF.Sqrt)
            kb.wait("vector", "se")
            kb.v("reciprocal", out=rstd[:], in_=rstd[:])
            kb.wait("tensor", "ve", "se")
            kb.t("matmul", out=ps_bc[:, 0:512], lhsT=C["ONES_ROW_F32"][:], rhs=rstd[:], start=True, stop=True)
            kb.wait("vector", "pe")
            for hc in range(64):
                kb.v("tensor_tensor", out=hstore[:, hc * 512:(hc + 1) * 512], in0=hstore[:, hc * 512:(hc + 1) * 512], in1=ps_bc[:, 0:512], op=ALU.mult)
            for tg in range(2):
                for cg in range(8):
                    kb.wait("sync", "pe")
                    kb.load(wp_g[:].rearrange("p (c j) -> p c j", j=D), wq_pT[cg * 1024:(cg + 1) * 1024, :].rearrange("(c p) j -> p c j", p=128))
                    kb.wait("tensor", "dl")
                    for ttj in range(2):
                        tt = tg * 2 + ttj
                        kb.wait("tensor", "ve")
                        for cj in range(8):
                            c = cg * 8 + cj
                            for n0 in range(0, D, 512):
                                kb.t("matmul", out=ps_big[:, n0:n0 + 512], lhsT=hstore[:, c * 512 + tt * 128: c * 512 + (tt + 1) * 128], rhs=wp_g[:, cj * D + n0: cj * D + n0 + 512], start=(cj == 0), stop=(cj == 7))
                        kb.wait("vector", "pe")
                        if cg == 0:
                            kb.v("tensor_copy", out=out_sb[:, ttj * D:(ttj + 1) * D], in_=ps_big[:, 0:D])
                        else:
                            kb.v("tensor_tensor", out=out_sb[:, ttj * D:(ttj + 1) * D], in0=out_sb[:, ttj * D:(ttj + 1) * D], in1=ps_big[:, 0:D], op=ALU.add)
                for ttj in range(2):
                    gt = qq * 4 + tg * 2 + ttj
                    kb.wait("sync", "ve")
                    kb.load(xres[:], x_in[gt * 128:(gt + 1) * 128, :])
                    kb.wait("vector", "dl")
                    kb.v("tensor_tensor", out=out_sb[:, ttj * D:(ttj + 1) * D], in0=out_sb[:, ttj * D:(ttj + 1) * D], in1=xres[:], op=ALU.add)
                    kb.store(x_out[gt * 128:(gt + 1) * 128, :], out_sb[:, ttj * D:(ttj + 1) * D], wait_on=("ve",))
                kb.wait("vector", "st")
    kb.fence()



F0 = 5632
NFC = F0 // 128
L0FFN_PHASE = 99
L0FFN_NFC = NFC
L0FFN_END = 3


def emit_l0_ffn_prepass(kb, C, W, scr):
    w1bf, w3bf, w2bf, w1T, w3T, w2T = scr
    with kb.scope() as ps:
        stage = kb.sb_in(ps, "mf_stage", [128, 2048], BF16)
        for (src, dst, nr, ncol) in ((W["w1"], w1bf, F0, D), (W["w3"], w3bf, F0, D)):
            for r0 in range(0, nr, 128):
                kb.wait("gpsimd", "st")
                kb.load(stage[:], src[r0:r0 + 128, :], queue="gpsimd")
                kb.store(dst[r0:r0 + 128, :], stage[:], wait_on=("dg",))
        for r0 in range(0, D, 128):
            for c0 in range(0, F0, 1408):
                kb.wait("gpsimd", "st")
                kb.load(stage[:, 0:1408], W["w2"][r0:r0 + 128, c0:c0 + 1408], queue="gpsimd")
                kb.store(w2bf[r0:r0 + 128, c0:c0 + 1408], stage[:, 0:1408], wait_on=("dg",))
        for (srcbf, dstT, nr, ncol) in ((w1bf, w1T, F0, D), (w3bf, w3T, F0, D)):
            for c in range(NC):
                for p0 in range(0, nr, 1408):
                    kb.wait("sync", "st")
                    kb.loadT(stage[:, 0:1408], srcbf[p0:p0 + 1408, c * 128:(c + 1) * 128])
                    kb.store(dstT[c * 128:(c + 1) * 128, p0:p0 + 1408], stage[:, 0:1408], wait_on=("dl",))
        for f in range(NFC):
            kb.wait("sync", "st")
            kb.loadT(stage[:], w2bf[0:D, f * 128:(f + 1) * 128])
            kb.store(w2T[f * 128:(f + 1) * 128, :], stage[:], wait_on=("dl",))
    kb.fence()


def emit_l0_ffn(kb, C, x_in, x_out, W, scr, out_acc):
    nc = kb.nc
    if L0FFN_PHASE < 1:
        return
    w1bf, w3bf, w2bf, w1T, w3T, w2T = scr
    with kb.scope() as sc:
        xn_fm = kb.sb_in(sc, "mf_xn", [128, NC * S], BF16, side="right")
        with kb.scope() as lnsc:
            xtile = kb.sb_in(lnsc, "mf_xtile", [128, D], F32)
            scrn = ln_scratch(kb, lnsc, "mf")
            emit_ln(kb, C, x_in, xn_fm, xtile, scrn)
        kb.fence()
        gwT = kb.sb_in(sc, "mf_gwT", [128, 128], BF16)
        w_tm = kb.sb_in(sc, "mf_wtm", [128, 128], F32)
        u1G = [kb.sb_in(sc, f"mf_u1g{g}", [80, S], BF16) for g in range(3)]
        u3G = [kb.sb_in(sc, f"mf_u3g{g}", [80, S], BF16) for g in range(3)]
        b2TG = [kb.sb_in(sc, f"mf_b2tg{g}", [80, S], BF16) for g in range(3)]
        a2_all = kb.sb_in(sc, "mf_a2", [128, F0], BF16)

        def eg(e):
            return e // 3, 32 * (e % 3)

        with kb.scope() as p1:
            ps_r = kb.ps_in(p1, "mf_ps_r", [128, 512], F32)
            ps_t = kb.ps_in(p1, "mf_ps_t", [128, 512], BF16)
            ps_u = kb.ps_in(p1, "mf_ps_u", [16, S], F32)
            gw_nat = kb.sb_in(p1, "mf_gwnat", [8, D], BF16)
            a1nat = kb.sb_in(p1, "mf_a1nat", [128, D], BF16)
            a1T = kb.sb_in(p1, "mf_a1T", [128, D], BF16)
            m1n = kb.sb_in(p1, "mf_m1n", [128, 1], F32)
            m2n = kb.sb_in(p1, "mf_m2n", [128, 1], F32)
            sg1 = kb.sb_in(p1, "mf_sg1", [128, 1], F32)
            sg2 = kb.sb_in(p1, "mf_sg2", [128, 1], F32)
            tl = kb.sb_in(p1, "mf_tl", [128, 8], F32)
            t2 = kb.sb_in(p1, "mf_t2", [128, 8], F32)
            mk1 = kb.sb_in(p1, "mf_mk1", [128, 8], F32)
            mk2 = kb.sb_in(p1, "mf_mk2", [128, 8], F32)
            b2nat = kb.sb_in(p1, "mf_b2nat", [128, 256], BF16)
            kb.load(gw_nat[:], W["gate_w"], queue="gpsimd")
            kb.wait("tensor", "dg", "ve", "se")
            for c in range(NC):
                kb.t("transpose", out=ps_t[:, 0:8], in_=gw_nat[:, c * 128:(c + 1) * 128], identity=C["I16"][0:8, 0:8])
                kb.wait("scalar", "pe")
                kb.s("copy", out=gwT[:, c * 8:(c + 1) * 8], in_=ps_t[:, 0:8])
                kb.wait("tensor", "se")
            for tt in range(NT):
                kb.wait("tensor", "se", "ve")
                for c in range(NC):
                    kb.t("matmul", out=ps_r[:, 0:8], lhsT=xn_fm[:, c * S + tt * 128: c * S + (tt + 1) * 128], rhs=gwT[:, c * 8:(c + 1) * 8], start=(c == 0), stop=(c == NC - 1))
                kb.wait("vector", "pe")
                kb.v("tensor_reduce", out=m1n[:], in_=ps_r[:, 0:8], axis=AX.X, op=ALU.max, negate=True)
                kb.v("tensor_scalar", out=tl[:], in0=ps_r[:, 0:8], scalar1=m1n[:], scalar2=None, op0=ALU.add)
                kb.v("tensor_scalar", out=mk1[:], in0=tl[:], scalar1=0.0, scalar2=None, op0=ALU.is_ge)
                kb.v("scalar_tensor_tensor", out=t2[:], in0=mk1[:], scalar=-1e30, in1=tl[:], op0=ALU.mult, op1=ALU.add)
                kb.v("tensor_reduce", out=m2n[:], in_=t2[:], axis=AX.X, op=ALU.max, negate=True)
                kb.wait("scalar", "ve")
                kb.s("activation", out=sg1[:], in_=m2n[:], func=AF.Sigmoid)
                kb.wait("vector", "se")
                kb.v("tensor_scalar", out=sg2[:], in0=sg1[:], scalar1=-1.0, scalar2=1.0, op0=ALU.mult, op1=ALU.add)
                kb.v("tensor_scalar", out=t2[:], in0=t2[:], scalar1=m2n[:], scalar2=None, op0=ALU.add)
                kb.v("tensor_scalar", out=mk2[:], in0=t2[:], scalar1=0.0, scalar2=None, op0=ALU.is_ge)
                kb.v("tensor_scalar", out=mk1[:], in0=mk1[:], scalar1=sg1[:], scalar2=None, op0=ALU.mult)
                kb.v("scalar_tensor_tensor", out=w_tm[:, tt * 8:(tt + 1) * 8], in0=mk2[:], scalar=sg2[:], in1=mk1[:], op0=ALU.mult, op1=ALU.add)
            for (a_ap, uG) in ((W["a1"], u1G), (W["a3"], u3G)):
                kb.wait("gpsimd", "pe")
                kb.load(a1nat[:], a_ap.rearrange("e r d -> (e r) d"), queue="gpsimd")
                kb.wait("tensor", "dg", "se")
                for c4 in range(4):
                    kb.wait("tensor", "se")
                    for j in range(4):
                        c = c4 * 4 + j
                        kb.t("transpose", out=ps_t[:, j * 128:(j + 1) * 128], in_=a1nat[:, c * 128:(c + 1) * 128], identity=C["IDENT_BF"][:])
                    kb.wait("scalar", "pe")
                    for j in range(4):
                        c = c4 * 4 + j
                        kb.s("copy", out=a1T[:, c * 128:(c + 1) * 128], in_=ps_t[:, j * 128:(j + 1) * 128])
                for e in range(8):
                    g, pb = eg(e)
                    kb.wait("tensor", "se", "ve")
                    for c in range(NC):
                        for n0 in range(0, S, 512):
                            kb.t("matmul", out=ps_u[0:16, n0:n0 + 512], lhsT=a1T[:, c * 128 + e * 16: c * 128 + (e + 1) * 16], rhs=xn_fm[:, c * S + n0: c * S + n0 + 512], start=(c == 0), stop=(c == NC - 1))
                    kb.wait("scalar", "pe")
                    kb.s("mul", out=uG[g][pb:pb + 16, :], in_=ps_u[0:16, :], mul=2.0)
            for e in range(8):
                g, pb = eg(e)
                kb.wait("gpsimd", "pe", "se")
                kb.load(b2nat[:].rearrange("p (dc r) -> p dc r", r=16), W["b2"][e, :, :].rearrange("(dc p) r -> p dc r", p=128), queue="gpsimd")
                kb.wait("tensor", "dg", "se")
                for dc in range(NC):
                    kb.t("transpose", out=ps_t[0:16, 0:128], in_=b2nat[:, dc * 16:(dc + 1) * 16], identity=C["IDENT_BF"][:])
                    kb.wait("scalar", "pe")
                    kb.s("mul", out=b2TG[g][pb:pb + 16, dc * 128:(dc + 1) * 128], in_=ps_t[0:16, 0:128], mul=2.0)
                    kb.wait("tensor", "se")
            kb.load(a2_all[:], W["a2"].rearrange("e r f -> (e r) f"), queue="gpsimd")
            kb.wait("sync", "dg")
        kb.fence()

        for qq in range(2 if L0FFN_PHASE >= 3 else (1 if L0FFN_PHASE >= 2 else 0)):
            q0 = qq * 1024
            HT = 1024
            NTT = 8
            with kb.scope() as qs:
                w1c = kb.sb_in(qs, "mf_w1c", [128, D], BF16)
                w3c = kb.sb_in(qs, "mf_w3c", [128, D], BF16)
                w2t4 = kb.sb_in(qs, "mf_w2t4", [128, 4 * D], BF16)
                b1blk = kb.sb_in(qs, "mf_b1blk", [128, 128], BF16)
                b3blk = kb.sb_in(qs, "mf_b3blk", [128, 128], BF16)
                b1TG = [kb.sb_in(qs, f"mf_b1tg{g}", [80, 128], BF16) for g in range(3)]
                b3TG = [kb.sb_in(qs, f"mf_b3tg{g}", [80, 128], BF16) for g in range(3)]
                a2T = kb.sb_in(qs, "mf_a2T", [128, 128], BF16)
                b1sb = kb.sb_in(qs, "mf_b1sb", [128, HT], BF16)
                b3sb = kb.sb_in(qs, "mf_b3sb", [128, HT], BF16)
                h1 = kb.sb_in(qs, "mf_h1", [128, HT], BF16)
                s1 = kb.sb_in(qs, "mf_s1", [128, HT], BF16)
                h3 = kb.sb_in(qs, "mf_h3", [128, HT], BF16)
                mm_ = kb.sb_in(qs, "mf_m", [128, HT], BF16)
                wact = kb.sb_in(qs, "mf_wact", [128, HT], BF16)
                acc4 = kb.sb_in(qs, "mf_acc4", [128, 4 * HT], BF16)
                wbc = kb.sb_in(qs, "mf_wbc", [128, 8 * HT], BF16)
                wrow = kb.sb_in(qs, "mf_wrow", [1, HT], F32)
                g2sb = kb.sb_in(qs, "mf_g2sb", [128, 8 * NTT * 16], BF16)
                g2TG = [kb.sb_in(qs, f"mf_g2tg{g}", [80, NTT * 128], BF16) for g in range(3)]
                stage = kb.sb_in(qs, "mf_stage", [128, D], F32)
                xres = kb.sb_in(qs, "mf_xres", [128, D], F32)
                ps_g2 = kb.ps_in(qs, "mf_ps_g2", [128, 8 * NTT * 16], F32)
                psA = kb.ps_in(qs, "mf_psA", [128, HT], F32)
                psB = kb.ps_in(qs, "mf_psB", [128, 2048], F32)
                ps_t2 = psB[:, 0:256].bitcast(BF16)
                for e in range(8):
                    kb.wait("tensor", "se", "ve")
                    for j in range(NTT):
                        tt = qq * NTT + j
                        kb.t("transpose", out=psB[0:1, j * 128:(j + 1) * 128], in_=w_tm[:, tt * 8 + e: tt * 8 + e + 1], identity=C["IDENT_F32"][:])
                    kb.wait("scalar", "pe")
                    kb.s("copy", out=wrow[:], in_=psB[0:1, 0:HT])
                    kb.wait("tensor", "se")
                    for n0 in range(0, HT, 512):
                        kb.t("matmul", out=psB[:, n0:n0 + 512], lhsT=C["ONES_ROW_F32"][:], rhs=wrow[0:1, n0:n0 + 512], start=True, stop=True)
                    kb.wait("scalar", "pe")
                    kb.s("copy", out=wbc[:, e * HT:(e + 1) * HT], in_=psB[:, 0:HT])
                for fc in range(L0FFN_NFC):
                    sl = fc % 4
                    kb.wait("sync", "pe")
                    kb.load(w1c[:].rearrange("p (c j) -> p c j", j=128), w1T[:, fc * 128:(fc + 1) * 128].rearrange("(c p) j -> p c j", p=128))
                    kb.load(w3c[:].rearrange("p (c j) -> p c j", j=128), w3T[:, fc * 128:(fc + 1) * 128].rearrange("(c p) j -> p c j", p=128))
                    kb.load(w2t4[:, sl * D:(sl + 1) * D], w2T[fc * 128:(fc + 1) * 128, :])
                    kb.wait("gpsimd", "pe")
                    kb.load(b1blk[:].rearrange("p (e r) -> p e r", r=16), W["b1"][:, fc * 128:(fc + 1) * 128, :].rearrange("e p r -> p e r"), queue="gpsimd")
                    kb.load(b3blk[:].rearrange("p (e r) -> p e r", r=16), W["b3"][:, fc * 128:(fc + 1) * 128, :].rearrange("e p r -> p e r"), queue="gpsimd")
                    kb.wait("tensor", "se", "ve", "dg")
                    kb.t("transpose", out=ps_t2[:, 0:128], in_=a2_all[:, fc * 128:(fc + 1) * 128], identity=C["IDENT_BF"][:])
                    kb.wait("scalar", "pe")
                    kb.s("copy", out=a2T[:], in_=ps_t2[:, 0:128])
                    for e in range(8):
                        g, pb = eg(e)
                        kb.wait("tensor", "se")
                        kb.t("transpose", out=ps_t2[0:16, 0:128], in_=b1blk[:, e * 16:(e + 1) * 16], identity=C["IDENT_BF"][:])
                        kb.wait("scalar", "pe")
                        kb.s("copy", out=b1TG[g][pb:pb + 16, :], in_=ps_t2[0:16, 0:128])
                        kb.wait("tensor", "se")
                        kb.t("transpose", out=ps_t2[0:16, 0:128], in_=b3blk[:, e * 16:(e + 1) * 16], identity=C["IDENT_BF"][:])
                        kb.wait("scalar", "pe")
                        kb.s("copy", out=b3TG[g][pb:pb + 16, :], in_=ps_t2[0:16, 0:128])
                    kb.wait("tensor", "se", "ve", "dl")
                    for n0 in range(0, HT, 512):
                        for c in range(NC):
                            kb.t("matmul", out=psA[:, n0:n0 + 512], lhsT=w1c[:, c * 128:(c + 1) * 128], rhs=xn_fm[:, c * S + q0 + n0: c * S + q0 + n0 + 512], start=(c == 0), stop=(c == NC - 1))
                    kb.wait("scalar", "pe")
                    kb.s("copy", out=b1sb[:], in_=psA[:, 0:HT])
                    kb.wait("tensor", "se")
                    for n0 in range(0, HT, 512):
                        for c in range(NC):
                            kb.t("matmul", out=psA[:, n0:n0 + 512], lhsT=w3c[:, c * 128:(c + 1) * 128], rhs=xn_fm[:, c * S + q0 + n0: c * S + q0 + n0 + 512], start=(c == 0), stop=(c == NC - 1))
                    kb.wait("scalar", "pe")
                    kb.s("copy", out=b3sb[:], in_=psA[:, 0:HT])
                    for e in range(8):
                        g, pb = eg(e)
                        kb.wait("tensor", "se", "ve")
                        for n0 in range(0, HT, 512):
                            kb.t("matmul", out=psA[:, n0:n0 + 512], lhsT=b1TG[g][pb:pb + 16, :], rhs=u1G[g][pb:pb + 16, q0 + n0:q0 + n0 + 512], start=True, stop=True)
                        kb.wait("vector", "pe")
                        kb.v("scalar_tensor_tensor", out=h1[:], in0=psA[:, 0:HT], scalar=1.0, in1=b1sb[:], op0=ALU.mult, op1=ALU.add, nodrain=True)
                        kb.wait("scalar", "ve")
                        kb.s("activation", out=s1[:], in_=h1[:], func=AF.Silu, nodrain=True)
                        kb.wait("tensor", "ve")
                        for n0 in range(0, HT, 512):
                            kb.t("matmul", out=psA[:, n0:n0 + 512], lhsT=b3TG[g][pb:pb + 16, :], rhs=u3G[g][pb:pb + 16, q0 + n0:q0 + n0 + 512], start=True, stop=True)
                        kb.wait("vector", "pe", "se")
                        kb.v("scalar_tensor_tensor", out=h3[:], in0=psA[:, 0:HT], scalar=1.0, in1=b3sb[:], op0=ALU.mult, op1=ALU.add)
                        kb.v("tensor_tensor", out=mm_[:], in0=s1[:], in1=h3[:], op=ALU.mult)
                        kb.v("tensor_tensor", out=wact[:], in0=mm_[:], in1=wbc[:, e * HT:(e + 1) * HT], op=ALU.mult)
                        if e == 0:
                            kb.v("tensor_copy", out=acc4[:, sl * HT:(sl + 1) * HT], in_=wact[:], nodrain=True)
                        else:
                            kb.v("tensor_tensor", out=acc4[:, sl * HT:(sl + 1) * HT], in0=acc4[:, sl * HT:(sl + 1) * HT], in1=wact[:], op=ALU.add, nodrain=True)
                        kb.wait("tensor", "ve")
                        for tt in range(NTT):
                            kb.t("matmul", out=ps_g2[:, (e * NTT + tt) * 16:(e * NTT + tt + 1) * 16], lhsT=wact[:, tt * 128:(tt + 1) * 128], rhs=a2T[:, e * 16:(e + 1) * 16], start=(fc == 0), stop=(fc == L0FFN_NFC - 1))
                    if sl == 3:
                        for tt in range(NTT):
                            gt = qq * NTT + tt
                            kb.wait("tensor", "ve", "se", "dl")
                            for j in range(4):
                                for n0 in range(0, D, 512):
                                    kb.t("matmul", out=psB[:, n0:n0 + 512], lhsT=acc4[:, j * HT + tt * 128: j * HT + (tt + 1) * 128], rhs=w2t4[:, j * D + n0: j * D + n0 + 512], start=(j == 0), stop=(j == 3))
                            kb.wait("vector", "pe", "dma")
                            kb.v("tensor_copy", out=stage[:], in_=psB[:, 0:D])
                            kb.wait("gpsimd", "ve", "dma")
                            if fc == 3:
                                kb._inc(nc.gpsimd.dma_start(out=out_acc[gt * 128:(gt + 1) * 128, :], in_=stage[:]), "st", 16)
                            else:
                                kb._inc(nc.gpsimd.dma_start(out=out_acc[gt * 128:(gt + 1) * 128, :], in_=stage[:], accum_op=ALU.add), "st", 16)
                if L0FFN_NFC < NFC:
                    continue
                kb.wait("vector", "pe")
                kb.v("tensor_copy", out=g2sb[:], in_=ps_g2[:, 0:8 * NTT * 16])
                for e in range(8):
                    g, pb = eg(e)
                    for tt in range(NTT):
                        kb.wait("tensor", "ve", "se")
                        kb.t("transpose", out=ps_t2[0:16, 0:128], in_=g2sb[:, (e * NTT + tt) * 16:(e * NTT + tt + 1) * 16], identity=C["IDENT_BF"][:])
                        kb.wait("scalar", "pe")
                        kb.s("copy", out=g2TG[g][pb:pb + 16, tt * 128:(tt + 1) * 128], in_=ps_t2[0:16, 0:128])
                for tt in range(NTT if L0FFN_END >= 2 else 0):
                    gt = qq * NTT + tt
                    kb.wait("sync", "ve", "dma")
                    kb.load(xres[:], x_in[gt * 128:(gt + 1) * 128, :])
                    kb.wait("vector", "dma")
                    kb.load(stage[:], out_acc[gt * 128:(gt + 1) * 128, :])
                    first = True
                    for e in range(8):
                        g, pb = eg(e)
                        kb.wait("tensor", "ve", "se")
                        for n0 in range(0, D, 512):
                            kb.t("matmul", out=psB[:, n0:n0 + 512], lhsT=g2TG[g][pb:pb + 16, tt * 128:(tt + 1) * 128], rhs=b2TG[g][pb:pb + 16, n0:n0 + 512], start=True, stop=True)
                        kb.wait("vector", "pe", "dl")
                        kb.v("tensor_tensor", out=stage[:], in0=stage[:], in1=psB[:, 0:D], op=ALU.add)
                    kb.wait("vector", "dl")
                    kb.v("tensor_tensor", out=stage[:], in0=stage[:], in1=xres[:], op=ALU.add)
                    if L0FFN_END >= 3:
                        kb.store(x_out[gt * 128:(gt + 1) * 128, :], stage[:], wait_on=("ve",))
                    kb.wait("vector", "dma")
            kb.fence()
    kb.fence()



SURPRISE_THR_ARG = float((1.0 + 1e-6) * math.log(0.7 / 0.3))
SUR_SCALE = float(1.0 / (1.0 + 1e-6))
SCAN_STEPS = S


def emit_l1_attn_a(kb, C, x_in, x_out, W):
    nc = kb.nc
    with kb.scope() as sc:
        k_fm = kb.sb_in(sc, "sa_k", [128, S], F32)
        q_fm = kb.sb_in(sc, "sa_q", [128, S], F32)
        v_fm = kb.sb_in(sc, "sa_v", [128, S], F32)
        af = kb.sb_in(sc, "sa_af", [128, S], F32)
        af1m = kb.sb_in(sc, "sa_af1m", [128, S], F32)
        as_ = kb.sb_in(sc, "sa_as", [128, S], F32)
        as1mU = kb.sb_in(sc, "sa_as1mU", [128, S], F32)
        nu = kb.sb_in(sc, "sa_nu", [128, S], F32)
        outF = kb.sb_in(sc, "sa_outF", [128, S], F32)
        outS = kb.sb_in(sc, "sa_outS", [128, S], F32)
        nSarr = kb.sb_in(sc, "sa_nS", [128, S], F32)
        pw_tm = kb.sb_in(sc, "sa_pwtm", [128, 3 * NT], F32)
        fH = kb.sb_in(sc, "sa_fH", [128, 128], F32)
        sH = kb.sb_in(sc, "sa_sH", [128, 128], F32)
        zc = kb.sb_in(sc, "sa_zc", [128, 1], F32)

        with kb.scope() as bsc:
            xn_fm = kb.sb_in(bsc, "sa_xn", [128, NC * S], BF16)
            with kb.scope() as lnsc:
                xtile = kb.sb_in(lnsc, "sa_xtile", [128, D], F32)
                scrn = ln_scratch(kb, lnsc, "sa")
                emit_ln(kb, C, x_in, xn_fm, xtile, scrn)
            kb.fence()
            wnat = kb.sb_in(bsc, "sa_wnat", [128, NC * 128], BF16)
            colb = kb.sb_in(bsc, "sa_colb", [128, 1], F32)
            col2 = kb.sb_in(bsc, "sa_col2", [128, 1], F32)
            tmpb = kb.sb_in(bsc, "sa_tmpb", [128, S], F32)
            rowb = kb.sb_in(bsc, "sa_rowb", [1, S], F32)
            vbf = kb.sb_in(bsc, "sa_vbf", [128, S], BF16)
            nmw = kb.sb_in(bsc, "sa_nmw", [128, 256], BF16)
            nmw2 = kb.sb_in(bsc, "sa_nmw2", [128, 256], BF16)
            g1 = kb.sb_in(bsc, "sa_g1", [128, 2 * S], BF16)
            pww = kb.sb_in(bsc, "sa_pww", [128, 48], BF16)
            prow = kb.sb_in(bsc, "sa_prow", [1, 8], F32)
            tmpc = kb.sb_in(bsc, "sa_tmpc", [128, S], F32)
            ps_p = kb.ps_in(bsc, "sa_ps_p", [128, S], F32)
            ps_t = kb.ps_in(bsc, "sa_ps_t", [128, 512], BF16)
            ps_sm = kb.ps_in(bsc, "sa_ps_sm", [128, 16], F32)

            def load_w_nat(ap):
                kb.wait("gpsimd", "pe")
                kb.load(wnat[:].rearrange("p (c j) -> p c j", j=128), ap.rearrange("(c p) j -> p c j", p=128), queue="gpsimd")

            def proj_psum(dst_ps):
                for n0 in range(0, S, 512):
                    for c in range(NC):
                        kb.t("matmul", out=dst_ps[:, n0:n0 + 512], lhsT=wnat[:, c * 128:(c + 1) * 128], rhs=xn_fm[:, c * S + n0:c * S + n0 + 512], start=(c == 0), stop=(c == NC - 1))

            def l2norm_from_psum(dst):
                kb.wait("scalar", "ve", "pe")
                kb.s("activation", out=tmpc[:], in_=ps_p[:, 0:S], func=AF.Square)
                kb.wait("vector", "pe", "se")
                kb.v("tensor_copy", out=tmpb[:], in_=ps_p[:, 0:S])
                kb.wait("tensor", "se", "ve")
                for n0 in range(0, S, 512):
                    kb.t("matmul", out=ps_p[0:1, n0:n0 + 512], lhsT=C["ONES_COL_F32"][:], rhs=tmpc[:, n0:n0 + 512], start=True, stop=True)
                kb.wait("scalar", "pe")
                kb.s("activation", out=rowb[:], in_=ps_p[0:1, 0:S], func=AF.Sqrt)
                kb.wait("vector", "se")
                kb.v("tensor_scalar", out=rowb[:], in0=rowb[:], scalar1=1e-12, scalar2=None, op0=ALU.max)
                kb.v("reciprocal", out=rowb[:], in_=rowb[:])
                kb.wait("tensor", "ve", "se")
                for n0 in range(0, S, 512):
                    kb.t("matmul", out=ps_p[:, n0:n0 + 512], lhsT=C["ONES_ROW_F32"][:], rhs=rowb[0:1, n0:n0 + 512], start=True, stop=True)
                kb.wait("vector", "pe")
                kb.v("tensor_tensor", out=dst[:], in0=tmpb[:], in1=ps_p[:, 0:S], op=ALU.mult)

            load_w_nat(W["Wk"])
            kb.wait("tensor", "dg", "se", "ve")
            proj_psum(ps_p)
            l2norm_from_psum(k_fm)
            load_w_nat(W["Wq"])
            kb.wait("tensor", "dg", "se", "ve")
            proj_psum(ps_p)
            l2norm_from_psum(q_fm)
            load_w_nat(W["Wv"])
            kb.wait("tensor", "dg", "se", "ve")
            proj_psum(ps_p)
            kb.wait("vector", "pe")
            kb.v("tensor_copy", out=v_fm[:], in_=ps_p[:, 0:S])
            kb.v("tensor_copy", out=vbf[:], in_=v_fm[:])
            for (Wg, bg, lam, a_dst, a1m_dst) in ((W["Waf"], W["baf"], W["fast_lambda"], af, af1m),
                                                  (W["Was"], W["bas"], W["slow_lambda"], as_, as1mU)):
                kb.wait("sync", "se", "ve")
                kb.load(colb[:], bg.rearrange("(p one) -> p one", one=1), queue="sync")
                kb.load(col2[:], lam.rearrange("(p one) -> p one", one=1), queue="sync")
                load_w_nat(Wg)
                kb.wait("tensor", "dg", "se", "ve")
                proj_psum(ps_p)
                kb.wait("scalar", "dl", "ve")
                kb.s("activation", out=col2[:], in_=col2[:], func=AF.Sigmoid)
                kb.s("activation", out=col2[:], in_=col2[:], func=AF.Ln, bias=C["EPS8_COL"][:])
                kb.wait("vector", "se")
                kb.v("tensor_scalar", out=col2[:], in0=col2[:], scalar1=8.0, scalar2=None, op0=ALU.mult)
                kb.wait("scalar", "pe", "ve")
                kb.s("activation", out=tmpb[:], in_=ps_p[:, 0:S], func=AF.Sigmoid, bias=colb[:])
                kb.wait("vector", "se")
                kb.v("tensor_scalar", out=tmpb[:], in0=tmpb[:], scalar1=col2[:], scalar2=None, op0=ALU.mult)
                kb.wait("scalar", "ve")
                kb.s("activation", out=a_dst[:], in_=tmpb[:], func=AF.Exp)
                kb.wait("vector", "se")
                kb.v("tensor_scalar", out=a1m_dst[:], in0=a_dst[:], scalar1=-1.0, scalar2=1.0, op0=ALU.mult, op1=ALU.add)
            kb.wait("sync", "se", "ve")
            kb.load(colb[:], W["ir_b1"].rearrange("(p one) -> p one", one=1), queue="sync")
            load_w_nat(W["ir_w1"])
            kb.wait("tensor", "dg", "se", "ve")
            proj_psum(ps_p)
            kb.wait("scalar", "pe", "dl", "ve")
            kb.s("activation", out=tmpb[:], in_=ps_p[:, 0:S], func=AF.Gelu, bias=colb[:])
            kb.wait("sync", "se", "ve")
            kb.load(col2[:], W["ir_w2"], queue="sync")
            irb2 = kb.sb_in(bsc, "sa_irb2", [1, 1], F32)
            kb.load(irb2[:], W["ir_b2"].rearrange("(one o2) -> one o2", one=1), queue="sync")
            kb.wait("tensor", "se", "dl", "ve")
            for n0 in range(0, S, 512):
                kb.t("matmul", out=ps_p[0:1, n0:n0 + 512], lhsT=col2[:], rhs=tmpb[:, n0:n0 + 512], start=True, stop=True)
            kb.wait("scalar", "pe")
            kb.s("activation", out=rowb[:], in_=ps_p[0:1, 0:S], func=AF.Sigmoid, bias=irb2[:])
            kb.wait("vector", "se")
            kb.v("tensor_scalar", out=rowb[:], in0=rowb[:], scalar1=0.5, scalar2=None, op0=ALU.is_gt)
            kb.wait("tensor", "ve")
            for n0 in range(0, S, 512):
                kb.t("matmul", out=ps_p[:, n0:n0 + 512], lhsT=C["ONES_ROW_F32"][:], rhs=rowb[0:1, n0:n0 + 512], start=True, stop=True)
            kb.wait("vector", "pe")
            kb.v("tensor_tensor", out=as1mU[:], in0=as1mU[:], in1=ps_p[:, 0:S], op=ALU.mult)
            pwb = kb.sb_in(bsc, "sa_pwb", [1, 4], BF16)
            kb.load(pwb[0:1, 0:3], W["pw_b"].rearrange("(one j) -> one j", one=1), queue="gpsimd")
            pwt_bf = kb.sb_in(bsc, "sa_pwtbf", [3, S], BF16)
            kb.load(pwt_bf[:], W["pw_wT"], queue="gpsimd")
            kb.wait("tensor", "dg", "ve", "se")
            for c in range(NC):
                kb.t("transpose", out=ps_t[:, 0:3], in_=pwt_bf[:, c * 128:(c + 1) * 128], identity=C["I16"][0:3, 0:3])
                kb.wait("scalar", "pe")
                kb.s("copy", out=pww[:, c * 3:(c + 1) * 3], in_=ps_t[:, 0:3])
                kb.wait("tensor", "se")
            for tt in range(NT):
                kb.wait("tensor", "se", "ve", "dg")
                for c in range(NC):
                    kb.t("matmul", out=ps_sm[:, 0:3], lhsT=xn_fm[:, c * S + tt * 128: c * S + (tt + 1) * 128], rhs=pww[:, c * 3:(c + 1) * 3], start=(c == 0), stop=False)
                kb.t("matmul", out=ps_sm[:, 0:3], lhsT=C["ONES_ROW"][:], rhs=pwb[0:1, 0:3], start=False, stop=True)
                kb.wait("vector", "pe")
                kb.v("tensor_reduce", out=colb[:], in_=ps_sm[:, 0:3], axis=AX.X, op=ALU.max, negate=True)
                kb.wait("scalar", "ve")
                kb.s("activation", out=pw_tm[:, tt * 3:(tt + 1) * 3], in_=ps_sm[:, 0:3], func=AF.Exp, bias=colb[:], accum_out=col2[:])
                kb.wait("vector", "se")
                kb.v("reciprocal", out=col2[:], in_=col2[:])
                kb.v("tensor_scalar", out=pw_tm[:, tt * 3:(tt + 1) * 3], in0=pw_tm[:, tt * 3:(tt + 1) * 3], scalar1=col2[:], scalar2=None, op0=ALU.mult)
            kb.load(nmw[:], W["nm_w1"], queue="gpsimd")
            kb.load(nmw2[:].rearrange("p (c j) -> p c j", j=128), W["nm_w2"].rearrange("(c p) j -> p c j", p=128), queue="gpsimd")
            b1c = kb.sb_in(bsc, "sa_b1c", [128, 2], F32)
            b2c = kb.sb_in(bsc, "sa_b2c", [128, 1], F32)
            kb.load(b1c[:, 0:1], W["nm_b1"][0:128].rearrange("(p one) -> p one", one=1), queue="sync")
            kb.load(b1c[:, 1:2], W["nm_b1"][128:256].rearrange("(p one) -> p one", one=1), queue="sync")
            kb.load(b2c[:], W["nm_b2"].rearrange("(p one) -> p one", one=1), queue="sync")
            kb.wait("tensor", "dg", "dl", "se", "ve")
            for mc in range(2):
                for n0 in range(0, S, 512):
                    kb.t("matmul", out=ps_p[:, n0:n0 + 512], lhsT=nmw[:, mc * 128:(mc + 1) * 128], rhs=vbf[:, n0:n0 + 512], start=True, stop=True)
                kb.wait("scalar", "pe", "ve")
                kb.s("activation", out=g1[:, mc * S:(mc + 1) * S], in_=ps_p[:, 0:S], func=AF.Gelu, bias=b1c[:, mc:mc + 1])
                kb.wait("tensor", "se")
            for n0 in range(0, S, 512):
                for mc in range(2):
                    kb.t("matmul", out=ps_p[:, n0:n0 + 512], lhsT=nmw2[:, mc * 128:(mc + 1) * 128], rhs=g1[:, mc * S + n0: mc * S + n0 + 512], start=(mc == 0), stop=(mc == 1))
            kb.wait("scalar", "pe")
            kb.s("activation", out=nu[:], in_=ps_p[:, 0:S], func=AF.Identity, bias=b2c[:])
        kb.fence()

        if DEBUG:
            kb.dump("dbg_k", k_fm[:], [128, S], F32)
            kb.dump("dbg_q", q_fm[:], [128, S], F32)
            kb.dump("dbg_v", v_fm[:], [128, S], F32)
            kb.dump("dbg_af", af[:], [128, S], F32)
            kb.dump("dbg_af1m", af1m[:], [128, S], F32)
            kb.dump("dbg_as", as_[:], [128, S], F32)
            kb.dump("dbg_as1mU", as1mU[:], [128, S], F32)
            kb.dump("dbg_nu", nu[:], [128, S], F32)
            kb.dump("dbg_pwtm", pw_tm[:], [128, 3 * NT], F32)
        kb.g("memset", fH[:], 0.0)
        kb.g("memset", sH[:], 0.0)
        kb.g("memset", zc[:], 0.0)
        with kb.scope() as ssc:
            kvrow = kb.sb_in(ssc, "sa_kvrow", [1, 256], F32)
            kr = kvrow[0:1, 0:128]
            vr = kvrow[0:1, 128:256]
            predh = kb.sb_in(ssc, "sa_predh", [1, 128], F32)
            e_sb = kb.sb_in(ssc, "sa_e", [1, 128], F32)
            e2_sb = kb.sb_in(ssc, "sa_e2", [1, 128], F32)
            err_sb = kb.sb_in(ssc, "sa_err", [1, 1], F32)
            row2 = kb.sb_in(ssc, "sa_row2", [1, 2], F32)
            wf_sb = kb.sb_in(ssc, "sa_wf", [1, 128], F32)
            ws_sb = kb.sb_in(ssc, "sa_ws", [1, 128], F32)
            gf = kb.sb_in(ssc, "sa_gf", [128, 1], F32)
            gs = kb.sb_in(ssc, "sa_gs", [128, 1], F32)
            tf_sb = kb.sb_in(ssc, "sa_tf", [128, 128], F32)
            ts_sb = kb.sb_in(ssc, "sa_ts", [128, 128], F32)
            d_sb = kb.sb_in(ssc, "sa_d", [128, 1], F32)
            psKV = kb.ps_in(ssc, "sa_psKV", [1, 512], F32)
            psP = kb.ps_in(ssc, "sa_psP", [1, 512], F32)
            psB = kb.ps_in(ssc, "sa_psB", [128, 512], F32)
            psOF = kb.ps_in(ssc, "sa_psOF", [128, 512], F32)
            psOS = kb.ps_in(ssc, "sa_psOS", [128, 512], F32)
            psOutF = kb.ps_in(ssc, "sa_psOutF", [128, 512], F32)
            psOutS = kb.ps_in(ssc, "sa_psOutS", [128, 512], F32)
            kb.fence()
            for t in range(SCAN_STEPS):
                cc = t % 512
                tcol = slice(t, t + 1)
                nSprev = zc[:] if t == 0 else nSarr[:, t - 1:t]
                kb.wait("tensor", "se", "ve")
                kb.t("transpose", out=psKV[0:1, 0:128], in_=k_fm[:, tcol], identity=C["IDENT_F32"][:])
                kb.t("transpose", out=psKV[0:1, 128:256], in_=v_fm[:, tcol], identity=C["IDENT_F32"][:])
                kb.t("matmul", out=psP[0:1, 0:128], lhsT=q_fm[:, tcol], rhs=fH[:], start=True, stop=False)
                kb.t("matmul", out=psP[0:1, 0:128], lhsT=q_fm[:, tcol], rhs=sH[:], start=False, stop=True)
                kb.t("matmul", out=psP[0:1, 128:256], lhsT=k_fm[:, tcol], rhs=fH[:], start=True, stop=True)
                kb.t("matmul", out=psP[0:1, 256:384], lhsT=k_fm[:, tcol], rhs=sH[:], start=True, stop=True)
                kb.wait("scalar", "pe", "ve")
                kb.s("copy", out=kvrow[:], in_=psKV[0:1, 0:256], nodrain=True)
                kb.wait("vector", "se", "pe")
                kb.v("scalar_tensor_tensor", out=e_sb[:], in0=psP[0:1, 0:128], scalar=-0.5, in1=vr, op0=ALU.mult, op1=ALU.add)
                kb.v("scalar_tensor_tensor", out=e2_sb[:], in0=e_sb[:], scalar=1.0, in1=e_sb[:], op0=ALU.mult, op1=ALU.mult, accum_out=err_sb[:])
                kb.v("tensor_tensor", out=wf_sb[:], in0=vr, in1=psP[0:1, 128:256], op=ALU.subtract, nodrain=True)
                kb.v("tensor_tensor", out=ws_sb[:], in0=vr, in1=psP[0:1, 256:384], op=ALU.subtract, nodrain=True)
                kb.wait("scalar", "ve")
                kb.s("activation", out=row2[0:1, 0:1], in_=err_sb[:], func=AF.Sigmoid, scale=SUR_SCALE, nodrain=True)
                kb.v("tensor_scalar", out=row2[0:1, 1:2], in0=err_sb[:], scalar1=SURPRISE_THR_ARG, scalar2=0.1, op0=ALU.is_gt, op1=ALU.mult, nodrain=True)
                kb.wait("tensor", "ve", "se")
                kb.t("matmul", out=psB[:, 0:2], lhsT=C["ONES_ROW_F32"][:], rhs=row2[:], start=True, stop=True)
                kb.t("matmul", out=psOF[:, 0:128], lhsT=kr, rhs=wf_sb[:], start=True, stop=True)
                kb.t("matmul", out=psOS[:, 0:128], lhsT=kr, rhs=ws_sb[:], start=True, stop=True)
                kb.wait("vector", "pe")
                kb.v("tensor_scalar", out=tf_sb[:], in0=psOF[:, 0:128], scalar1=af1m[:, tcol], scalar2=psB[:, 0:1], op0=ALU.mult, op1=ALU.mult)
                kb.v("scalar_tensor_tensor", out=fH[:], in0=fH[:], scalar=af[:, tcol], in1=tf_sb[:], op0=ALU.mult, op1=ALU.add, nodrain=True)
                kb.v("tensor_scalar", out=ts_sb[:], in0=psOS[:, 0:128], scalar1=as1mU[:, tcol], scalar2=psB[:, 0:1], op0=ALU.mult, op1=ALU.mult)
                kb.v("scalar_tensor_tensor", out=sH[:], in0=sH[:], scalar=as_[:, tcol], in1=ts_sb[:], op0=ALU.mult, op1=ALU.add, nodrain=True)
                kb.v("tensor_tensor", out=d_sb[:], in0=nu[:, tcol], in1=nSprev, op=ALU.subtract)
                kb.v("scalar_tensor_tensor", out=nSarr[:, tcol], in0=d_sb[:], scalar=psB[:, 1:2], in1=nSprev, op0=ALU.mult, op1=ALU.add, nodrain=True)
                kb.wait("tensor", "ve")
                kb.t("matmul", out=psOutF[:, cc:cc + 1], lhsT=fH[:], rhs=q_fm[:, tcol], start=True, stop=True)
                kb.t("matmul", out=psOutS[:, cc:cc + 1], lhsT=sH[:], rhs=q_fm[:, tcol], start=True, stop=True)
                if cc == 511:
                    kb.wait("vector", "pe")
                    kb.v("tensor_copy", out=outF[:, t - 511:t + 1], in_=psOutF[:, 0:512], nodrain=True)
                    kb.v("tensor_copy", out=outS[:, t - 511:t + 1], in_=psOutS[:, 0:512], nodrain=True)
        kb.fence()

        if DEBUG:
            kb.dump("dbg_outF", outF[:], [128, S], F32)
            kb.dump("dbg_outS", outS[:], [128, S], F32)
            kb.dump("dbg_nSarr", nSarr[:], [128, S], F32)
        with kb.scope() as fsc:
            outz = kb.sb_in(fsc, "sa_outz", [128, S], F32)
            tmpf = kb.sb_in(fsc, "sa_tmpf", [128, S], F32)
            prow = kb.sb_in(fsc, "sa_prow2", [1, S], F32)
            nwcol = kb.sb_in(fsc, "sa_nwcol", [128, 1], F32)
            out_bf = kb.sb_in(fsc, "sa_outbf", [128, S], BF16)
            wo_bf = kb.sb_in(fsc, "sa_wobf", [128, D], BF16)
            xres = kb.sb_in(fsc, "sa_xres", [128, D], F32)
            outb = kb.sb_in(fsc, "sa_outb", [128, D], F32)
            ps_big = kb.ps_in(fsc, "sa_ps_big", [128, S], F32)
            ps_t3 = kb.ps_in(fsc, "sa_ps_t3", [128, 512], F32)
            kb.load(nwcol[:], W["norm_w"].rearrange("(p one) -> p one", one=1), queue="sync")
            kb.load(wo_bf[:], W["Wo"], queue="gpsimd")
            for j, arr in ((0, outF), (1, outS), (2, nSarr)):
                for t4 in range(0, NT, 4):
                    kb.wait("tensor", "se", "ve")
                    for i in range(4):
                        tt = t4 + i
                        kb.t("transpose", out=ps_t3[0:1, i * 128:(i + 1) * 128], in_=pw_tm[:, tt * 3 + j: tt * 3 + j + 1], identity=C["IDENT_F32"][:])
                    kb.wait("scalar", "pe")
                    kb.s("copy", out=prow[0:1, t4 * 128:(t4 + 4) * 128], in_=ps_t3[0:1, 0:512])
                kb.wait("tensor", "se")
                for n0 in range(0, S, 512):
                    kb.t("matmul", out=ps_big[:, n0:n0 + 512], lhsT=C["ONES_ROW_F32"][:], rhs=prow[0:1, n0:n0 + 512], start=True, stop=True)
                kb.wait("vector", "pe")
                if j == 0:
                    kb.v("tensor_tensor", out=outz[:], in0=arr[:], in1=ps_big[:, 0:S], op=ALU.mult)
                else:
                    kb.v("tensor_tensor", out=tmpf[:], in0=arr[:], in1=ps_big[:, 0:S], op=ALU.mult)
                    kb.v("tensor_tensor", out=outz[:], in0=outz[:], in1=tmpf[:], op=ALU.add)
            kb.wait("scalar", "ve")
            kb.s("activation", out=tmpf[:], in_=outz[:], func=AF.Square)
            kb.wait("tensor", "se", "ve")
            for n0 in range(0, S, 512):
                kb.t("matmul", out=ps_big[0:1, n0:n0 + 512], lhsT=C["ONES_COL_F32"][:], rhs=tmpf[:, n0:n0 + 512], start=True, stop=True)
            kb.wait("vector", "pe")
            kb.v("tensor_scalar", out=ps_big[0:1, 0:S], in0=ps_big[0:1, 0:S], scalar1=1.0 / 128.0, scalar2=1e-6, op0=ALU.mult, op1=ALU.add)
            kb.wait("scalar", "ve")
            kb.s("activation", out=prow[:], in_=ps_big[0:1, 0:S], func=AF.Sqrt)
            kb.wait("vector", "se")
            kb.v("reciprocal", out=prow[:], in_=prow[:])
            kb.wait("tensor", "ve")
            for n0 in range(0, S, 512):
                kb.t("matmul", out=ps_big[:, n0:n0 + 512], lhsT=C["ONES_ROW_F32"][:], rhs=prow[0:1, n0:n0 + 512], start=True, stop=True)
            kb.wait("vector", "pe")
            kb.v("tensor_tensor", out=outz[:], in0=outz[:], in1=ps_big[:, 0:S], op=ALU.mult)
            kb.v("tensor_scalar", out=out_bf[:], in0=outz[:], scalar1=nwcol[:], scalar2=None, op0=ALU.mult)
            for tt in range(NT):
                kb.wait("sync", "ve")
                kb.load(xres[:], x_in[tt * 128:(tt + 1) * 128, :])
                kb.wait("tensor", "ve", "dg")
                for n0 in range(0, D, 512):
                    kb.t("matmul", out=ps_big[:, n0:n0 + 512], lhsT=out_bf[:, tt * 128:(tt + 1) * 128], rhs=wo_bf[:, n0:n0 + 512], start=True, stop=True)
                kb.wait("vector", "pe", "dl")
                kb.v("tensor_tensor", out=outb[:], in0=ps_big[:, 0:D], in1=xres[:], op=ALU.add)
                kb.store(x_out[tt * 128:(tt + 1) * 128, :], outb[:], wait_on=("ve",))
                kb.wait("vector", "st")
    kb.fence()



SUB_WEIGHTS = {
    "l0_attn_a": ["L0_attn_a_W_K", "L0_attn_a_W_Q", "L0_attn_a_W_V", "L0_attn_a_W_O"],
    "l0_attn_b": ["L0_attn_b_Wq", "L0_attn_b_Wk", "L0_attn_b_Wv", "L0_attn_b_Wo",
                  "L0_attn_b_lambda_q1", "L0_attn_b_lambda_k1", "L0_attn_b_lambda_q2",
                  "L0_attn_b_lambda_k2", "L0_attn_b_ln_w", "L0_attn_b_ln_b"],
    "l1_attn_b": ["L1_attn_b_Wq", "L1_attn_b_bq", "L1_attn_b_Wk", "L1_attn_b_bk",
                  "L1_attn_b_Wv", "L1_attn_b_bv", "L1_attn_b_Wo", "L1_attn_b_bo"],
    "l1_ffn": ["L1_ffn_w_gate_up", "L1_ffn_w_proj"],
    "l0_ffn": ["L0_ffn_gate_w", "L0_ffn_w1", "L0_ffn_w3", "L0_ffn_w2", "L0_ffn_a1",
               "L0_ffn_b1", "L0_ffn_a3", "L0_ffn_b3", "L0_ffn_a2", "L0_ffn_b2"],
    "l1_attn_a": ["L1_attn_a_Wk", "L1_attn_a_Wv", "L1_attn_a_Wq", "L1_attn_a_fast_lambda",
                  "L1_attn_a_slow_lambda", "L1_attn_a_Waf", "L1_attn_a_baf", "L1_attn_a_Was",
                  "L1_attn_a_bas", "L1_attn_a_nm_w1", "L1_attn_a_nm_b1", "L1_attn_a_nm_w2",
                  "L1_attn_a_nm_b2", "L1_attn_a_ir_w1", "L1_attn_a_ir_b1", "L1_attn_a_ir_w2",
                  "L1_attn_a_ir_b2", "L1_attn_a_pw_b", "L1_attn_a_Wo", "L1_attn_a_norm_w"],
}
SUB_CONSTS = {
    "l0_attn_a": ["IDENT_BF", "MASKA"],
    "l0_attn_b": ["IDENT_BF", "IDENT_F32", "ONES_ROW_F32", "MASKB", "COS", "SIN", "PROT_T"],
    "l1_attn_b": ["IDENT_BF", "IDENT_F32", "ONES_ROW", "ONES_COL", "DECAY_TM", "DECAY_BC"],
    "l1_ffn": ["IDENT_BF", "ONES_ROW_F32", "ONES_COL"],
    "l0_ffn": ["IDENT_BF", "IDENT_F32", "I16", "ONES_ROW_F32"],
    "l1_attn_a": ["IDENT_BF", "IDENT_F32", "I16", "ONES_ROW", "ONES_ROW_F32", "ONES_COL_F32", "EPS8_COL"],
}


def build_l0_attn_a():
    kb = KB()
    x_in = kb.din("x_in", [S, D], F32)
    x_out = kb.dout("x_out", [S, D], F32)
    W = {
        "W_K": kb.din("L0_attn_a_W_K", [16, 128, D], F32),
        "W_Q": kb.din("L0_attn_a_W_Q", [16, 128, D], F32),
        "W_V": kb.din("L0_attn_a_W_V", [16, 128, D], F32),
        "W_O": kb.din("L0_attn_a_W_O", [D, D], F32),
    }
    C = emit_consts(kb, SUB_CONSTS["l0_attn_a"])
    emit_l0_attn_a(kb, C, x_in, x_out, W)
    kb.finish()
    kb.close()
    return kb


def build_l0_attn_b():
    kb = KB()
    x_in = kb.din("x_in", [S, D], F32)
    x_out = kb.dout("x_out", [S, D], F32)
    W = {
        "Wq": kb.din("L0_attn_b_Wq", [D, D], F32),
        "Wk": kb.din("L0_attn_b_Wk", [D, D], F32),
        "Wv": kb.din("L0_attn_b_Wv", [D, D], F32),
        "Wo": kb.din("L0_attn_b_Wo", [D, D], F32),
        "lambda_q1": kb.din("L0_attn_b_lambda_q1", [64], F32),
        "lambda_k1": kb.din("L0_attn_b_lambda_k1", [64], F32),
        "lambda_q2": kb.din("L0_attn_b_lambda_q2", [64], F32),
        "lambda_k2": kb.din("L0_attn_b_lambda_k2", [64], F32),
        "ln_w": kb.din("L0_attn_b_ln_w", [128], F32),
        "ln_b": kb.din("L0_attn_b_ln_b", [128], F32),
    }
    C = emit_consts(kb, SUB_CONSTS["l0_attn_b"])
    emit_l0_attn_b(kb, C, x_in, x_out, W)
    kb.finish()
    kb.close()
    return kb


def build_l1_attn_b():
    kb = KB()
    x_in = kb.din("x_in", [S, D], F32)
    x_out = kb.dout("x_out", [S, D], F32)
    W = {
        "Wq": kb.din("L1_attn_b_Wq", [D, D], F32),
        "bq": kb.din("L1_attn_b_bq", [D], F32),
        "Wk": kb.din("L1_attn_b_Wk", [D, D], F32),
        "bk": kb.din("L1_attn_b_bk", [D], F32),
        "Wv": kb.din("L1_attn_b_Wv", [D, D], F32),
        "bv": kb.din("L1_attn_b_bv", [D], F32),
        "Wo": kb.din("L1_attn_b_Wo", [D, D], F32),
        "bo": kb.din("L1_attn_b_bo", [D], F32),
        "DECAY_BC": kb.din("DECAY_BC", [128, S], F32),
    }
    C = emit_consts(kb, SUB_CONSTS["l1_attn_b"][:-1])
    emit_l1_attn_b(kb, C, x_in, x_out, W)
    kb.finish()
    kb.close()
    return kb


def build_l0_ffn():
    kb = KB()
    x_in = kb.din("x_in", [S, D], F32)
    x_out = kb.dout("x_out", [S, D], F32)
    W = {
        "gate_w": kb.din("L0_ffn_gate_w", [8, D], F32),
        "w1": kb.din("L0_ffn_w1", [F0, D], F32),
        "w3": kb.din("L0_ffn_w3", [F0, D], F32),
        "w2": kb.din("L0_ffn_w2", [D, F0], F32),
        "a1": kb.din("L0_ffn_a1", [8, 16, D], F32),
        "b1": kb.din("L0_ffn_b1", [8, F0, 16], F32),
        "a3": kb.din("L0_ffn_a3", [8, 16, D], F32),
        "b3": kb.din("L0_ffn_b3", [8, F0, 16], F32),
        "a2": kb.din("L0_ffn_a2", [8, 16, F0], F32),
        "b2": kb.din("L0_ffn_b2", [8, D, 16], F32),
    }
    C = emit_consts(kb, SUB_CONSTS["l0_ffn"])
    scr = (kb.dtmp("mf_w1bf", [F0, D], BF16), kb.dtmp("mf_w3bf", [F0, D], BF16),
           kb.dtmp("mf_w2bf", [D, F0], BF16), kb.dtmp("mf_w1T", [D, F0], BF16),
           kb.dtmp("mf_w3T", [D, F0], BF16), kb.dtmp("mf_w2T", [F0, D], BF16))
    out_acc = kb.dtmp("mf_outacc", [S, D], F32)
    emit_l0_ffn_prepass(kb, C, W, scr)
    emit_l0_ffn(kb, C, x_in, x_out, W, scr, out_acc)
    kb.finish()
    kb.close()
    return kb


def build_l1_attn_a():
    kb = KB()
    x_in = kb.din("x_in", [S, D], F32)
    x_out = kb.dout("x_out", [S, D], F32)
    W = {}
    for short, name, shape in (
        ("Wk", "L1_attn_a_Wk", [D, 128]), ("Wv", "L1_attn_a_Wv", [D, 128]), ("Wq", "L1_attn_a_Wq", [D, 128]),
        ("fast_lambda", "L1_attn_a_fast_lambda", [128]), ("slow_lambda", "L1_attn_a_slow_lambda", [128]),
        ("Waf", "L1_attn_a_Waf", [D, 128]), ("baf", "L1_attn_a_baf", [128]),
        ("Was", "L1_attn_a_Was", [D, 128]), ("bas", "L1_attn_a_bas", [128]),
        ("nm_w1", "L1_attn_a_nm_w1", [128, 256]), ("nm_b1", "L1_attn_a_nm_b1", [256]),
        ("nm_w2", "L1_attn_a_nm_w2", [256, 128]), ("nm_b2", "L1_attn_a_nm_b2", [128]),
        ("ir_w1", "L1_attn_a_ir_w1", [D, 128]), ("ir_b1", "L1_attn_a_ir_b1", [128]),
        ("ir_w2", "L1_attn_a_ir_w2", [128, 1]), ("ir_b2", "L1_attn_a_ir_b2", [1]),
        ("pw_b", "L1_attn_a_pw_b", [3]), ("Wo", "L1_attn_a_Wo", [128, D]),
        ("norm_w", "L1_attn_a_norm_w", [128]),
    ):
        W[short] = kb.din(name, shape, F32)
    W["pw_wT"] = kb.din("L1_attn_a_pw_wT", [3, D], F32)
    C = emit_consts(kb, SUB_CONSTS["l1_attn_a"])
    emit_l1_attn_a(kb, C, x_in, x_out, W)
    kb.finish()
    kb.close()
    return kb


def build_l1_ffn():
    kb = KB()
    x_in = kb.din("x_in", [S, D], F32)
    x_out = kb.dout("x_out", [S, D], F32)
    W = {
        "w_gate_up": kb.din("L1_ffn_w_gate_up", [2 * H1F, D], F32),
        "w_proj": kb.din("L1_ffn_w_proj", [D, H1F], F32),
    }
    C = emit_consts(kb, SUB_CONSTS["l1_ffn"])
    emit_l1_ffn(kb, C, x_in, x_out, W)
    kb.finish()
    kb.close()
    return kb


BUILDERS = {
    1: (build_l0_attn_a, SUB_WEIGHTS["l0_attn_a"], SUB_CONSTS["l0_attn_a"]),
    2: (build_l0_ffn, SUB_WEIGHTS["l0_ffn"], SUB_CONSTS["l0_ffn"]),
    4: (build_l0_ffn, SUB_WEIGHTS["l0_ffn"], SUB_CONSTS["l0_ffn"]),
    5: (build_l1_attn_a, SUB_WEIGHTS["l1_attn_a"], SUB_CONSTS["l1_attn_a"]),
    3: (build_l0_attn_b, SUB_WEIGHTS["l0_attn_b"], SUB_CONSTS["l0_attn_b"]),
    6: (build_l1_ffn, SUB_WEIGHTS["l1_ffn"], SUB_CONSTS["l1_ffn"]),
    7: (build_l1_attn_b, SUB_WEIGHTS["l1_attn_b"], SUB_CONSTS["l1_attn_b"]),
    8: (build_l1_ffn, SUB_WEIGHTS["l1_ffn"], SUB_CONSTS["l1_ffn"]),
}


ALL_CONSTS = ["IDENT_BF", "IDENT_F32", "I16", "ONES_ROW", "ONES_COL", "ONES_ROW_F32",
              "ONES_COL_F32", "EPS8_COL", "MASKA", "MASKB", "COS", "SIN", "PROT_T",
              "DECAY_TM", "DECAY_BC"]
SMALL_CONSTS = ["IDENT_BF", "IDENT_F32", "I16", "ONES_ROW", "ONES_COL", "ONES_ROW_F32",
                "ONES_COL_F32", "EPS8_COL", "MASKA", "MASKB", "PROT_T", "DECAY_TM"]
BIG_CONSTS = ["COS", "SIN", "DECAY_BC"]


def build_fused():
    kb = KB()
    x_in = kb.din("x_in", [S, D], F32)
    x_out = kb.dout("x_out", [S, D], F32)
    xa = kb.dtmp("x_a", [S, D], F32)
    xb = kb.dtmp("x_b", [S, D], F32)
    C = emit_consts(kb, SMALL_CONSTS)
    CAPS = {nm: kb.din(nm, *CONST_SPECS[nm]) for nm in BIG_CONSTS}

    def with_big(scope, names):
        d = dict(C)
        for nm in names:
            tile = kb.sb_in(scope, "cb_" + nm, *CONST_SPECS[nm])
            kb.load(tile[:], CAPS[nm])
            d[nm] = tile
        for eng in ("tensor", "vector", "scalar", "gpsimd"):
            kb.wait(eng, "dl")
        return d
    W_aa = {
        "W_K": kb.din("L0_attn_a_W_K", [16, 128, D], F32),
        "W_Q": kb.din("L0_attn_a_W_Q", [16, 128, D], F32),
        "W_V": kb.din("L0_attn_a_W_V", [16, 128, D], F32),
        "W_O": kb.din("L0_attn_a_W_O", [D, D], F32),
    }
    W_mf = {
        "gate_w": kb.din("L0_ffn_gate_w", [8, D], F32),
        "w1": kb.din("L0_ffn_w1", [F0, D], F32),
        "w3": kb.din("L0_ffn_w3", [F0, D], F32),
        "w2": kb.din("L0_ffn_w2", [D, F0], F32),
        "a1": kb.din("L0_ffn_a1", [8, 16, D], F32),
        "b1": kb.din("L0_ffn_b1", [8, F0, 16], F32),
        "a3": kb.din("L0_ffn_a3", [8, 16, D], F32),
        "b3": kb.din("L0_ffn_b3", [8, F0, 16], F32),
        "a2": kb.din("L0_ffn_a2", [8, 16, F0], F32),
        "b2": kb.din("L0_ffn_b2", [8, D, 16], F32),
    }
    W_ab = {
        "Wq": kb.din("L0_attn_b_Wq", [D, D], F32),
        "Wk": kb.din("L0_attn_b_Wk", [D, D], F32),
        "Wv": kb.din("L0_attn_b_Wv", [D, D], F32),
        "Wo": kb.din("L0_attn_b_Wo", [D, D], F32),
        "lambda_q1": kb.din("L0_attn_b_lambda_q1", [64], F32),
        "lambda_k1": kb.din("L0_attn_b_lambda_k1", [64], F32),
        "lambda_q2": kb.din("L0_attn_b_lambda_q2", [64], F32),
        "lambda_k2": kb.din("L0_attn_b_lambda_k2", [64], F32),
        "ln_w": kb.din("L0_attn_b_ln_w", [128], F32),
        "ln_b": kb.din("L0_attn_b_ln_b", [128], F32),
    }
    W_sa = {}
    for short, name, shape in (
        ("Wk", "L1_attn_a_Wk", [D, 128]), ("Wv", "L1_attn_a_Wv", [D, 128]), ("Wq", "L1_attn_a_Wq", [D, 128]),
        ("fast_lambda", "L1_attn_a_fast_lambda", [128]), ("slow_lambda", "L1_attn_a_slow_lambda", [128]),
        ("Waf", "L1_attn_a_Waf", [D, 128]), ("baf", "L1_attn_a_baf", [128]),
        ("Was", "L1_attn_a_Was", [D, 128]), ("bas", "L1_attn_a_bas", [128]),
        ("nm_w1", "L1_attn_a_nm_w1", [128, 256]), ("nm_b1", "L1_attn_a_nm_b1", [256]),
        ("nm_w2", "L1_attn_a_nm_w2", [256, 128]), ("nm_b2", "L1_attn_a_nm_b2", [128]),
        ("ir_w1", "L1_attn_a_ir_w1", [D, 128]), ("ir_b1", "L1_attn_a_ir_b1", [128]),
        ("ir_w2", "L1_attn_a_ir_w2", [128, 1]), ("ir_b2", "L1_attn_a_ir_b2", [1]),
        ("pw_b", "L1_attn_a_pw_b", [3]), ("Wo", "L1_attn_a_Wo", [128, D]),
        ("norm_w", "L1_attn_a_norm_w", [128]),
    ):
        W_sa[short] = kb.din(name, shape, F32)
    W_sa["pw_wT"] = kb.din("L1_attn_a_pw_wT", [3, D], F32)
    W_l1f = {
        "w_gate_up": kb.din("L1_ffn_w_gate_up", [2 * H1F, D], F32),
        "w_proj": kb.din("L1_ffn_w_proj", [D, H1F], F32),
    }
    W_lb = {
        "Wq": kb.din("L1_attn_b_Wq", [D, D], F32),
        "bq": kb.din("L1_attn_b_bq", [D], F32),
        "Wk": kb.din("L1_attn_b_Wk", [D, D], F32),
        "bk": kb.din("L1_attn_b_bk", [D], F32),
        "Wv": kb.din("L1_attn_b_Wv", [D, D], F32),
        "bv": kb.din("L1_attn_b_bv", [D], F32),
        "Wo": kb.din("L1_attn_b_Wo", [D, D], F32),
        "bo": kb.din("L1_attn_b_bo", [D], F32),
        "DECAY_BC": CAPS["DECAY_BC"],
    }
    mf_scr = (kb.dtmp("mf_w1bf", [F0, D], BF16), kb.dtmp("mf_w3bf", [F0, D], BF16),
              kb.dtmp("mf_w2bf", [D, F0], BF16), kb.dtmp("mf_w1T", [D, F0], BF16),
              kb.dtmp("mf_w3T", [D, F0], BF16), kb.dtmp("mf_w2T", [F0, D], BF16))
    l1f_scr = (kb.dtmp("l1f_gu", [2 * H1F, D], BF16), kb.dtmp("l1f_p", [D, H1F], BF16),
               kb.dtmp("l1f_guT", [D, 2 * H1F], BF16), kb.dtmp("l1f_pT", [H1F, D], BF16))

    emit_l0_attn_a(kb, C, x_in, xa, W_aa)
    kb.fence()
    mf_oacc = kb.dtmp("mf_outacc", [S, D], F32)
    emit_l0_ffn_prepass(kb, C, W_mf, mf_scr)
    emit_l0_ffn(kb, C, xa, xb, W_mf, mf_scr, mf_oacc)
    kb.fence()
    with kb.scope() as s_ab:
        emit_l0_attn_b(kb, with_big(s_ab, ["COS", "SIN"]), xb, xa, W_ab)
        kb.fence()
    emit_l0_ffn(kb, C, xa, xb, W_mf, mf_scr, mf_oacc)
    kb.fence()
    emit_l1_attn_a(kb, C, xb, xa, W_sa)
    kb.fence()
    emit_l1_ffn(kb, C, xa, xb, W_l1f, quantize=True, scratch=l1f_scr)
    kb.fence()
    emit_l1_attn_b(kb, C, xb, xa, W_lb)
    kb.fence()
    emit_l1_ffn(kb, C, xa, x_out, W_l1f, quantize=False, scratch=l1f_scr)
    kb.finish()
    kb.close()
    return kb


FUSED = True


def extra_inputs(sub, inputs):
    if sub == 5:
        return {"L1_attn_a_pw_wT": np.ascontiguousarray(inputs["L1_attn_a_pw_w"].T)}
    return {}


def kernel(**inputs):
    _install_memtrim()
    consts = make_consts()
    x = np.ascontiguousarray(inputs["L0_attn_a_x"], dtype=np.float32)
    ncores = 8
    if FUSED:
        kb = build_fused()
        wnames = sorted(set(n for ns in SUB_WEIGHTS.values() for n in ns))
        in_maps = []
        extra = extra_inputs(5, inputs)
        for i in range(ncores):
            m = {"x_in": x[i]}
            for n in wnames:
                m[n] = inputs[n]
            for n in ALL_CONSTS:
                m[n] = consts[n]
            m.update(extra)
            in_maps.append(m)
        res = run_bass_kernel_spmd(kb.nc, in_maps, core_ids=list(range(ncores)))
        return np.ascontiguousarray(np.stack([np.asarray(r["x_out"]) for r in res.results]))
    order = [1, 2, 3, 4, 5, 6, 7, 8]
    built = {}
    for sub in order:
        build, wnames, cnames = BUILDERS[sub]
        if build not in built:
            built[build] = build()
        kb = built[build]
        in_maps = []
        extra = extra_inputs(sub, inputs)
        for i in range(ncores):
            m = {"x_in": x[i]}
            for n in wnames:
                m[n] = inputs[n]
            for n in cnames:
                m[n] = consts[n]
            m.update(extra)
            in_maps.append(m)
        res = run_bass_kernel_spmd(kb.nc, in_maps, core_ids=list(range(ncores)))
        x = np.ascontiguousarray(np.stack([np.asarray(r["x_out"]) for r in res.results]))
    return x


# version 06
# speedup vs baseline: 1.1771x; 1.1771x over previous
import math
import contextlib
import numpy as np
import ml_dtypes

import concourse.bass as bass
import concourse.mybir as mybir
from concourse.bass_utils import run_bass_kernel_spmd


def _install_memtrim():
    import concourse.bass2jax as b2j
    if getattr(b2j, "_memtrim_installed", False):
        return
    orig = b2j.run_bass_via_pjrt

    def wrapped(*a, **k):
        r = orig(*a, **k)
        import gc as _gc
        import ctypes as _ct
        _gc.collect()
        try:
            _ct.CDLL("libc.so.6").malloc_trim(0)
        except Exception:
            pass
        return r

    b2j.run_bass_via_pjrt = wrapped
    b2j._memtrim_installed = True

BFNP = ml_dtypes.bfloat16
F32 = mybir.dt.float32
BF16 = mybir.dt.bfloat16
AF = mybir.ActivationFunctionType
ALU = mybir.AluOpType
AX = mybir.AxisListType

S = 2048
D = 2048
NT = S // 128
NC = D // 128
LN_EPS = 1e-5
ISQ = float(1.0 / math.sqrt(128.0))


class KB:

    KEY = {"ve": "cs", "se": "cs", "ge": "cs", "dl": "dma", "dg": "dma", "st": "dma",
           "pe": "pe", "cs": "cs", "dma": "dma"}

    def __init__(self):
        self.nc = bass.Bass("TRN2", target_bir_lowering=False)
        self.es = contextlib.ExitStack()
        self.sem = {}
        self.cnt = {}
        for name in ("pe", "cs", "dma"):
            self.sem[name] = self.es.enter_context(self.nc.semaphore(name))
            self.cnt[name] = 0
        self._uid = 0
        self._lastw = {e: {} for e in ("tensor", "vector", "scalar", "gpsimd", "sync")}

    def _nm(self, name):
        self._uid += 1
        return f"{name}_{self._uid}"

    def sb(self, name, shape, dt):
        return self.es.enter_context(self.nc.sbuf_tensor(self._nm(name), shape, dt))

    def ps(self, name, shape, dt=F32):
        return self.es.enter_context(self.nc.psum_tensor(self._nm(name), shape, dt))

    def scope(self):
        return contextlib.ExitStack()

    def sb_in(self, scope, name, shape, dt, side=None):
        kw = {"side": side} if side else {}
        return scope.enter_context(self.nc.sbuf_tensor(self._nm(name), shape, dt, **kw))

    def ps_in(self, scope, name, shape, dt=F32):
        return scope.enter_context(self.nc.psum_tensor(self._nm(name), shape, dt))

    def din(self, name, shape, dt):
        return self.nc.dram_tensor(name, shape, dt, kind="ExternalInput").ap()

    def dout(self, name, shape, dt):
        return self.nc.dram_tensor(name, shape, dt, kind="ExternalOutput").ap()

    def dtmp(self, name, shape, dt):
        return self.nc.dram_tensor(name, shape, dt, kind="Internal").ap()

    def wait(self, eng, *names):
        e = getattr(self.nc, eng)
        lw = self._lastw[eng]
        for n in set(self.KEY[n] for n in names):
            c = self.cnt[n]
            if c > 0 and lw.get(n, -1) < c:
                e.wait_ge(self.sem[n], c)
                lw[n] = c

    def _inc(self, instr, name, k):
        name = self.KEY[name]
        instr.then_inc(self.sem[name], k)
        self.cnt[name] += k

    def t(self, meth, *a, **k):
        instr = getattr(self.nc.tensor, meth)(*a, **k)
        self._inc(instr, "pe", 1)
        return instr

    def v(self, meth, *a, **k):
        nodrain = k.pop("nodrain", False)
        instr = getattr(self.nc.vector, meth)(*a, **k)
        self._inc(instr, "ve", 1)
        if not nodrain:
            self.nc.vector.drain()
        return instr

    def s(self, meth, *a, **k):
        nodrain = k.pop("nodrain", meth in ("copy", "mul"))
        instr = getattr(self.nc.scalar, meth)(*a, **k)
        self._inc(instr, "se", 1)
        if not nodrain:
            self.nc.scalar.drain()
        return instr

    def g(self, meth, *a, **k):
        instr = getattr(self.nc.gpsimd, meth)(*a, **k)
        self._inc(instr, "ge", 1)
        self.nc.gpsimd.drain()
        return instr

    def load(self, out, in_, queue="sync"):
        if queue == "sync":
            self._inc(self.nc.sync.dma_start(out=out, in_=in_), "dl", 16)
        else:
            self._inc(self.nc.gpsimd.dma_start(out=out, in_=in_), "dg", 16)

    def loadT(self, out, in_):
        self._inc(self.nc.sync.dma_start(out=out, in_=in_, transpose=True), "dl", 16)

    def store(self, out, in_, wait_on=("ve",)):
        self.wait("sync", *wait_on)
        self._inc(self.nc.sync.dma_start(out=out, in_=in_), "st", 16)

    def dump(self, name, tile, shape, dt, wait_on=("ve", "se", "pe")):
        ap = self.dout(name, shape, dt)
        self.store(ap, tile, wait_on=wait_on)

    def fence(self):
        for eng in ("tensor", "vector", "scalar", "gpsimd", "sync"):
            self.wait(eng, "pe", "cs", "dma")

    def finish(self):
        self.nc.sync.wait_ge(self.sem["dma"], self.cnt["dma"])

    def close(self):
        self.es.close()



def make_consts():
    c = {}
    c["IDENT_BF"] = np.eye(128, dtype=BFNP)
    c["IDENT_F32"] = np.eye(128, dtype=np.float32)
    c["I16"] = np.eye(16, dtype=BFNP)
    c["ONES_ROW"] = np.ones((1, 128), dtype=BFNP)
    c["ONES_COL"] = np.ones((128, 1), dtype=BFNP)
    iu = np.triu(np.ones((128, 128), dtype=np.float32), k=1)
    c["MASKA"] = (-1e10) * iu
    c["MASKB"] = (-1e30) * iu
    hd = 64
    inv = 1.0 / (10000.0 ** (np.arange(0, hd, 2, dtype=np.float32) / hd))
    t = np.arange(S, dtype=np.float32)
    fr = np.repeat(t[:, None] * inv[None, :], 2, axis=1)
    cos = np.cos(fr).T.astype(np.float32)
    sin = np.sin(fr).T.astype(np.float32)
    c["COS"] = np.concatenate([cos, cos], axis=0)
    c["SIN"] = np.concatenate([sin, sin], axis=0)
    P = np.zeros((128, 128), dtype=np.float32)
    for g in range(2):
        for i in range(hd // 2):
            P[g * hd + 2 * i, g * hd + 2 * i + 1] = -1.0
            P[g * hd + 2 * i + 1, g * hd + 2 * i] = 1.0
    c["PROT_T"] = P.T.copy().astype(BFNP)
    dec = np.exp(-np.arange(S, dtype=np.float32) / float(S))
    c["DECAY_ROW"] = dec.reshape(1, S).copy()
    c["DECAY_TM"] = dec.reshape(NT, 128).T.copy()
    c["DECAY_BC"] = np.tile(dec.reshape(1, S), (128, 1))
    c["ONES_ROW_F32"] = np.ones((1, 128), dtype=np.float32)
    c["ONES_COL_F32"] = np.ones((128, 1), dtype=np.float32)
    c["EPS8_COL"] = np.full((128, 1), 1e-8, dtype=np.float32)
    return c


CONST_SPECS = {
    "IDENT_BF": ([128, 128], BF16), "IDENT_F32": ([128, 128], F32),
    "I16": ([16, 16], BF16), "ONES_ROW": ([1, 128], BF16), "ONES_COL": ([128, 1], BF16),
    "MASKA": ([128, 128], F32), "MASKB": ([128, 128], F32),
    "COS": ([128, S], F32), "SIN": ([128, S], F32), "PROT_T": ([128, 128], BF16),
    "DECAY_ROW": ([1, S], F32), "DECAY_TM": ([128, NT], F32),
    "DECAY_BC": ([128, S], F32), "ONES_ROW_F32": ([1, 128], F32),
    "ONES_COL_F32": ([128, 1], F32), "EPS8_COL": ([128, 1], F32),
}


def emit_consts(kb, names):
    t = {}
    specs = {k: CONST_SPECS[k] for k in names}
    for name, (shape, dt) in specs.items():
        ap = kb.din(name, shape, dt)
        tile = kb.sb("c_" + name, shape, dt)
        kb.load(tile[:], ap)
        t[name] = tile
    for eng in ("tensor", "vector", "scalar", "gpsimd"):
        kb.wait(eng, "dl")
    return t


def emit_ln(kb, C, x_in, xn_fm, xtile, scr):
    nc = kb.nc
    mean, rstd, var = scr["mean"], scr["rstd"], scr["var"]
    sumc, sqc = scr["sumc"], scr["sqc"]
    xsq = scr["xsq"]
    xn_tm = scr["xn_tm"]
    pst = scr["pst"]
    for tt in range(NT):
        kb.wait("sync", "pe")
        kb.wait("sync", "ve", "se")
        kb.load(xtile[:], x_in[tt * 128:(tt + 1) * 128, :])
        kb.wait("vector", "dl")
        kb.v("tensor_reduce", out=sumc[:], in_=xtile[:], axis=AX.X, op=ALU.add)
        kb.wait("scalar", "dl")
        kb.wait("scalar", "ve")
        kb.s("activation", out=xsq[:], in_=xtile[:], func=AF.Square, accum_out=sqc[:])
        kb.wait("vector", "se")
        kb.v("tensor_scalar", out=mean[:], in0=sumc[:], scalar1=1.0 / D, scalar2=None, op0=ALU.mult)
        kb.v("tensor_tensor", out=var[:], in0=mean[:], in1=mean[:], op=ALU.mult)
        kb.v("scalar_tensor_tensor", out=var[:], in0=sqc[:], scalar=1.0 / D, in1=var[:], op0=ALU.mult, op1=ALU.subtract)
        kb.v("tensor_scalar", out=var[:], in0=var[:], scalar1=LN_EPS, scalar2=None, op0=ALU.add)
        kb.wait("scalar", "ve")
        kb.s("activation", out=rstd[:], in_=var[:], func=AF.Sqrt)
        kb.wait("vector", "se")
        kb.v("reciprocal", out=rstd[:], in_=rstd[:])
        kb.wait("vector", "pe")
        kb.v("tensor_scalar", out=xn_tm[:], in0=xtile[:], scalar1=mean[:], scalar2=rstd[:], op0=ALU.subtract, op1=ALU.mult)
        kb.wait("tensor", "ve")
        for c4 in range(4):
            kb.wait("tensor", "se")
            for j in range(4):
                c = c4 * 4 + j
                kb.t("transpose", out=pst[:, j * 128:(j + 1) * 128], in_=xn_tm[:, c * 128:(c + 1) * 128], identity=C["IDENT_BF"][:])
            kb.wait("scalar", "pe")
            for j in range(4):
                c = c4 * 4 + j
                kb.s("copy", out=xn_fm[:, c * S + tt * 128: c * S + (tt + 1) * 128], in_=pst[:, j * 128:(j + 1) * 128])


def ln_scratch(kb, scope, tag=""):
    scr = {
        "mean": kb.sb_in(scope, f"ln_mean{tag}", [128, 1], F32),
        "rstd": kb.sb_in(scope, f"ln_rstd{tag}", [128, 1], F32),
        "var": kb.sb_in(scope, f"ln_var{tag}", [128, 1], F32),
        "sumc": kb.sb_in(scope, f"ln_sum{tag}", [128, 1], F32),
        "sqc": kb.sb_in(scope, f"ln_sq{tag}", [128, 1], F32),
        "xsq": kb.sb_in(scope, f"ln_xsq{tag}", [128, D], F32),
        "xn_tm": kb.sb_in(scope, f"ln_xn_tm{tag}", [128, D], BF16),
        "pst": kb.ps_in(scope, f"ln_pst{tag}", [128, 512], BF16),
    }
    return scr



DEBUG = False


def emit_l0_attn_a(kb, C, x_in, x_out, W):
    nc = kb.nc
    with kb.scope() as sc:
        z_set = kb.sb_in(sc, "aa_z", [128, 16 * S], BF16, side="right")
        with kb.scope() as xnsc:
            xn_fm = kb.sb_in(xnsc, "aa_xn_fm", [128, NC * S], BF16)
            with kb.scope() as lnsc:
                xtile = kb.sb_in(lnsc, "aa_xtile", [128, D], F32)
                scr = ln_scratch(kb, lnsc, "aa")
                emit_ln(kb, C, x_in, xn_fm, xtile, scr)
            kb.fence()
            hsc = xnsc
            w_nat = kb.sb_in(hsc, "aa_wnat", [128, D], BF16)
            wT = kb.sb_in(hsc, "aa_wT", [128, D], BF16)
            q_h = kb.sb_in(hsc, "aa_qh", [128, S], BF16)
            k_h = kb.sb_in(hsc, "aa_kh", [128, S], BF16)
            v4 = kb.sb_in(hsc, "aa_v4", [128, 4 * NT * 128], BF16)
            w4T = kb.sb_in(hsc, "aa_w4T", [128, NC * 512], BF16)
            p_sb = kb.sb_in(hsc, "aa_p", [128, S], BF16)
            pt_sb = kb.sb_in(hsc, "aa_pt", [128, S], BF16)
            nmax = kb.sb_in(hsc, "aa_nmax", [128, 1], F32)
            bias = kb.sb_in(hsc, "aa_bias", [128, 1], F32)
            rs = kb.sb_in(hsc, "aa_rs", [128, 1], F32)
            rinv = kb.sb_in(hsc, "aa_rinv", [128, 1], F32)
            ps_s = kb.ps_in(hsc, "aa_ps_s", [128, S], F32)
            ps_tr = kb.ps_in(hsc, "aa_ps_tr", [128, 1024], BF16)
            ps_z = kb.ps_in(hsc, "aa_ps_z", [128, 512], F32)
            ps_v = kb.ps_in(hsc, "aa_ps_v", [128, 512], F32)

            def transpose_weight(src_hbm_head, dst_wT):
                kb.wait("gpsimd", "pe")
                kb.load(w_nat[:], src_hbm_head, queue="gpsimd")
                kb.wait("tensor", "dg")
                for c4 in range(4):
                    kb.wait("tensor", "se")
                    for j in range(4):
                        c = c4 * 4 + j
                        kb.t("transpose", out=ps_tr[:, j * 128:(j + 1) * 128], in_=w_nat[:, c * 128:(c + 1) * 128], identity=C["IDENT_BF"][:])
                    kb.wait("scalar", "pe")
                    for j in range(4):
                        c = c4 * 4 + j
                        kb.s("copy", out=dst_wT[:, c * 128:(c + 1) * 128], in_=ps_tr[:, j * 128:(j + 1) * 128])

            def proj_fm(dst, wTt):
                for n0 in range(0, S, 512):
                    kb.wait("tensor", "se", "ve")
                    for c in range(NC):
                        kb.t("matmul", out=ps_v[:, 0:512], lhsT=wTt[:, c * 128:(c + 1) * 128], rhs=xn_fm[:, c * S + n0: c * S + n0 + 512], start=(c == 0), stop=(c == NC - 1))
                    kb.wait("scalar", "pe")
                    kb.s("copy", out=dst[:, n0:n0 + 512], in_=ps_v[:, 0:512])

            for g4 in range(4):
                for j in range(4):
                    h = g4 * 4 + j
                    kb.wait("gpsimd", "pe")
                    kb.load(w_nat[:], W["W_V"][h, :, :], queue="gpsimd")
                    kb.wait("tensor", "dg")
                    for c4 in range(4):
                        kb.wait("tensor", "se")
                        for jj in range(4):
                            c = c4 * 4 + jj
                            kb.t("transpose", out=ps_tr[:, jj * 128:(jj + 1) * 128], in_=w_nat[:, c * 128:(c + 1) * 128], identity=C["IDENT_BF"][:])
                        kb.wait("scalar", "pe")
                        for jj in range(4):
                            c = c4 * 4 + jj
                            kb.s("copy", out=w4T[:, c * 512 + j * 128: c * 512 + (j + 1) * 128], in_=ps_tr[:, jj * 128:(jj + 1) * 128])
                for tt in range(NT):
                    kb.wait("tensor", "se", "ve")
                    for c in range(NC):
                        kb.t("matmul", out=ps_v[:, 0:512], lhsT=xn_fm[:, c * S + tt * 128: c * S + (tt + 1) * 128], rhs=w4T[:, c * 512:(c + 1) * 512], start=(c == 0), stop=(c == NC - 1))
                    kb.wait("scalar", "pe")
                    for j in range(4):
                        kb.s("copy", out=v4[:, (j * NT + tt) * 128:(j * NT + tt + 1) * 128], in_=ps_v[:, j * 128:(j + 1) * 128])

                for j in range(4):
                    h = g4 * 4 + j
                    transpose_weight(W["W_Q"][h, :, :], wT)
                    if DEBUG and h == 0:
                        kb.dump("dbg_wqT", wT[:], [128, D], BF16)
                    proj_fm(q_h, wT)
                    if DEBUG and h == 0:
                        kb.dump("dbg_qh", q_h[:], [128, S], BF16)
                    transpose_weight(W["W_K"][h, :, :], wT)
                    proj_fm(k_h, wT)
                    if DEBUG and h == 0:
                        kb.dump("dbg_kh", k_h[:], [128, S], BF16)
                        kb.dump("dbg_v40", v4[:, 0:NT * 128], [128, NT * 128], BF16)
                        kb.dump("dbg_xn0", xn_fm[:, 0:S], [128, S], BF16)
                    for qt in range(NT):
                        kmax = (qt + 1) * 128
                        kb.wait("tensor", "se", "ve")
                        for n0 in range(0, kmax, 512):
                            nn = min(512, kmax - n0)
                            kb.t("matmul", out=ps_s[:, n0:n0 + nn], lhsT=q_h[:, qt * 128:(qt + 1) * 128], rhs=k_h[:, n0:n0 + nn], start=True, stop=True)
                        kb.wait("vector", "pe")
                        kb.v("tensor_tensor", out=ps_s[:, qt * 128:kmax], in0=ps_s[:, qt * 128:kmax], in1=C["MASKA"][:], op=ALU.add)
                        kb.v("tensor_reduce", out=nmax[:], in_=ps_s[:, 0:kmax], axis=AX.X, op=ALU.max, negate=True)
                        kb.v("tensor_scalar", out=bias[:], in0=nmax[:], scalar1=ISQ, scalar2=None, op0=ALU.mult, nodrain=True)
                        kb.wait("scalar", "ve")
                        kb.s("activation", out=p_sb[:, 0:kmax], in_=ps_s[:, 0:kmax], func=AF.Exp, scale=ISQ, bias=bias[:], accum_out=rs[:])
                        kb.wait("vector", "se")
                        kb.v("reciprocal", out=rinv[:], in_=rs[:])
                        kb.v("tensor_scalar", out=p_sb[:, 0:kmax], in0=p_sb[:, 0:kmax], scalar1=rinv[:], scalar2=None, op0=ALU.mult, nodrain=True)
                        kb.wait("tensor", "ve")
                        for k4 in range(0, qt + 1, 4):
                            kjs = list(range(k4, min(k4 + 4, qt + 1)))
                            kb.wait("tensor", "se")
                            for idx, kc in enumerate(kjs):
                                kb.t("transpose", out=ps_tr[:, idx * 128:(idx + 1) * 128], in_=p_sb[:, kc * 128:(kc + 1) * 128], identity=C["IDENT_BF"][:])
                            kb.wait("scalar", "pe")
                            for idx, kc in enumerate(kjs):
                                kb.s("copy", out=pt_sb[:, kc * 128:(kc + 1) * 128], in_=ps_tr[:, idx * 128:(idx + 1) * 128])
                        kb.wait("tensor", "se")
                        for kc in range(qt + 1):
                            kb.t("matmul", out=ps_z[:, 0:128], lhsT=v4[:, (j * NT + kc) * 128:(j * NT + kc + 1) * 128], rhs=pt_sb[:, kc * 128:(kc + 1) * 128], start=(kc == 0), stop=(kc == qt))
                        kb.wait("scalar", "pe")
                        kb.s("copy", out=z_set[:, (h * NT + qt) * 128:(h * NT + qt + 1) * 128], in_=ps_z[:, 0:128])
                        if DEBUG and h == 0 and qt == 3:
                            kb.dump("dbg_p3", p_sb[:, 0:512], [128, 512], BF16)
                            kb.dump("dbg_rs", rs[:], [128, 1], F32)

        kb.fence()
        with kb.scope() as osc:
            woT = kb.sb_in(osc, "aa_woT", [128, 16 * S], BF16)
            w_nat2 = kb.sb_in(osc, "aa_wnat2", [128, D], BF16)
            xres = kb.sb_in(osc, "aa_xres", [128, D], F32)
            outb = kb.sb_in(osc, "aa_outb", [128, D], F32)
            ps_o = kb.ps_in(osc, "aa_ps_o", [128, S], F32)
            ps_t2 = kb.ps_in(osc, "aa_ps_t2", [128, 1024], BF16)
            for dch in range(NC):
                kb.wait("gpsimd", "pe")
                kb.load(w_nat2[:], W["W_O"][dch * 128:(dch + 1) * 128, :], queue="gpsimd")
                kb.wait("tensor", "dg")
                for c4 in range(4):
                    kb.wait("tensor", "se")
                    for jj in range(4):
                        f = c4 * 4 + jj
                        kb.t("transpose", out=ps_t2[:, jj * 128:(jj + 1) * 128], in_=w_nat2[:, f * 128:(f + 1) * 128], identity=C["IDENT_BF"][:])
                    kb.wait("scalar", "pe")
                    for jj in range(4):
                        f = c4 * 4 + jj
                        kb.s("copy", out=woT[:, f * S + dch * 128: f * S + (dch + 1) * 128], in_=ps_t2[:, jj * 128:(jj + 1) * 128])
            for tt in range(NT):
                kb.wait("sync", "ve")
                kb.load(xres[:], x_in[tt * 128:(tt + 1) * 128, :])
                kb.wait("tensor", "ve", "se")
                for h in range(16):
                    for n0 in range(0, S, 512):
                        kb.t("matmul", out=ps_o[:, n0:n0 + 512], lhsT=z_set[:, (h * NT + tt) * 128:(h * NT + tt + 1) * 128], rhs=woT[:, h * S + n0: h * S + n0 + 512], start=(h == 0), stop=(h == 15))
                kb.wait("vector", "pe", "dl")
                kb.v("tensor_tensor", out=outb[:], in0=ps_o[:, 0:S], in1=xres[:], op=ALU.add)
                kb.store(x_out[tt * 128:(tt + 1) * 128, :], outb[:], wait_on=("ve",))
                kb.wait("vector", "st")
                kb.wait("vector", "st")



def emit_l1_attn_b(kb, C, x_in, x_out, W):
    nc = kb.nc
    with kb.scope() as sc:
        att_set = kb.sb_in(sc, "lb_att", [128, 16 * S], BF16, side="right")
        with kb.scope() as xnsc:
            xn_fm = kb.sb_in(xnsc, "lb_xn_fm", [128, NC * S], BF16)
            with kb.scope() as lnsc:
                xtile = kb.sb_in(lnsc, "lb_xtile", [128, D], F32)
                scr = ln_scratch(kb, lnsc, "lb")
                emit_ln(kb, C, x_in, xn_fm, xtile, scr)
            kb.fence()
            hsc = xnsc
            b_rows = kb.sb_in(hsc, "lb_brows", [1, 3 * D], BF16)
            kb.load(b_rows[0:1, 0:D], W["bk"].rearrange("(one d) -> one d", one=1), queue="gpsimd")
            kb.load(b_rows[0:1, D:2 * D], W["bv"].rearrange("(one d) -> one d", one=1), queue="gpsimd")
            kb.load(b_rows[0:1, 2 * D:3 * D], W["bo"].rearrange("(one d) -> one d", one=1), queue="gpsimd")
            bq_row = kb.sb_in(hsc, "lb_bqrow", [1, D], BF16)
            kb.load(bq_row[:], W["bq"].rearrange("(one d) -> one d", one=1), queue="gpsimd")
            decay_bc = kb.sb_in(hsc, "lb_decbc", [128, S], F32)
            kb.load(decay_bc[:], W["DECAY_BC"], queue="sync")
            w_nat = kb.sb_in(hsc, "lb_wnat", [128, D], BF16)
            wT = kb.sb_in(hsc, "lb_wT", [128, D], BF16)
            wkT = kb.sb_in(hsc, "lb_wkT", [128, D], BF16)
            wvT = kb.sb_in(hsc, "lb_wvT", [128, D], BF16)
            q_h = kb.sb_in(hsc, "lb_qh", [128, S], BF16)
            k_tm = kb.sb_in(hsc, "lb_ktm", [128, S], BF16)
            v_tm = kb.sb_in(hsc, "lb_vtm", [128, S], BF16)
            kv_sb = kb.sb_in(hsc, "lb_kv", [128, 128], BF16)
            ksum = kb.sb_in(hsc, "lb_ksum", [128, 1], BF16)
            bq_col = kb.sb_in(hsc, "lb_bqcol", [128, 1], F32)
            zcol = kb.sb_in(hsc, "lb_zcol", [128, 1], F32)
            att_bf = kb.sb_in(hsc, "lb_attbf", [128, 128], BF16)
            ps_a = kb.ps_in(hsc, "lb_ps_a", [128, 512], F32)
            ps_q = kb.ps_in(hsc, "lb_ps_q", [128, 512], F32)
            ps_kv = kb.ps_in(hsc, "lb_ps_kv", [128, 512], F32)
            ps_tr = kb.ps_in(hsc, "lb_ps_tr", [128, 1024], BF16)

            def transpose_w(src_rows, dst):
                kb.wait("gpsimd", "pe")
                kb.load(w_nat[:], src_rows, queue="gpsimd")
                kb.wait("tensor", "dg")
                for c4 in range(4):
                    kb.wait("tensor", "se")
                    for j in range(4):
                        c = c4 * 4 + j
                        kb.t("transpose", out=ps_tr[:, j * 128:(j + 1) * 128], in_=w_nat[:, c * 128:(c + 1) * 128], identity=C["IDENT_BF"][:])
                    kb.wait("scalar", "pe")
                    for j in range(4):
                        c = c4 * 4 + j
                        kb.s("copy", out=dst[:, c * 128:(c + 1) * 128], in_=ps_tr[:, j * 128:(j + 1) * 128])

            for h in range(16):
                hs = slice(h * 128, (h + 1) * 128)
                transpose_w(W["Wk"][hs, :], wkT)
                transpose_w(W["Wv"][hs, :], wvT)
                transpose_w(W["Wq"][hs, :], wT)
                kb.wait("tensor", "ve")
                kb.t("matmul", out=ps_kv[:, 0:1], lhsT=bq_row[0:1, hs], rhs=C["ONES_COL"][0:1, 0:1], start=True, stop=True)
                kb.wait("vector", "pe")
                kb.v("tensor_copy", out=bq_col[:], in_=ps_kv[:, 0:1])
                for tt in range(NT):
                    kb.wait("tensor", "se", "ve")
                    for c in range(NC):
                        kb.t("matmul", out=ps_a[:, 0:128], lhsT=xn_fm[:, c * S + tt * 128: c * S + (tt + 1) * 128], rhs=wkT[:, c * 128:(c + 1) * 128], start=(c == 0), stop=False)
                    kb.t("matmul", out=ps_a[:, 0:128], lhsT=C["ONES_ROW"][:], rhs=b_rows[0:1, hs], start=False, stop=True)
                    kb.wait("vector", "pe")
                    kb.v("tensor_scalar", out=k_tm[:, tt * 128:(tt + 1) * 128], in0=ps_a[:, 0:128], scalar1=C["DECAY_TM"][:, tt:tt + 1], scalar2=0.0, op0=ALU.mult, op1=ALU.max)
                    kb.wait("tensor", "ve")
                    for c in range(NC):
                        kb.t("matmul", out=ps_a[:, 0:128], lhsT=xn_fm[:, c * S + tt * 128: c * S + (tt + 1) * 128], rhs=wvT[:, c * 128:(c + 1) * 128], start=(c == 0), stop=False)
                    kb.t("matmul", out=ps_a[:, 0:128], lhsT=C["ONES_ROW"][:], rhs=b_rows[0:1, D + h * 128:D + (h + 1) * 128], start=False, stop=True)
                    kb.wait("scalar", "pe")
                    kb.s("copy", out=v_tm[:, tt * 128:(tt + 1) * 128], in_=ps_a[:, 0:128])
                for n0 in range(0, S, 512):
                    kb.wait("tensor", "se", "ve")
                    for c in range(NC):
                        kb.t("matmul", out=ps_q[:, 0:512], lhsT=wT[:, c * 128:(c + 1) * 128], rhs=xn_fm[:, c * S + n0: c * S + n0 + 512], start=(c == 0), stop=(c == NC - 1))
                    kb.wait("vector", "pe")
                    kb.v("tensor_scalar", out=ps_q[:, 0:512], in0=ps_q[:, 0:512], scalar1=bq_col[:], scalar2=0.0, op0=ALU.add, op1=ALU.max)
                    kb.v("tensor_tensor", out=q_h[:, n0:n0 + 512], in0=ps_q[:, 0:512], in1=decay_bc[:, n0:n0 + 512], op=ALU.mult)
                kb.wait("tensor", "se", "ve")
                for tt in range(NT):
                    kb.t("matmul", out=ps_kv[:, 0:128], lhsT=k_tm[:, tt * 128:(tt + 1) * 128], rhs=v_tm[:, tt * 128:(tt + 1) * 128], start=(tt == 0), stop=(tt == NT - 1))
                for tt in range(NT):
                    kb.t("matmul", out=ps_kv[:, 128:129], lhsT=k_tm[:, tt * 128:(tt + 1) * 128], rhs=C["ONES_COL"][:], start=(tt == 0), stop=(tt == NT - 1))
                kb.wait("scalar", "pe")
                kb.s("copy", out=kv_sb[:], in_=ps_kv[:, 0:128])
                kb.s("copy", out=ksum[:], in_=ps_kv[:, 128:129])
                if DEBUG and h == 0:
                    kb.dump("dbg_ktm", k_tm[:], [128, S], BF16)
                    kb.dump("dbg_vtm", v_tm[:], [128, S], BF16)
                    kb.dump("dbg_qh", q_h[:], [128, S], BF16)
                    kb.dump("dbg_kv", kv_sb[:], [128, 128], BF16)
                    kb.dump("dbg_ksum", ksum[:], [128, 1], BF16)
                    kb.dump("dbg_xn0", xn_fm[:, 0:S], [128, S], BF16)
                for tt in range(NT):
                    kb.wait("tensor", "se", "ve")
                    kb.t("matmul", out=ps_a[:, 0:128], lhsT=q_h[:, tt * 128:(tt + 1) * 128], rhs=kv_sb[:], start=True, stop=True)
                    kb.t("matmul", out=ps_a[:, 128:129], lhsT=q_h[:, tt * 128:(tt + 1) * 128], rhs=ksum[:], start=True, stop=True)
                    kb.wait("vector", "pe")
                    kb.v("tensor_scalar", out=zcol[:], in0=ps_a[:, 128:129], scalar1=1e-6, scalar2=None, op0=ALU.add)
                    kb.v("reciprocal", out=zcol[:], in_=zcol[:])
                    kb.v("tensor_scalar", out=att_bf[:], in0=ps_a[:, 0:128], scalar1=zcol[:], scalar2=None, op0=ALU.mult)
                    kb.wait("tensor", "ve", "se")
                    kb.t("transpose", out=ps_tr[:, 0:128], in_=att_bf[:], identity=C["IDENT_BF"][:])
                    kb.wait("scalar", "pe")
                    kb.s("copy", out=att_set[:, (h * NT + tt) * 128:(h * NT + tt + 1) * 128], in_=ps_tr[:, 0:128])

        kb.fence()
        with kb.scope() as osc:
            woT = kb.sb_in(osc, "lb_woT", [128, 16 * S], BF16)
            w_nat2 = kb.sb_in(osc, "lb_wnat2", [128, D], BF16)
            bo_row = kb.sb_in(osc, "lb_borow", [1, D], BF16)
            kb.load(bo_row[:], W["bo"].rearrange("(one d) -> one d", one=1), queue="gpsimd")
            xres = kb.sb_in(osc, "lb_xres", [128, D], F32)
            outb = kb.sb_in(osc, "lb_outb", [128, D], F32)
            ps_o = kb.ps_in(osc, "lb_ps_o", [128, S], F32)
            ps_t2 = kb.ps_in(osc, "lb_ps_t2", [128, 1024], BF16)
            for dch in range(NC):
                kb.wait("gpsimd", "pe")
                kb.load(w_nat2[:], W["Wo"][dch * 128:(dch + 1) * 128, :], queue="gpsimd")
                kb.wait("tensor", "dg")
                for c4 in range(4):
                    kb.wait("tensor", "se")
                    for jj in range(4):
                        f = c4 * 4 + jj
                        kb.t("transpose", out=ps_t2[:, jj * 128:(jj + 1) * 128], in_=w_nat2[:, f * 128:(f + 1) * 128], identity=C["IDENT_BF"][:])
                    kb.wait("scalar", "pe")
                    for jj in range(4):
                        f = c4 * 4 + jj
                        kb.s("copy", out=woT[:, f * S + dch * 128: f * S + (dch + 1) * 128], in_=ps_t2[:, jj * 128:(jj + 1) * 128])
            for tt in range(NT):
                kb.wait("sync", "ve")
                kb.load(xres[:], x_in[tt * 128:(tt + 1) * 128, :])
                kb.wait("tensor", "ve", "se", "dg")
                for h in range(16):
                    for n0 in range(0, S, 512):
                        kb.t("matmul", out=ps_o[:, n0:n0 + 512], lhsT=att_set[:, (h * NT + tt) * 128:(h * NT + tt + 1) * 128], rhs=woT[:, h * S + n0: h * S + n0 + 512], start=(h == 0), stop=False)
                for n0 in range(0, S, 512):
                    kb.t("matmul", out=ps_o[:, n0:n0 + 512], lhsT=C["ONES_ROW"][:], rhs=bo_row[0:1, n0:n0 + 512], start=False, stop=True)
                kb.wait("vector", "pe", "dl")
                kb.v("tensor_tensor", out=outb[:], in0=ps_o[:, 0:S], in1=xres[:], op=ALU.add)
                kb.store(x_out[tt * 128:(tt + 1) * 128, :], outb[:], wait_on=("ve",))
                kb.wait("vector", "st")



LAMBDA_INIT = 0.8 - 0.6 * math.exp(-0.3 * 12)


def emit_l0_attn_b(kb, C, x_in, x_out, W):
    nc = kb.nc
    with kb.scope() as sc:
        att_set = kb.sb_in(sc, "ab_att", [128, 16 * S], BF16, side="right")
        with kb.scope() as xnsc:
            xn_fm = kb.sb_in(xnsc, "ab_xn_fm", [128, NC * S], BF16)
            with kb.scope() as lnsc:
                xtile = kb.sb_in(lnsc, "ab_xtile", [128, D], F32)
                scr = ln_scratch(kb, lnsc, "ab")
                emit_ln(kb, C, x_in, xn_fm, xtile, scr)
            kb.fence()
            hsc = xnsc
            lrow = kb.sb_in(hsc, "ab_lrow", [1, 4 * 64], F32)
            for i, nm in enumerate(["lambda_q1", "lambda_k1", "lambda_q2", "lambda_k2"]):
                kb.load(lrow[0:1, i * 64:(i + 1) * 64], W[nm].rearrange("(one d) -> one d", one=1), queue="sync")
            lscr = kb.sb_in(hsc, "ab_lscr", [1, 64], F32)
            lvals = kb.sb_in(hsc, "ab_lvals", [1, 4], F32)
            neglam_col = kb.sb_in(hsc, "ab_neglam", [128, 1], F32)
            kb.wait("vector", "dl")
            kb.v("tensor_tensor", out=lscr[:], in0=lrow[0:1, 0:64], in1=lrow[0:1, 64:128], op=ALU.mult)
            kb.v("tensor_reduce", out=lvals[0:1, 0:1], in_=lscr[:], axis=AX.X, op=ALU.add)
            kb.v("tensor_tensor", out=lscr[:], in0=lrow[0:1, 128:192], in1=lrow[0:1, 192:256], op=ALU.mult)
            kb.v("tensor_reduce", out=lvals[0:1, 1:2], in_=lscr[:], axis=AX.X, op=ALU.add)
            kb.wait("scalar", "ve")
            kb.s("activation", out=lvals[0:1, 2:3], in_=lvals[0:1, 0:1], func=AF.Exp)
            kb.s("activation", out=lvals[0:1, 3:4], in_=lvals[0:1, 1:2], func=AF.Exp)
            kb.wait("vector", "se")
            kb.v("tensor_tensor", out=lvals[0:1, 0:1], in0=lvals[0:1, 3:4], in1=lvals[0:1, 2:3], op=ALU.subtract)
            kb.v("tensor_scalar", out=lvals[0:1, 0:1], in0=lvals[0:1, 0:1], scalar1=LAMBDA_INIT, scalar2=None, op0=ALU.subtract)
            kb.wait("tensor", "ve")
            ps_sm = kb.ps_in(hsc, "ab_ps_sm", [128, 512], F32)
            kb.t("matmul", out=ps_sm[:, 0:1], lhsT=C["ONES_ROW_F32"][:], rhs=lvals[0:1, 0:1], start=True, stop=True)
            kb.wait("vector", "pe")
            kb.v("tensor_copy", out=neglam_col[:], in_=ps_sm[:, 0:1])
            lnrow = kb.sb_in(hsc, "ab_lnrow", [1, 256], F32)
            kb.load(lnrow[0:1, 0:128], W["ln_w"].rearrange("(one d) -> one d", one=1), queue="sync")
            kb.load(lnrow[0:1, 128:256], W["ln_b"].rearrange("(one d) -> one d", one=1), queue="sync")
            wbc = kb.sb_in(hsc, "ab_wbc", [128, 128], F32)
            bbc = kb.sb_in(hsc, "ab_bbc", [128, 128], F32)
            kb.wait("vector", "dl")
            kb.v("tensor_scalar", out=lnrow[:], in0=lnrow[:], scalar1=float(1.0 - LAMBDA_INIT), scalar2=None, op0=ALU.mult)
            kb.wait("tensor", "ve")
            kb.t("matmul", out=ps_sm[:, 0:128], lhsT=C["ONES_ROW_F32"][:], rhs=lnrow[0:1, 0:128], start=True, stop=True)
            kb.t("matmul", out=ps_sm[:, 128:256], lhsT=C["ONES_ROW_F32"][:], rhs=lnrow[0:1, 128:256], start=True, stop=True)
            kb.wait("vector", "pe")
            kb.v("tensor_copy", out=wbc[:], in_=ps_sm[:, 0:128])
            kb.v("tensor_copy", out=bbc[:], in_=ps_sm[:, 128:256])

            wcol = kb.sb_in(hsc, "ab_wcol", [128, D], BF16)
            q_h = kb.sb_in(hsc, "ab_qh", [128, S], BF16)
            k_h = kb.sb_in(hsc, "ab_kh", [128, S], BF16)
            v_fm = kb.sb_in(hsc, "ab_vfm", [128, S], BF16)
            v_tm = kb.sb_in(hsc, "ab_vtm", [128, S], BF16)
            raw = kb.sb_in(hsc, "ab_raw", [128, S], BF16)
            rope_t1 = kb.sb_in(hsc, "ab_rt1", [128, S], F32)
            p_sb = kb.sb_in(hsc, "ab_p", [128, S], BF16)
            pt_sb = kb.sb_in(hsc, "ab_pt", [128, S], BF16)
            z1_sb = kb.sb_in(hsc, "ab_z1", [128, 128], F32)
            cc = kb.sb_in(hsc, "ab_cc", [128, 128], F32)
            csq = kb.sb_in(hsc, "ab_csq", [128, 128], F32)
            nmax = kb.sb_in(hsc, "ab_nmax", [128, 1], F32)
            bias = kb.sb_in(hsc, "ab_bias", [128, 1], F32)
            rs = kb.sb_in(hsc, "ab_rs", [128, 1], F32)
            rinv = kb.sb_in(hsc, "ab_rinv", [128, 1], F32)
            ssq = kb.sb_in(hsc, "ab_ssq", [128, 1], F32)
            rstd = kb.sb_in(hsc, "ab_rstd", [128, 1], F32)
            ps_s = kb.ps_in(hsc, "ab_ps_s", [128, S], F32)
            ps_tr = kb.ps_in(hsc, "ab_ps_tr", [128, 1024], BF16)
            ps_z = kb.ps_in(hsc, "ab_ps_z", [128, 512], F32)

            def col_load(dst, W_ap, cout):
                src = W_ap[:, cout * 128:(cout + 1) * 128].rearrange("(c p) j -> p c j", p=128)
                kb.wait("gpsimd", "pe")
                kb.load(dst[:].rearrange("p (c j) -> p c j", j=128), src, queue="gpsimd")

            def proj_chunk(dst_raw, h):
                for n0 in range(0, S, 512):
                    kb.wait("tensor", "se", "ve", "dg")
                    for c in range(NC):
                        kb.t("matmul", out=ps_s[:, n0:n0 + 512], lhsT=wcol[:, c * 128:(c + 1) * 128], rhs=xn_fm[:, c * S + n0: c * S + n0 + 512], start=(c == 0), stop=(c == NC - 1))
                kb.wait("scalar", "pe")
                for n0 in range(0, S, 512):
                    kb.s("copy", out=dst_raw[:, n0:n0 + 512], in_=ps_s[:, n0:n0 + 512])

            def rope(dst, src):
                kb.wait("tensor", "se", "ve")
                for n0 in range(0, S, 512):
                    kb.t("matmul", out=ps_s[:, n0:n0 + 512], lhsT=C["PROT_T"][:], rhs=src[:, n0:n0 + 512], start=True, stop=True)
                kb.wait("vector", "pe")
                kb.v("tensor_tensor", out=rope_t1[:], in0=ps_s[:, 0:S], in1=C["SIN"][:], op=ALU.mult)
                kb.v("tensor_tensor", out=dst[:], in0=src[:], in1=C["COS"][:], op=ALU.mult)
                kb.v("tensor_tensor", out=dst[:], in0=dst[:], in1=rope_t1[:], op=ALU.add)

            for h in range(16):
                col_load(wcol, W["Wq"], h)
                proj_chunk(raw, h)
                rope(q_h, raw)
                col_load(wcol, W["Wk"], h)
                proj_chunk(raw, h)
                rope(k_h, raw)
                col_load(wcol, W["Wv"], h)
                proj_chunk(v_fm, h)
                for t4 in range(4):
                    kb.wait("tensor", "se", "ve")
                    for j in range(4):
                        tt = t4 * 4 + j
                        kb.t("transpose", out=ps_tr[:, j * 128:(j + 1) * 128], in_=v_fm[:, tt * 128:(tt + 1) * 128], identity=C["IDENT_BF"][:])
                    kb.wait("scalar", "pe")
                    for j in range(4):
                        tt = t4 * 4 + j
                        kb.s("copy", out=v_tm[:, tt * 128:(tt + 1) * 128], in_=ps_tr[:, j * 128:(j + 1) * 128])
                for qt in range(NT):
                    kmax = (qt + 1) * 128
                    for br in range(2):
                        pb = slice(64 * br, 64 * (br + 1))
                        kb.wait("tensor", "se", "ve")
                        for n0 in range(0, kmax, 512):
                            nn = min(512, kmax - n0)
                            kb.t("matmul", out=ps_s[:, n0:n0 + nn], lhsT=q_h[pb, qt * 128:(qt + 1) * 128], rhs=k_h[pb, n0:n0 + nn], start=True, stop=True)
                        kb.wait("vector", "pe")
                        kb.v("tensor_tensor", out=ps_s[:, qt * 128:kmax], in0=ps_s[:, qt * 128:kmax], in1=C["MASKB"][:], op=ALU.add)
                        kb.v("tensor_reduce", out=nmax[:], in_=ps_s[:, 0:kmax], axis=AX.X, op=ALU.max, negate=True)
                        kb.v("tensor_scalar", out=bias[:], in0=nmax[:], scalar1=0.125, scalar2=None, op0=ALU.mult, nodrain=True)
                        kb.wait("scalar", "ve")
                        kb.s("activation", out=p_sb[:, 0:kmax], in_=ps_s[:, 0:kmax], func=AF.Exp, scale=0.125, bias=bias[:], accum_out=rs[:])
                        kb.wait("vector", "se")
                        kb.v("reciprocal", out=rinv[:], in_=rs[:])
                        kb.v("tensor_scalar", out=p_sb[:, 0:kmax], in0=p_sb[:, 0:kmax], scalar1=rinv[:], scalar2=None, op0=ALU.mult, nodrain=True)
                        kb.wait("tensor", "ve")
                        for k4 in range(0, qt + 1, 4):
                            kjs = list(range(k4, min(k4 + 4, qt + 1)))
                            kb.wait("tensor", "se")
                            for idx, kc in enumerate(kjs):
                                kb.t("transpose", out=ps_tr[:, idx * 128:(idx + 1) * 128], in_=p_sb[:, kc * 128:(kc + 1) * 128], identity=C["IDENT_BF"][:])
                            kb.wait("scalar", "pe")
                            for idx, kc in enumerate(kjs):
                                kb.s("copy", out=pt_sb[:, kc * 128:(kc + 1) * 128], in_=ps_tr[:, idx * 128:(idx + 1) * 128])
                        kb.wait("tensor", "se", "ve")
                        for kc in range(qt + 1):
                            kb.t("matmul", out=ps_z[:, br * 128:(br + 1) * 128], lhsT=pt_sb[:, kc * 128:(kc + 1) * 128], rhs=v_tm[:, kc * 128:(kc + 1) * 128], start=(kc == 0), stop=(kc == qt))
                    kb.wait("vector", "pe")
                    kb.v("tensor_copy", out=z1_sb[:], in_=ps_z[:, 0:128])
                    kb.v("scalar_tensor_tensor", out=cc[:], in0=ps_z[:, 128:256], scalar=neglam_col[:], in1=z1_sb[:], op0=ALU.mult, op1=ALU.add)
                    kb.wait("scalar", "ve")
                    kb.s("activation", out=csq[:], in_=cc[:], func=AF.Square)
                    kb.wait("vector", "se")
                    kb.v("tensor_reduce", out=ssq[:], in_=csq[:], axis=AX.X, op=ALU.add)
                    kb.v("tensor_scalar", out=ssq[:], in0=ssq[:], scalar1=1.0 / 128.0, scalar2=1e-8, op0=ALU.mult, op1=ALU.add)
                    kb.wait("scalar", "ve")
                    kb.s("activation", out=rstd[:], in_=ssq[:], func=AF.Sqrt)
                    kb.wait("vector", "se")
                    kb.v("reciprocal", out=rstd[:], in_=rstd[:])
                    kb.v("scalar_tensor_tensor", out=cc[:], in0=cc[:], scalar=rstd[:], in1=wbc[:], op0=ALU.mult, op1=ALU.mult)
                    kb.v("tensor_tensor", out=cc[:], in0=cc[:], in1=bbc[:], op=ALU.add)
                    kb.v("tensor_copy", out=p_sb[:, 0:128], in_=cc[:])
                    kb.wait("tensor", "ve", "se")
                    kb.t("transpose", out=ps_tr[:, 0:128], in_=p_sb[:, 0:128], identity=C["IDENT_BF"][:])
                    kb.wait("scalar", "pe")
                    kb.s("copy", out=att_set[:, (h * NT + qt) * 128:(h * NT + qt + 1) * 128], in_=ps_tr[:, 0:128])

        kb.fence()
        with kb.scope() as osc:
            wo_nat = kb.sb_in(osc, "ab_wonat", [128, 16 * S], BF16)
            xres = kb.sb_in(osc, "ab_xres", [128, D], F32)
            outb = kb.sb_in(osc, "ab_outb", [128, D], F32)
            ps_o = kb.ps_in(osc, "ab_ps_o", [128, S], F32)
            for c in range(NC):
                kb.load(wo_nat[:, c * S:(c + 1) * S], W["Wo"][c * 128:(c + 1) * 128, :], queue="gpsimd")
            for tt in range(NT):
                kb.wait("sync", "ve")
                kb.load(xres[:], x_in[tt * 128:(tt + 1) * 128, :])
                kb.wait("tensor", "ve", "se", "dg")
                for c in range(16):
                    for n0 in range(0, S, 512):
                        kb.t("matmul", out=ps_o[:, n0:n0 + 512], lhsT=att_set[:, (c * NT + tt) * 128:(c * NT + tt + 1) * 128], rhs=wo_nat[:, c * S + n0: c * S + n0 + 512], start=(c == 0), stop=(c == 15))
                kb.wait("vector", "pe", "dl")
                kb.v("tensor_tensor", out=outb[:], in0=ps_o[:, 0:S], in1=xres[:], op=ALU.add)
                kb.store(x_out[tt * 128:(tt + 1) * 128, :], outb[:], wait_on=("ve",))
                kb.wait("vector", "st")



H1F = 8192
RMS_EPS_F32 = float(np.finfo(np.float32).eps)


def emit_ternary_quant(kb, C, w_src, n_rows, n_cols, w_dst):
    with kb.scope() as qsc:
        wb = kb.sb_in(qsc, "tq_wb", [128, 2048], BF16)
        gs = kb.sb_in(qsc, "tq_gs", [128, 32], F32)
        sc_ = kb.sb_in(qsc, "tq_sc", [128, 32], BF16)
        rc = kb.sb_in(qsc, "tq_rc", [128, 32], F32)
        t = kb.sb_in(qsc, "tq_t", [128, 2048], BF16)
        a = kb.sb_in(qsc, "tq_a", [128, 2048], BF16)
        b = kb.sb_in(qsc, "tq_b", [128, 2048], BF16)
        wq = kb.sb_in(qsc, "tq_wq", [128, 2048], BF16)
        for r0 in range(0, n_rows, 128):
            for c0 in range(0, n_cols, 2048):
                kb.wait("gpsimd", "ve")
                kb.load(wb[:], w_src[r0:r0 + 128, c0:c0 + 2048], queue="gpsimd")
                kb.wait("vector", "dg", "st")
                w3 = wb[:].rearrange("p (g r) -> p g r", r=64)
                kb.v("tensor_reduce", out=gs[:], in_=w3, axis=AX.X, op=ALU.add, apply_absolute_value=True)
                kb.v("tensor_scalar", out=sc_[:], in0=gs[:], scalar1=1.0 / 64, scalar2=1e-8, op0=ALU.mult, op1=ALU.max)
                kb.v("reciprocal", out=rc[:], in_=sc_[:])
                s3 = sc_[:].rearrange("p (g r) -> p g r", r=1)
                r3 = rc[:].rearrange("p (g r) -> p g r", r=1)
                w3b, s3b = bass.broadcast_tensor_aps(w3, s3)
                _, r3b = bass.broadcast_tensor_aps(w3, r3)
                kb.v("tensor_tensor", out=t[:].rearrange("p (g r) -> p g r", r=64), in0=w3b, in1=r3b, op=ALU.mult)
                kb.v("tensor_scalar", out=a[:], in0=t[:], scalar1=0.5, scalar2=None, op0=ALU.is_gt)
                kb.v("tensor_scalar", out=b[:], in0=t[:], scalar1=-0.5, scalar2=None, op0=ALU.is_lt)
                kb.v("tensor_tensor", out=wq[:], in0=a[:], in1=b[:], op=ALU.subtract)
                kb.v("tensor_tensor", out=wq[:].rearrange("p (g r) -> p g r", r=64), in0=wq[:].rearrange("p (g r) -> p g r", r=64), in1=s3b, op=ALU.mult)
                kb.store(w_dst[r0:r0 + 128, c0:c0 + 2048], wq[:], wait_on=("ve",))
    kb.fence()


def emit_l1_ffn(kb, C, x_in, x_out, W, quantize=True, scratch=None):
    nc = kb.nc
    if scratch is None:
        scratch = (kb.dtmp("l1f_gu", [2 * H1F, D], BF16), kb.dtmp("l1f_p", [D, H1F], BF16),
                   kb.dtmp("l1f_guT", [D, 2 * H1F], BF16), kb.dtmp("l1f_pT", [H1F, D], BF16))
    wq_gu, wq_p, wq_guT, wq_pT = scratch
    if quantize:
        emit_ternary_quant(kb, C, W["w_gate_up"], 2 * H1F, D, wq_gu)
        emit_ternary_quant(kb, C, W["w_proj"], D, H1F, wq_p)
        with kb.scope() as tsc:
            tstage = kb.sb_in(tsc, "l1f_tstage", [128, 2048], BF16)
            for c in range(NC):
                for p0 in range(0, 2 * H1F, 2048):
                    kb.wait("sync", "st")
                    kb.loadT(tstage[:], wq_gu[p0:p0 + 2048, c * 128:(c + 1) * 128])
                    kb.store(wq_guT[c * 128:(c + 1) * 128, p0:p0 + 2048], tstage[:], wait_on=("dl",))
            for hc in range(H1F // 128):
                kb.wait("sync", "st")
                kb.loadT(tstage[:], wq_p[0:D, hc * 128:(hc + 1) * 128])
                kb.store(wq_pT[hc * 128:(hc + 1) * 128, :], tstage[:], wait_on=("dl",))
        kb.fence()
    with kb.scope() as sc:
        xn_fm = kb.sb_in(sc, "l1f_xn", [128, NC * S], BF16, side="right")
        with kb.scope() as lnsc:
            xtile = kb.sb_in(lnsc, "l1f_xtile", [128, D], F32)
            scr = ln_scratch(kb, lnsc, "l1f")
            emit_ln(kb, C, x_in, xn_fm, xtile, scr)
        kb.fence()
        hstore = kb.sb_in(sc, "l1f_h", [128, 64 * 512], BF16)
        wt = kb.sb_in(sc, "l1f_wt", [128, NC * 512], BF16)
        sqt = kb.sb_in(sc, "l1f_sqt", [128, 512], BF16)
        rstd = kb.sb_in(sc, "l1f_rstd", [1, 512], F32)
        wp_g = kb.sb_in(sc, "l1f_wpg", [128, 8 * D], BF16)
        out_sb = kb.sb_in(sc, "l1f_out", [128, 2 * D], F32)
        xres = kb.sb_in(sc, "l1f_xres", [128, D], F32)
        ps_big = kb.ps_in(sc, "l1f_ps_big", [128, 2048], F32)
        ps_ss = kb.ps_in(sc, "l1f_ps_ss", [1, 512], F32)
        ps_bc = kb.ps_in(sc, "l1f_ps_bc", [128, 512], F32)
        for qq in range(4):
            q0 = qq * 512
            for mg in range(32):
                kb.wait("sync", "pe")
                kb.load(wt[:].rearrange("p (c j) -> p c j", j=512), wq_guT[:, mg * 512:(mg + 1) * 512].rearrange("(c p) j -> p c j", p=128))
                kb.wait("tensor", "dl", "se", "ve")
                for c in range(NC):
                    for j in range(4):
                        kb.t("matmul", out=ps_big[:, j * 512:(j + 1) * 512], lhsT=wt[:, c * 512 + j * 128: c * 512 + (j + 1) * 128], rhs=xn_fm[:, c * S + q0:c * S + q0 + 512], start=(c == 0), stop=(c == NC - 1))
                for j in range(4):
                    mc = mg * 4 + j
                    if mc < 64:
                        kb.wait("scalar", "pe")
                        kb.s("activation", out=hstore[:, mc * 512:(mc + 1) * 512], in_=ps_big[:, j * 512:(j + 1) * 512], func=AF.Silu, nodrain=True)
                    else:
                        hc = mc - 64
                        kb.wait("vector", "pe")
                        kb.v("tensor_tensor", out=hstore[:, hc * 512:(hc + 1) * 512], in0=hstore[:, hc * 512:(hc + 1) * 512], in1=ps_big[:, j * 512:(j + 1) * 512], op=ALU.mult)
                        kb.wait("scalar", "ve")
                        kb.s("activation", out=sqt[:], in_=hstore[:, hc * 512:(hc + 1) * 512], func=AF.Square, nodrain=True)
                        kb.wait("tensor", "se")
                        kb.t("matmul", out=ps_ss[:, 0:512], lhsT=C["ONES_COL"][:], rhs=sqt[:], start=(hc == 0), stop=(hc == 63))
            kb.wait("vector", "pe")
            kb.v("tensor_scalar", out=rstd[:], in0=ps_ss[:, 0:512], scalar1=1.0 / H1F, scalar2=RMS_EPS_F32, op0=ALU.mult, op1=ALU.add)
            kb.wait("scalar", "ve")
            kb.s("activation", out=rstd[:], in_=rstd[:], func=AF.Sqrt)
            kb.wait("vector", "se")
            kb.v("reciprocal", out=rstd[:], in_=rstd[:])
            kb.wait("tensor", "ve", "se")
            kb.t("matmul", out=ps_bc[:, 0:512], lhsT=C["ONES_ROW_F32"][:], rhs=rstd[:], start=True, stop=True)
            kb.wait("vector", "pe")
            for hc in range(64):
                kb.v("tensor_tensor", out=hstore[:, hc * 512:(hc + 1) * 512], in0=hstore[:, hc * 512:(hc + 1) * 512], in1=ps_bc[:, 0:512], op=ALU.mult)
            for tg in range(2):
                for cg in range(8):
                    kb.wait("sync", "pe")
                    kb.load(wp_g[:].rearrange("p (c j) -> p c j", j=D), wq_pT[cg * 1024:(cg + 1) * 1024, :].rearrange("(c p) j -> p c j", p=128))
                    kb.wait("tensor", "dl")
                    for ttj in range(2):
                        tt = tg * 2 + ttj
                        kb.wait("tensor", "ve")
                        for cj in range(8):
                            c = cg * 8 + cj
                            for n0 in range(0, D, 512):
                                kb.t("matmul", out=ps_big[:, n0:n0 + 512], lhsT=hstore[:, c * 512 + tt * 128: c * 512 + (tt + 1) * 128], rhs=wp_g[:, cj * D + n0: cj * D + n0 + 512], start=(cj == 0), stop=(cj == 7))
                        kb.wait("vector", "pe")
                        if cg == 0:
                            kb.v("tensor_copy", out=out_sb[:, ttj * D:(ttj + 1) * D], in_=ps_big[:, 0:D])
                        else:
                            kb.v("tensor_tensor", out=out_sb[:, ttj * D:(ttj + 1) * D], in0=out_sb[:, ttj * D:(ttj + 1) * D], in1=ps_big[:, 0:D], op=ALU.add)
                for ttj in range(2):
                    gt = qq * 4 + tg * 2 + ttj
                    kb.wait("sync", "ve")
                    kb.load(xres[:], x_in[gt * 128:(gt + 1) * 128, :])
                    kb.wait("vector", "dl")
                    kb.v("tensor_tensor", out=out_sb[:, ttj * D:(ttj + 1) * D], in0=out_sb[:, ttj * D:(ttj + 1) * D], in1=xres[:], op=ALU.add)
                    kb.store(x_out[gt * 128:(gt + 1) * 128, :], out_sb[:, ttj * D:(ttj + 1) * D], wait_on=("ve",))
                kb.wait("vector", "st")
    kb.fence()



F0 = 5632
NFC = F0 // 128
L0FFN_PHASE = 99
L0FFN_NFC = NFC
L0FFN_END = 3


def emit_l0_ffn_prepass(kb, C, W, scr):
    w1bf, w3bf, w2bf, w1T, w3T, w2T = scr
    with kb.scope() as ps:
        stage = kb.sb_in(ps, "mf_stage", [128, 2048], BF16)
        for (src, dst, nr, ncol) in ((W["w1"], w1bf, F0, D), (W["w3"], w3bf, F0, D)):
            for r0 in range(0, nr, 128):
                kb.wait("gpsimd", "st")
                kb.load(stage[:], src[r0:r0 + 128, :], queue="gpsimd")
                kb.store(dst[r0:r0 + 128, :], stage[:], wait_on=("dg",))
        for r0 in range(0, D, 128):
            for c0 in range(0, F0, 1408):
                kb.wait("gpsimd", "st")
                kb.load(stage[:, 0:1408], W["w2"][r0:r0 + 128, c0:c0 + 1408], queue="gpsimd")
                kb.store(w2bf[r0:r0 + 128, c0:c0 + 1408], stage[:, 0:1408], wait_on=("dg",))
        for (srcbf, dstT, nr, ncol) in ((w1bf, w1T, F0, D), (w3bf, w3T, F0, D)):
            for c in range(NC):
                for p0 in range(0, nr, 1408):
                    kb.wait("sync", "st")
                    kb.loadT(stage[:, 0:1408], srcbf[p0:p0 + 1408, c * 128:(c + 1) * 128])
                    kb.store(dstT[c * 128:(c + 1) * 128, p0:p0 + 1408], stage[:, 0:1408], wait_on=("dl",))
        for f in range(NFC):
            kb.wait("sync", "st")
            kb.loadT(stage[:], w2bf[0:D, f * 128:(f + 1) * 128])
            kb.store(w2T[f * 128:(f + 1) * 128, :], stage[:], wait_on=("dl",))
    kb.fence()


def emit_l0_ffn(kb, C, x_in, x_out, W, scr, out_acc):
    nc = kb.nc
    if L0FFN_PHASE < 1:
        return
    w1bf, w3bf, w2bf, w1T, w3T, w2T = scr
    with kb.scope() as sc:
        xn_fm = kb.sb_in(sc, "mf_xn", [128, NC * S], BF16, side="right")
        with kb.scope() as lnsc:
            xtile = kb.sb_in(lnsc, "mf_xtile", [128, D], F32)
            scrn = ln_scratch(kb, lnsc, "mf")
            emit_ln(kb, C, x_in, xn_fm, xtile, scrn)
        kb.fence()
        gwT = kb.sb_in(sc, "mf_gwT", [128, 128], BF16)
        w_tm = kb.sb_in(sc, "mf_wtm", [128, 128], F32)
        u1G = [kb.sb_in(sc, f"mf_u1g{g}", [80, S], BF16) for g in range(3)]
        u3G = [kb.sb_in(sc, f"mf_u3g{g}", [80, S], BF16) for g in range(3)]
        b2TG = [kb.sb_in(sc, f"mf_b2tg{g}", [80, S], BF16) for g in range(3)]
        a2_all = kb.sb_in(sc, "mf_a2", [128, F0], BF16)

        def eg(e):
            return e // 3, 32 * (e % 3)

        with kb.scope() as p1:
            ps_r = kb.ps_in(p1, "mf_ps_r", [128, 512], F32)
            ps_t = kb.ps_in(p1, "mf_ps_t", [128, 512], BF16)
            ps_u = kb.ps_in(p1, "mf_ps_u", [16, S], F32)
            gw_nat = kb.sb_in(p1, "mf_gwnat", [8, D], BF16)
            a1nat = kb.sb_in(p1, "mf_a1nat", [128, D], BF16)
            a1T = kb.sb_in(p1, "mf_a1T", [128, D], BF16)
            m1n = kb.sb_in(p1, "mf_m1n", [128, 1], F32)
            m2n = kb.sb_in(p1, "mf_m2n", [128, 1], F32)
            sg1 = kb.sb_in(p1, "mf_sg1", [128, 1], F32)
            sg2 = kb.sb_in(p1, "mf_sg2", [128, 1], F32)
            tl = kb.sb_in(p1, "mf_tl", [128, 8], F32)
            t2 = kb.sb_in(p1, "mf_t2", [128, 8], F32)
            mk1 = kb.sb_in(p1, "mf_mk1", [128, 8], F32)
            mk2 = kb.sb_in(p1, "mf_mk2", [128, 8], F32)
            b2nat = kb.sb_in(p1, "mf_b2nat", [128, 256], BF16)
            kb.load(gw_nat[:], W["gate_w"], queue="gpsimd")
            kb.wait("tensor", "dg", "ve", "se")
            for c in range(NC):
                kb.t("transpose", out=ps_t[:, 0:8], in_=gw_nat[:, c * 128:(c + 1) * 128], identity=C["I16"][0:8, 0:8])
                kb.wait("scalar", "pe")
                kb.s("copy", out=gwT[:, c * 8:(c + 1) * 8], in_=ps_t[:, 0:8])
                kb.wait("tensor", "se")
            for tt in range(NT):
                kb.wait("tensor", "se", "ve")
                for c in range(NC):
                    kb.t("matmul", out=ps_r[:, 0:8], lhsT=xn_fm[:, c * S + tt * 128: c * S + (tt + 1) * 128], rhs=gwT[:, c * 8:(c + 1) * 8], start=(c == 0), stop=(c == NC - 1))
                kb.wait("vector", "pe")
                kb.v("tensor_reduce", out=m1n[:], in_=ps_r[:, 0:8], axis=AX.X, op=ALU.max, negate=True)
                kb.v("tensor_scalar", out=tl[:], in0=ps_r[:, 0:8], scalar1=m1n[:], scalar2=None, op0=ALU.add)
                kb.v("tensor_scalar", out=mk1[:], in0=tl[:], scalar1=0.0, scalar2=None, op0=ALU.is_ge)
                kb.v("scalar_tensor_tensor", out=t2[:], in0=mk1[:], scalar=-1e30, in1=tl[:], op0=ALU.mult, op1=ALU.add)
                kb.v("tensor_reduce", out=m2n[:], in_=t2[:], axis=AX.X, op=ALU.max, negate=True)
                kb.wait("scalar", "ve")
                kb.s("activation", out=sg1[:], in_=m2n[:], func=AF.Sigmoid)
                kb.wait("vector", "se")
                kb.v("tensor_scalar", out=sg2[:], in0=sg1[:], scalar1=-1.0, scalar2=1.0, op0=ALU.mult, op1=ALU.add)
                kb.v("tensor_scalar", out=t2[:], in0=t2[:], scalar1=m2n[:], scalar2=None, op0=ALU.add)
                kb.v("tensor_scalar", out=mk2[:], in0=t2[:], scalar1=0.0, scalar2=None, op0=ALU.is_ge)
                kb.v("tensor_scalar", out=mk1[:], in0=mk1[:], scalar1=sg1[:], scalar2=None, op0=ALU.mult)
                kb.v("scalar_tensor_tensor", out=w_tm[:, tt * 8:(tt + 1) * 8], in0=mk2[:], scalar=sg2[:], in1=mk1[:], op0=ALU.mult, op1=ALU.add)
            for (a_ap, uG) in ((W["a1"], u1G), (W["a3"], u3G)):
                kb.wait("gpsimd", "pe")
                kb.load(a1nat[:], a_ap.rearrange("e r d -> (e r) d"), queue="gpsimd")
                kb.wait("tensor", "dg", "se")
                for c4 in range(4):
                    kb.wait("tensor", "se")
                    for j in range(4):
                        c = c4 * 4 + j
                        kb.t("transpose", out=ps_t[:, j * 128:(j + 1) * 128], in_=a1nat[:, c * 128:(c + 1) * 128], identity=C["IDENT_BF"][:])
                    kb.wait("scalar", "pe")
                    for j in range(4):
                        c = c4 * 4 + j
                        kb.s("copy", out=a1T[:, c * 128:(c + 1) * 128], in_=ps_t[:, j * 128:(j + 1) * 128])
                for e in range(8):
                    g, pb = eg(e)
                    kb.wait("tensor", "se", "ve")
                    for c in range(NC):
                        for n0 in range(0, S, 512):
                            kb.t("matmul", out=ps_u[0:16, n0:n0 + 512], lhsT=a1T[:, c * 128 + e * 16: c * 128 + (e + 1) * 16], rhs=xn_fm[:, c * S + n0: c * S + n0 + 512], start=(c == 0), stop=(c == NC - 1))
                    kb.wait("scalar", "pe")
                    kb.s("mul", out=uG[g][pb:pb + 16, :], in_=ps_u[0:16, :], mul=2.0)
            for e in range(8):
                g, pb = eg(e)
                kb.wait("gpsimd", "pe", "se")
                kb.load(b2nat[:].rearrange("p (dc r) -> p dc r", r=16), W["b2"][e, :, :].rearrange("(dc p) r -> p dc r", p=128), queue="gpsimd")
                kb.wait("tensor", "dg", "se")
                for dc in range(NC):
                    kb.t("transpose", out=ps_t[0:16, 0:128], in_=b2nat[:, dc * 16:(dc + 1) * 16], identity=C["IDENT_BF"][:])
                    kb.wait("scalar", "pe")
                    kb.s("mul", out=b2TG[g][pb:pb + 16, dc * 128:(dc + 1) * 128], in_=ps_t[0:16, 0:128], mul=2.0)
                    kb.wait("tensor", "se")
            kb.load(a2_all[:], W["a2"].rearrange("e r f -> (e r) f"), queue="gpsimd")
            kb.wait("sync", "dg")
        kb.fence()

        for qq in range(2 if L0FFN_PHASE >= 3 else (1 if L0FFN_PHASE >= 2 else 0)):
            q0 = qq * 1024
            HT = 1024
            NTT = 8
            with kb.scope() as qs:
                w1c = kb.sb_in(qs, "mf_w1c", [128, D], BF16)
                w3c = kb.sb_in(qs, "mf_w3c", [128, D], BF16)
                w2t4 = kb.sb_in(qs, "mf_w2t4", [128, 4 * D], BF16)
                b1blk = kb.sb_in(qs, "mf_b1blk", [128, 128], BF16)
                b3blk = kb.sb_in(qs, "mf_b3blk", [128, 128], BF16)

                def emit_fc_loads(fc):
                    kb.wait("sync", "pe")
                    kb.load(w1c[:].rearrange("p (c j) -> p c j", j=128), w1T[:, fc * 128:(fc + 1) * 128].rearrange("(c p) j -> p c j", p=128))
                    kb.load(w3c[:].rearrange("p (c j) -> p c j", j=128), w3T[:, fc * 128:(fc + 1) * 128].rearrange("(c p) j -> p c j", p=128))
                    kb.wait("gpsimd", "pe")
                    kb.load(b1blk[:].rearrange("p (e r) -> p e r", r=16), W["b1"][:, fc * 128:(fc + 1) * 128, :].rearrange("e p r -> p e r"), queue="gpsimd")
                    kb.load(b3blk[:].rearrange("p (e r) -> p e r", r=16), W["b3"][:, fc * 128:(fc + 1) * 128, :].rearrange("e p r -> p e r"), queue="gpsimd")
                b1TG = [kb.sb_in(qs, f"mf_b1tg{g}", [80, 128], BF16) for g in range(3)]
                b3TG = [kb.sb_in(qs, f"mf_b3tg{g}", [80, 128], BF16) for g in range(3)]
                a2T = kb.sb_in(qs, "mf_a2T", [128, 128], BF16)
                b1sb = kb.sb_in(qs, "mf_b1sb", [128, HT], BF16)
                b3sb = kb.sb_in(qs, "mf_b3sb", [128, HT], BF16)
                h1 = kb.sb_in(qs, "mf_h1", [128, HT], BF16)
                s1 = kb.sb_in(qs, "mf_s1", [128, HT], BF16)
                h3 = kb.sb_in(qs, "mf_h3", [128, HT], BF16)
                mm_ = kb.sb_in(qs, "mf_m", [128, HT], BF16)
                wact = kb.sb_in(qs, "mf_wact", [128, HT], BF16)
                acc4 = kb.sb_in(qs, "mf_acc4", [128, 4 * HT], BF16)
                wbc = kb.sb_in(qs, "mf_wbc", [128, 8 * HT], BF16)
                wrow = kb.sb_in(qs, "mf_wrow", [1, HT], F32)
                g2sb = kb.sb_in(qs, "mf_g2sb", [128, 8 * NTT * 16], BF16)
                g2TG = [kb.sb_in(qs, f"mf_g2tg{g}", [80, NTT * 128], BF16) for g in range(3)]
                stage = kb.sb_in(qs, "mf_stage", [128, D], F32)
                xres = kb.sb_in(qs, "mf_xres", [128, D], F32)
                ps_g2 = kb.ps_in(qs, "mf_ps_g2", [128, 8 * NTT * 16], F32)
                psA = kb.ps_in(qs, "mf_psA", [128, HT], F32)
                psB = kb.ps_in(qs, "mf_psB", [128, 2048], F32)
                ps_t2 = psB[:, 0:256].bitcast(BF16)
                for e in range(8):
                    kb.wait("tensor", "se", "ve")
                    for j in range(NTT):
                        tt = qq * NTT + j
                        kb.t("transpose", out=psB[0:1, j * 128:(j + 1) * 128], in_=w_tm[:, tt * 8 + e: tt * 8 + e + 1], identity=C["IDENT_F32"][:])
                    kb.wait("scalar", "pe")
                    kb.s("copy", out=wrow[:], in_=psB[0:1, 0:HT])
                    kb.wait("tensor", "se")
                    for n0 in range(0, HT, 512):
                        kb.t("matmul", out=psB[:, n0:n0 + 512], lhsT=C["ONES_ROW_F32"][:], rhs=wrow[0:1, n0:n0 + 512], start=True, stop=True)
                    kb.wait("scalar", "pe")
                    kb.s("copy", out=wbc[:, e * HT:(e + 1) * HT], in_=psB[:, 0:HT])
                emit_fc_loads(0)
                for fc in range(L0FFN_NFC):
                    sl = fc % 4
                    kb.wait("sync", "pe")
                    kb.load(w2t4[:, sl * D:(sl + 1) * D], w2T[fc * 128:(fc + 1) * 128, :])
                    kb.wait("tensor", "se", "ve", "dg")
                    kb.t("transpose", out=ps_t2[:, 0:128], in_=a2_all[:, fc * 128:(fc + 1) * 128], identity=C["IDENT_BF"][:])
                    kb.wait("scalar", "pe")
                    kb.s("copy", out=a2T[:], in_=ps_t2[:, 0:128])
                    for e in range(8):
                        g, pb = eg(e)
                        kb.wait("tensor", "se")
                        kb.t("transpose", out=ps_t2[0:16, 0:128], in_=b1blk[:, e * 16:(e + 1) * 16], identity=C["IDENT_BF"][:])
                        kb.wait("scalar", "pe")
                        kb.s("copy", out=b1TG[g][pb:pb + 16, :], in_=ps_t2[0:16, 0:128])
                        kb.wait("tensor", "se")
                        kb.t("transpose", out=ps_t2[0:16, 0:128], in_=b3blk[:, e * 16:(e + 1) * 16], identity=C["IDENT_BF"][:])
                        kb.wait("scalar", "pe")
                        kb.s("copy", out=b3TG[g][pb:pb + 16, :], in_=ps_t2[0:16, 0:128])
                    kb.wait("tensor", "se", "ve", "dl")
                    for n0 in range(0, HT, 512):
                        for c in range(NC):
                            kb.t("matmul", out=psA[:, n0:n0 + 512], lhsT=w1c[:, c * 128:(c + 1) * 128], rhs=xn_fm[:, c * S + q0 + n0: c * S + q0 + n0 + 512], start=(c == 0), stop=(c == NC - 1))
                    kb.wait("scalar", "pe")
                    kb.s("copy", out=b1sb[:], in_=psA[:, 0:HT])
                    kb.wait("tensor", "se")
                    for n0 in range(0, HT, 512):
                        for c in range(NC):
                            kb.t("matmul", out=psA[:, n0:n0 + 512], lhsT=w3c[:, c * 128:(c + 1) * 128], rhs=xn_fm[:, c * S + q0 + n0: c * S + q0 + n0 + 512], start=(c == 0), stop=(c == NC - 1))
                    kb.wait("scalar", "pe")
                    kb.s("copy", out=b3sb[:], in_=psA[:, 0:HT])
                    if fc + 1 < L0FFN_NFC:
                        emit_fc_loads(fc + 1)
                    for e in range(8):
                        g, pb = eg(e)
                        kb.wait("tensor", "se", "ve")
                        for n0 in range(0, HT, 512):
                            kb.t("matmul", out=psA[:, n0:n0 + 512], lhsT=b1TG[g][pb:pb + 16, :], rhs=u1G[g][pb:pb + 16, q0 + n0:q0 + n0 + 512], start=True, stop=True)
                        kb.wait("vector", "pe")
                        kb.v("scalar_tensor_tensor", out=h1[:], in0=psA[:, 0:HT], scalar=1.0, in1=b1sb[:], op0=ALU.mult, op1=ALU.add, nodrain=True)
                        kb.wait("scalar", "ve")
                        kb.s("activation", out=s1[:], in_=h1[:], func=AF.Silu, nodrain=True)
                        kb.wait("tensor", "ve")
                        for n0 in range(0, HT, 512):
                            kb.t("matmul", out=psA[:, n0:n0 + 512], lhsT=b3TG[g][pb:pb + 16, :], rhs=u3G[g][pb:pb + 16, q0 + n0:q0 + n0 + 512], start=True, stop=True)
                        kb.wait("vector", "pe", "se")
                        kb.v("scalar_tensor_tensor", out=h3[:], in0=psA[:, 0:HT], scalar=1.0, in1=b3sb[:], op0=ALU.mult, op1=ALU.add)
                        kb.v("tensor_tensor", out=mm_[:], in0=s1[:], in1=h3[:], op=ALU.mult)
                        kb.v("tensor_tensor", out=wact[:], in0=mm_[:], in1=wbc[:, e * HT:(e + 1) * HT], op=ALU.mult)
                        if e == 0:
                            kb.v("tensor_copy", out=acc4[:, sl * HT:(sl + 1) * HT], in_=wact[:], nodrain=True)
                        else:
                            kb.v("tensor_tensor", out=acc4[:, sl * HT:(sl + 1) * HT], in0=acc4[:, sl * HT:(sl + 1) * HT], in1=wact[:], op=ALU.add, nodrain=True)
                        kb.wait("tensor", "ve")
                        for tt in range(NTT):
                            kb.t("matmul", out=ps_g2[:, (e * NTT + tt) * 16:(e * NTT + tt + 1) * 16], lhsT=wact[:, tt * 128:(tt + 1) * 128], rhs=a2T[:, e * 16:(e + 1) * 16], start=(fc == 0), stop=(fc == L0FFN_NFC - 1))
                    if sl == 3:
                        for tt in range(NTT):
                            gt = qq * NTT + tt
                            kb.wait("tensor", "ve", "se", "dl")
                            for j in range(4):
                                for n0 in range(0, D, 512):
                                    kb.t("matmul", out=psB[:, n0:n0 + 512], lhsT=acc4[:, j * HT + tt * 128: j * HT + (tt + 1) * 128], rhs=w2t4[:, j * D + n0: j * D + n0 + 512], start=(j == 0), stop=(j == 3))
                            kb.wait("vector", "pe", "dma")
                            kb.v("tensor_copy", out=stage[:], in_=psB[:, 0:D])
                            kb.wait("gpsimd", "ve", "dma")
                            if fc == 3:
                                kb._inc(nc.gpsimd.dma_start(out=out_acc[gt * 128:(gt + 1) * 128, :], in_=stage[:]), "st", 16)
                            else:
                                kb._inc(nc.gpsimd.dma_start(out=out_acc[gt * 128:(gt + 1) * 128, :], in_=stage[:], accum_op=ALU.add), "st", 16)
                if L0FFN_NFC < NFC:
                    continue
                kb.wait("vector", "pe")
                kb.v("tensor_copy", out=g2sb[:], in_=ps_g2[:, 0:8 * NTT * 16])
                for e in range(8):
                    g, pb = eg(e)
                    for tt in range(NTT):
                        kb.wait("tensor", "ve", "se")
                        kb.t("transpose", out=ps_t2[0:16, 0:128], in_=g2sb[:, (e * NTT + tt) * 16:(e * NTT + tt + 1) * 16], identity=C["IDENT_BF"][:])
                        kb.wait("scalar", "pe")
                        kb.s("copy", out=g2TG[g][pb:pb + 16, tt * 128:(tt + 1) * 128], in_=ps_t2[0:16, 0:128])
                for tt in range(NTT if L0FFN_END >= 2 else 0):
                    gt = qq * NTT + tt
                    kb.wait("sync", "ve", "dma")
                    kb.load(xres[:], x_in[gt * 128:(gt + 1) * 128, :])
                    kb.wait("vector", "dma")
                    kb.load(stage[:], out_acc[gt * 128:(gt + 1) * 128, :])
                    first = True
                    for e in range(8):
                        g, pb = eg(e)
                        kb.wait("tensor", "ve", "se")
                        for n0 in range(0, D, 512):
                            kb.t("matmul", out=psB[:, n0:n0 + 512], lhsT=g2TG[g][pb:pb + 16, tt * 128:(tt + 1) * 128], rhs=b2TG[g][pb:pb + 16, n0:n0 + 512], start=True, stop=True)
                        kb.wait("vector", "pe", "dl")
                        kb.v("tensor_tensor", out=stage[:], in0=stage[:], in1=psB[:, 0:D], op=ALU.add)
                    kb.wait("vector", "dl")
                    kb.v("tensor_tensor", out=stage[:], in0=stage[:], in1=xres[:], op=ALU.add)
                    if L0FFN_END >= 3:
                        kb.store(x_out[gt * 128:(gt + 1) * 128, :], stage[:], wait_on=("ve",))
                    kb.wait("vector", "dma")
            kb.fence()
    kb.fence()



SURPRISE_THR_ARG = float((1.0 + 1e-6) * math.log(0.7 / 0.3))
SUR_SCALE = float(1.0 / (1.0 + 1e-6))
SCAN_STEPS = S


def emit_l1_attn_a(kb, C, x_in, x_out, W):
    nc = kb.nc
    with kb.scope() as sc:
        k_fm = kb.sb_in(sc, "sa_k", [128, S], F32)
        q_fm = kb.sb_in(sc, "sa_q", [128, S], F32)
        v_fm = kb.sb_in(sc, "sa_v", [128, S], F32)
        af = kb.sb_in(sc, "sa_af", [128, S], F32)
        af1m = kb.sb_in(sc, "sa_af1m", [128, S], F32)
        as_ = kb.sb_in(sc, "sa_as", [128, S], F32)
        as1mU = kb.sb_in(sc, "sa_as1mU", [128, S], F32)
        nu = kb.sb_in(sc, "sa_nu", [128, S], F32)
        outF = kb.sb_in(sc, "sa_outF", [128, S], F32)
        outS = kb.sb_in(sc, "sa_outS", [128, S], F32)
        nSarr = kb.sb_in(sc, "sa_nS", [128, S], F32)
        pw_tm = kb.sb_in(sc, "sa_pwtm", [128, 3 * NT], F32)
        fH = kb.sb_in(sc, "sa_fH", [128, 128], F32)
        sH = kb.sb_in(sc, "sa_sH", [128, 128], F32)
        zc = kb.sb_in(sc, "sa_zc", [128, 1], F32)

        with kb.scope() as bsc:
            xn_fm = kb.sb_in(bsc, "sa_xn", [128, NC * S], BF16)
            with kb.scope() as lnsc:
                xtile = kb.sb_in(lnsc, "sa_xtile", [128, D], F32)
                scrn = ln_scratch(kb, lnsc, "sa")
                emit_ln(kb, C, x_in, xn_fm, xtile, scrn)
            kb.fence()
            wnat = kb.sb_in(bsc, "sa_wnat", [128, NC * 128], BF16)
            colb = kb.sb_in(bsc, "sa_colb", [128, 1], F32)
            col2 = kb.sb_in(bsc, "sa_col2", [128, 1], F32)
            tmpb = kb.sb_in(bsc, "sa_tmpb", [128, S], F32)
            rowb = kb.sb_in(bsc, "sa_rowb", [1, S], F32)
            vbf = kb.sb_in(bsc, "sa_vbf", [128, S], BF16)
            nmw = kb.sb_in(bsc, "sa_nmw", [128, 256], BF16)
            nmw2 = kb.sb_in(bsc, "sa_nmw2", [128, 256], BF16)
            g1 = kb.sb_in(bsc, "sa_g1", [128, 2 * S], BF16)
            pww = kb.sb_in(bsc, "sa_pww", [128, 48], BF16)
            prow = kb.sb_in(bsc, "sa_prow", [1, 8], F32)
            tmpc = kb.sb_in(bsc, "sa_tmpc", [128, S], F32)
            ps_p = kb.ps_in(bsc, "sa_ps_p", [128, S], F32)
            ps_t = kb.ps_in(bsc, "sa_ps_t", [128, 512], BF16)
            ps_sm = kb.ps_in(bsc, "sa_ps_sm", [128, 16], F32)

            def load_w_nat(ap):
                kb.wait("gpsimd", "pe")
                kb.load(wnat[:].rearrange("p (c j) -> p c j", j=128), ap.rearrange("(c p) j -> p c j", p=128), queue="gpsimd")

            def proj_psum(dst_ps):
                for n0 in range(0, S, 512):
                    for c in range(NC):
                        kb.t("matmul", out=dst_ps[:, n0:n0 + 512], lhsT=wnat[:, c * 128:(c + 1) * 128], rhs=xn_fm[:, c * S + n0:c * S + n0 + 512], start=(c == 0), stop=(c == NC - 1))

            def l2norm_from_psum(dst):
                kb.wait("scalar", "ve", "pe")
                kb.s("activation", out=tmpc[:], in_=ps_p[:, 0:S], func=AF.Square)
                kb.wait("vector", "pe", "se")
                kb.v("tensor_copy", out=tmpb[:], in_=ps_p[:, 0:S])
                kb.wait("tensor", "se", "ve")
                for n0 in range(0, S, 512):
                    kb.t("matmul", out=ps_p[0:1, n0:n0 + 512], lhsT=C["ONES_COL_F32"][:], rhs=tmpc[:, n0:n0 + 512], start=True, stop=True)
                kb.wait("scalar", "pe")
                kb.s("activation", out=rowb[:], in_=ps_p[0:1, 0:S], func=AF.Sqrt)
                kb.wait("vector", "se")
                kb.v("tensor_scalar", out=rowb[:], in0=rowb[:], scalar1=1e-12, scalar2=None, op0=ALU.max)
                kb.v("reciprocal", out=rowb[:], in_=rowb[:])
                kb.wait("tensor", "ve", "se")
                for n0 in range(0, S, 512):
                    kb.t("matmul", out=ps_p[:, n0:n0 + 512], lhsT=C["ONES_ROW_F32"][:], rhs=rowb[0:1, n0:n0 + 512], start=True, stop=True)
                kb.wait("vector", "pe")
                kb.v("tensor_tensor", out=dst[:], in0=tmpb[:], in1=ps_p[:, 0:S], op=ALU.mult)

            load_w_nat(W["Wk"])
            kb.wait("tensor", "dg", "se", "ve")
            proj_psum(ps_p)
            l2norm_from_psum(k_fm)
            load_w_nat(W["Wq"])
            kb.wait("tensor", "dg", "se", "ve")
            proj_psum(ps_p)
            l2norm_from_psum(q_fm)
            load_w_nat(W["Wv"])
            kb.wait("tensor", "dg", "se", "ve")
            proj_psum(ps_p)
            kb.wait("vector", "pe")
            kb.v("tensor_copy", out=v_fm[:], in_=ps_p[:, 0:S])
            kb.v("tensor_copy", out=vbf[:], in_=v_fm[:])
            for (Wg, bg, lam, a_dst, a1m_dst) in ((W["Waf"], W["baf"], W["fast_lambda"], af, af1m),
                                                  (W["Was"], W["bas"], W["slow_lambda"], as_, as1mU)):
                kb.wait("sync", "se", "ve")
                kb.load(colb[:], bg.rearrange("(p one) -> p one", one=1), queue="sync")
                kb.load(col2[:], lam.rearrange("(p one) -> p one", one=1), queue="sync")
                load_w_nat(Wg)
                kb.wait("tensor", "dg", "se", "ve")
                proj_psum(ps_p)
                kb.wait("scalar", "dl", "ve")
                kb.s("activation", out=col2[:], in_=col2[:], func=AF.Sigmoid)
                kb.s("activation", out=col2[:], in_=col2[:], func=AF.Ln, bias=C["EPS8_COL"][:])
                kb.wait("vector", "se")
                kb.v("tensor_scalar", out=col2[:], in0=col2[:], scalar1=8.0, scalar2=None, op0=ALU.mult)
                kb.wait("scalar", "pe", "ve")
                kb.s("activation", out=tmpb[:], in_=ps_p[:, 0:S], func=AF.Sigmoid, bias=colb[:])
                kb.wait("vector", "se")
                kb.v("tensor_scalar", out=tmpb[:], in0=tmpb[:], scalar1=col2[:], scalar2=None, op0=ALU.mult)
                kb.wait("scalar", "ve")
                kb.s("activation", out=a_dst[:], in_=tmpb[:], func=AF.Exp)
                kb.wait("vector", "se")
                kb.v("tensor_scalar", out=a1m_dst[:], in0=a_dst[:], scalar1=-1.0, scalar2=1.0, op0=ALU.mult, op1=ALU.add)
            kb.wait("sync", "se", "ve")
            kb.load(colb[:], W["ir_b1"].rearrange("(p one) -> p one", one=1), queue="sync")
            load_w_nat(W["ir_w1"])
            kb.wait("tensor", "dg", "se", "ve")
            proj_psum(ps_p)
            kb.wait("scalar", "pe", "dl", "ve")
            kb.s("activation", out=tmpb[:], in_=ps_p[:, 0:S], func=AF.Gelu, bias=colb[:])
            kb.wait("sync", "se", "ve")
            kb.load(col2[:], W["ir_w2"], queue="sync")
            irb2 = kb.sb_in(bsc, "sa_irb2", [1, 1], F32)
            kb.load(irb2[:], W["ir_b2"].rearrange("(one o2) -> one o2", one=1), queue="sync")
            kb.wait("tensor", "se", "dl", "ve")
            for n0 in range(0, S, 512):
                kb.t("matmul", out=ps_p[0:1, n0:n0 + 512], lhsT=col2[:], rhs=tmpb[:, n0:n0 + 512], start=True, stop=True)
            kb.wait("scalar", "pe")
            kb.s("activation", out=rowb[:], in_=ps_p[0:1, 0:S], func=AF.Sigmoid, bias=irb2[:])
            kb.wait("vector", "se")
            kb.v("tensor_scalar", out=rowb[:], in0=rowb[:], scalar1=0.5, scalar2=None, op0=ALU.is_gt)
            kb.wait("tensor", "ve")
            for n0 in range(0, S, 512):
                kb.t("matmul", out=ps_p[:, n0:n0 + 512], lhsT=C["ONES_ROW_F32"][:], rhs=rowb[0:1, n0:n0 + 512], start=True, stop=True)
            kb.wait("vector", "pe")
            kb.v("tensor_tensor", out=as1mU[:], in0=as1mU[:], in1=ps_p[:, 0:S], op=ALU.mult)
            pwb = kb.sb_in(bsc, "sa_pwb", [1, 4], BF16)
            kb.load(pwb[0:1, 0:3], W["pw_b"].rearrange("(one j) -> one j", one=1), queue="gpsimd")
            pwt_bf = kb.sb_in(bsc, "sa_pwtbf", [3, S], BF16)
            kb.load(pwt_bf[:], W["pw_wT"], queue="gpsimd")
            kb.wait("tensor", "dg", "ve", "se")
            for c in range(NC):
                kb.t("transpose", out=ps_t[:, 0:3], in_=pwt_bf[:, c * 128:(c + 1) * 128], identity=C["I16"][0:3, 0:3])
                kb.wait("scalar", "pe")
                kb.s("copy", out=pww[:, c * 3:(c + 1) * 3], in_=ps_t[:, 0:3])
                kb.wait("tensor", "se")
            for tt in range(NT):
                kb.wait("tensor", "se", "ve", "dg")
                for c in range(NC):
                    kb.t("matmul", out=ps_sm[:, 0:3], lhsT=xn_fm[:, c * S + tt * 128: c * S + (tt + 1) * 128], rhs=pww[:, c * 3:(c + 1) * 3], start=(c == 0), stop=False)
                kb.t("matmul", out=ps_sm[:, 0:3], lhsT=C["ONES_ROW"][:], rhs=pwb[0:1, 0:3], start=False, stop=True)
                kb.wait("vector", "pe")
                kb.v("tensor_reduce", out=colb[:], in_=ps_sm[:, 0:3], axis=AX.X, op=ALU.max, negate=True)
                kb.wait("scalar", "ve")
                kb.s("activation", out=pw_tm[:, tt * 3:(tt + 1) * 3], in_=ps_sm[:, 0:3], func=AF.Exp, bias=colb[:], accum_out=col2[:])
                kb.wait("vector", "se")
                kb.v("reciprocal", out=col2[:], in_=col2[:])
                kb.v("tensor_scalar", out=pw_tm[:, tt * 3:(tt + 1) * 3], in0=pw_tm[:, tt * 3:(tt + 1) * 3], scalar1=col2[:], scalar2=None, op0=ALU.mult)
            kb.load(nmw[:], W["nm_w1"], queue="gpsimd")
            kb.load(nmw2[:].rearrange("p (c j) -> p c j", j=128), W["nm_w2"].rearrange("(c p) j -> p c j", p=128), queue="gpsimd")
            b1c = kb.sb_in(bsc, "sa_b1c", [128, 2], F32)
            b2c = kb.sb_in(bsc, "sa_b2c", [128, 1], F32)
            kb.load(b1c[:, 0:1], W["nm_b1"][0:128].rearrange("(p one) -> p one", one=1), queue="sync")
            kb.load(b1c[:, 1:2], W["nm_b1"][128:256].rearrange("(p one) -> p one", one=1), queue="sync")
            kb.load(b2c[:], W["nm_b2"].rearrange("(p one) -> p one", one=1), queue="sync")
            kb.wait("tensor", "dg", "dl", "se", "ve")
            for mc in range(2):
                for n0 in range(0, S, 512):
                    kb.t("matmul", out=ps_p[:, n0:n0 + 512], lhsT=nmw[:, mc * 128:(mc + 1) * 128], rhs=vbf[:, n0:n0 + 512], start=True, stop=True)
                kb.wait("scalar", "pe", "ve")
                kb.s("activation", out=g1[:, mc * S:(mc + 1) * S], in_=ps_p[:, 0:S], func=AF.Gelu, bias=b1c[:, mc:mc + 1])
                kb.wait("tensor", "se")
            for n0 in range(0, S, 512):
                for mc in range(2):
                    kb.t("matmul", out=ps_p[:, n0:n0 + 512], lhsT=nmw2[:, mc * 128:(mc + 1) * 128], rhs=g1[:, mc * S + n0: mc * S + n0 + 512], start=(mc == 0), stop=(mc == 1))
            kb.wait("scalar", "pe")
            kb.s("activation", out=nu[:], in_=ps_p[:, 0:S], func=AF.Identity, bias=b2c[:])
        kb.fence()

        if DEBUG:
            kb.dump("dbg_k", k_fm[:], [128, S], F32)
            kb.dump("dbg_q", q_fm[:], [128, S], F32)
            kb.dump("dbg_v", v_fm[:], [128, S], F32)
            kb.dump("dbg_af", af[:], [128, S], F32)
            kb.dump("dbg_af1m", af1m[:], [128, S], F32)
            kb.dump("dbg_as", as_[:], [128, S], F32)
            kb.dump("dbg_as1mU", as1mU[:], [128, S], F32)
            kb.dump("dbg_nu", nu[:], [128, S], F32)
            kb.dump("dbg_pwtm", pw_tm[:], [128, 3 * NT], F32)
        kb.g("memset", fH[:], 0.0)
        kb.g("memset", sH[:], 0.0)
        kb.g("memset", zc[:], 0.0)
        with kb.scope() as ssc:
            kvrow = kb.sb_in(ssc, "sa_kvrow", [1, 256], F32)
            kr = kvrow[0:1, 0:128]
            vr = kvrow[0:1, 128:256]
            predh = kb.sb_in(ssc, "sa_predh", [1, 128], F32)
            e_sb = kb.sb_in(ssc, "sa_e", [1, 128], F32)
            e2_sb = kb.sb_in(ssc, "sa_e2", [1, 128], F32)
            err_sb = kb.sb_in(ssc, "sa_err", [1, 1], F32)
            row2 = kb.sb_in(ssc, "sa_row2", [1, 2], F32)
            wf_sb = kb.sb_in(ssc, "sa_wf", [1, 128], F32)
            ws_sb = kb.sb_in(ssc, "sa_ws", [1, 128], F32)
            gf = kb.sb_in(ssc, "sa_gf", [128, 1], F32)
            gs = kb.sb_in(ssc, "sa_gs", [128, 1], F32)
            tf_sb = kb.sb_in(ssc, "sa_tf", [128, 128], F32)
            ts_sb = kb.sb_in(ssc, "sa_ts", [128, 128], F32)
            d_sb = kb.sb_in(ssc, "sa_d", [128, 1], F32)
            psKV = kb.ps_in(ssc, "sa_psKV", [1, 512], F32)
            psP = kb.ps_in(ssc, "sa_psP", [1, 512], F32)
            psB = kb.ps_in(ssc, "sa_psB", [128, 512], F32)
            psOF = kb.ps_in(ssc, "sa_psOF", [128, 512], F32)
            psOS = kb.ps_in(ssc, "sa_psOS", [128, 512], F32)
            psOutF = kb.ps_in(ssc, "sa_psOutF", [128, 512], F32)
            psOutS = kb.ps_in(ssc, "sa_psOutS", [128, 512], F32)
            kb.fence()
            pend = []
            for t in range(SCAN_STEPS):
                cc = t % 512
                tcol = slice(t, t + 1)
                nSprev = zc[:] if t == 0 else nSarr[:, t - 1:t]
                kb.wait("tensor", "se", "ve")
                for tp in pend:
                    kb.t("matmul", out=psOutF[:, (tp % 512):(tp % 512) + 1], lhsT=fH[:], rhs=q_fm[:, tp:tp + 1], start=True, stop=True)
                    kb.t("matmul", out=psOutS[:, (tp % 512):(tp % 512) + 1], lhsT=sH[:], rhs=q_fm[:, tp:tp + 1], start=True, stop=True)
                pend = []
                kb.t("transpose", out=psKV[0:1, 0:128], in_=k_fm[:, tcol], identity=C["IDENT_F32"][:])
                kb.t("transpose", out=psKV[0:1, 128:256], in_=v_fm[:, tcol], identity=C["IDENT_F32"][:])
                kb.t("matmul", out=psP[0:1, 0:128], lhsT=q_fm[:, tcol], rhs=fH[:], start=True, stop=False)
                kb.t("matmul", out=psP[0:1, 0:128], lhsT=q_fm[:, tcol], rhs=sH[:], start=False, stop=True)
                kb.t("matmul", out=psP[0:1, 128:256], lhsT=k_fm[:, tcol], rhs=fH[:], start=True, stop=True)
                kb.t("matmul", out=psP[0:1, 256:384], lhsT=k_fm[:, tcol], rhs=sH[:], start=True, stop=True)
                kb.wait("scalar", "pe", "ve")
                kb.s("copy", out=kvrow[:], in_=psKV[0:1, 0:256], nodrain=True)
                kb.wait("vector", "se", "pe")
                kb.v("scalar_tensor_tensor", out=e_sb[:], in0=psP[0:1, 0:128], scalar=-0.5, in1=vr, op0=ALU.mult, op1=ALU.add)
                kb.v("scalar_tensor_tensor", out=e2_sb[:], in0=e_sb[:], scalar=1.0, in1=e_sb[:], op0=ALU.mult, op1=ALU.mult, accum_out=err_sb[:])
                kb.v("tensor_tensor", out=wf_sb[:], in0=vr, in1=psP[0:1, 128:256], op=ALU.subtract, nodrain=True)
                kb.v("tensor_tensor", out=ws_sb[:], in0=vr, in1=psP[0:1, 256:384], op=ALU.subtract, nodrain=True)
                kb.wait("scalar", "ve")
                kb.s("activation", out=row2[0:1, 0:1], in_=err_sb[:], func=AF.Sigmoid, scale=SUR_SCALE, nodrain=True)
                kb.v("tensor_scalar", out=row2[0:1, 1:2], in0=err_sb[:], scalar1=SURPRISE_THR_ARG, scalar2=0.1, op0=ALU.is_gt, op1=ALU.mult, nodrain=True)
                kb.wait("tensor", "ve", "se")
                kb.t("matmul", out=psB[:, 0:2], lhsT=C["ONES_ROW_F32"][:], rhs=row2[:], start=True, stop=True)
                kb.t("matmul", out=psOF[:, 0:128], lhsT=kr, rhs=wf_sb[:], start=True, stop=True)
                kb.t("matmul", out=psOS[:, 0:128], lhsT=kr, rhs=ws_sb[:], start=True, stop=True)
                kb.wait("vector", "pe")
                kb.v("tensor_scalar", out=tf_sb[:], in0=psOF[:, 0:128], scalar1=af1m[:, tcol], scalar2=psB[:, 0:1], op0=ALU.mult, op1=ALU.mult)
                kb.v("scalar_tensor_tensor", out=fH[:], in0=fH[:], scalar=af[:, tcol], in1=tf_sb[:], op0=ALU.mult, op1=ALU.add, nodrain=True)
                kb.v("tensor_scalar", out=ts_sb[:], in0=psOS[:, 0:128], scalar1=as1mU[:, tcol], scalar2=psB[:, 0:1], op0=ALU.mult, op1=ALU.mult)
                kb.v("scalar_tensor_tensor", out=sH[:], in0=sH[:], scalar=as_[:, tcol], in1=ts_sb[:], op0=ALU.mult, op1=ALU.add, nodrain=True)
                kb.v("tensor_tensor", out=d_sb[:], in0=nu[:, tcol], in1=nSprev, op=ALU.subtract)
                kb.v("scalar_tensor_tensor", out=nSarr[:, tcol], in0=d_sb[:], scalar=psB[:, 1:2], in1=nSprev, op0=ALU.mult, op1=ALU.add, nodrain=True)
                if cc == 511:
                    kb.wait("tensor", "ve")
                    kb.t("matmul", out=psOutF[:, cc:cc + 1], lhsT=fH[:], rhs=q_fm[:, tcol], start=True, stop=True)
                    kb.t("matmul", out=psOutS[:, cc:cc + 1], lhsT=sH[:], rhs=q_fm[:, tcol], start=True, stop=True)
                    kb.wait("vector", "pe")
                    kb.v("tensor_copy", out=outF[:, t - 511:t + 1], in_=psOutF[:, 0:512], nodrain=True)
                    kb.v("tensor_copy", out=outS[:, t - 511:t + 1], in_=psOutS[:, 0:512], nodrain=True)
                else:
                    pend.append(t)
        kb.fence()

        if DEBUG:
            kb.dump("dbg_outF", outF[:], [128, S], F32)
            kb.dump("dbg_outS", outS[:], [128, S], F32)
            kb.dump("dbg_nSarr", nSarr[:], [128, S], F32)
        with kb.scope() as fsc:
            outz = kb.sb_in(fsc, "sa_outz", [128, S], F32)
            tmpf = kb.sb_in(fsc, "sa_tmpf", [128, S], F32)
            prow = kb.sb_in(fsc, "sa_prow2", [1, S], F32)
            nwcol = kb.sb_in(fsc, "sa_nwcol", [128, 1], F32)
            out_bf = kb.sb_in(fsc, "sa_outbf", [128, S], BF16)
            wo_bf = kb.sb_in(fsc, "sa_wobf", [128, D], BF16)
            xres = kb.sb_in(fsc, "sa_xres", [128, D], F32)
            outb = kb.sb_in(fsc, "sa_outb", [128, D], F32)
            ps_big = kb.ps_in(fsc, "sa_ps_big", [128, S], F32)
            ps_t3 = kb.ps_in(fsc, "sa_ps_t3", [128, 512], F32)
            kb.load(nwcol[:], W["norm_w"].rearrange("(p one) -> p one", one=1), queue="sync")
            kb.load(wo_bf[:], W["Wo"], queue="gpsimd")
            for j, arr in ((0, outF), (1, outS), (2, nSarr)):
                for t4 in range(0, NT, 4):
                    kb.wait("tensor", "se", "ve")
                    for i in range(4):
                        tt = t4 + i
                        kb.t("transpose", out=ps_t3[0:1, i * 128:(i + 1) * 128], in_=pw_tm[:, tt * 3 + j: tt * 3 + j + 1], identity=C["IDENT_F32"][:])
                    kb.wait("scalar", "pe")
                    kb.s("copy", out=prow[0:1, t4 * 128:(t4 + 4) * 128], in_=ps_t3[0:1, 0:512])
                kb.wait("tensor", "se")
                for n0 in range(0, S, 512):
                    kb.t("matmul", out=ps_big[:, n0:n0 + 512], lhsT=C["ONES_ROW_F32"][:], rhs=prow[0:1, n0:n0 + 512], start=True, stop=True)
                kb.wait("vector", "pe")
                if j == 0:
                    kb.v("tensor_tensor", out=outz[:], in0=arr[:], in1=ps_big[:, 0:S], op=ALU.mult)
                else:
                    kb.v("tensor_tensor", out=tmpf[:], in0=arr[:], in1=ps_big[:, 0:S], op=ALU.mult)
                    kb.v("tensor_tensor", out=outz[:], in0=outz[:], in1=tmpf[:], op=ALU.add)
            kb.wait("scalar", "ve")
            kb.s("activation", out=tmpf[:], in_=outz[:], func=AF.Square)
            kb.wait("tensor", "se", "ve")
            for n0 in range(0, S, 512):
                kb.t("matmul", out=ps_big[0:1, n0:n0 + 512], lhsT=C["ONES_COL_F32"][:], rhs=tmpf[:, n0:n0 + 512], start=True, stop=True)
            kb.wait("vector", "pe")
            kb.v("tensor_scalar", out=ps_big[0:1, 0:S], in0=ps_big[0:1, 0:S], scalar1=1.0 / 128.0, scalar2=1e-6, op0=ALU.mult, op1=ALU.add)
            kb.wait("scalar", "ve")
            kb.s("activation", out=prow[:], in_=ps_big[0:1, 0:S], func=AF.Sqrt)
            kb.wait("vector", "se")
            kb.v("reciprocal", out=prow[:], in_=prow[:])
            kb.wait("tensor", "ve")
            for n0 in range(0, S, 512):
                kb.t("matmul", out=ps_big[:, n0:n0 + 512], lhsT=C["ONES_ROW_F32"][:], rhs=prow[0:1, n0:n0 + 512], start=True, stop=True)
            kb.wait("vector", "pe")
            kb.v("tensor_tensor", out=outz[:], in0=outz[:], in1=ps_big[:, 0:S], op=ALU.mult)
            kb.v("tensor_scalar", out=out_bf[:], in0=outz[:], scalar1=nwcol[:], scalar2=None, op0=ALU.mult)
            for tt in range(NT):
                kb.wait("sync", "ve")
                kb.load(xres[:], x_in[tt * 128:(tt + 1) * 128, :])
                kb.wait("tensor", "ve", "dg")
                for n0 in range(0, D, 512):
                    kb.t("matmul", out=ps_big[:, n0:n0 + 512], lhsT=out_bf[:, tt * 128:(tt + 1) * 128], rhs=wo_bf[:, n0:n0 + 512], start=True, stop=True)
                kb.wait("vector", "pe", "dl")
                kb.v("tensor_tensor", out=outb[:], in0=ps_big[:, 0:D], in1=xres[:], op=ALU.add)
                kb.store(x_out[tt * 128:(tt + 1) * 128, :], outb[:], wait_on=("ve",))
                kb.wait("vector", "st")
    kb.fence()



SUB_WEIGHTS = {
    "l0_attn_a": ["L0_attn_a_W_K", "L0_attn_a_W_Q", "L0_attn_a_W_V", "L0_attn_a_W_O"],
    "l0_attn_b": ["L0_attn_b_Wq", "L0_attn_b_Wk", "L0_attn_b_Wv", "L0_attn_b_Wo",
                  "L0_attn_b_lambda_q1", "L0_attn_b_lambda_k1", "L0_attn_b_lambda_q2",
                  "L0_attn_b_lambda_k2", "L0_attn_b_ln_w", "L0_attn_b_ln_b"],
    "l1_attn_b": ["L1_attn_b_Wq", "L1_attn_b_bq", "L1_attn_b_Wk", "L1_attn_b_bk",
                  "L1_attn_b_Wv", "L1_attn_b_bv", "L1_attn_b_Wo", "L1_attn_b_bo"],
    "l1_ffn": ["L1_ffn_w_gate_up", "L1_ffn_w_proj"],
    "l0_ffn": ["L0_ffn_gate_w", "L0_ffn_w1", "L0_ffn_w3", "L0_ffn_w2", "L0_ffn_a1",
               "L0_ffn_b1", "L0_ffn_a3", "L0_ffn_b3", "L0_ffn_a2", "L0_ffn_b2"],
    "l1_attn_a": ["L1_attn_a_Wk", "L1_attn_a_Wv", "L1_attn_a_Wq", "L1_attn_a_fast_lambda",
                  "L1_attn_a_slow_lambda", "L1_attn_a_Waf", "L1_attn_a_baf", "L1_attn_a_Was",
                  "L1_attn_a_bas", "L1_attn_a_nm_w1", "L1_attn_a_nm_b1", "L1_attn_a_nm_w2",
                  "L1_attn_a_nm_b2", "L1_attn_a_ir_w1", "L1_attn_a_ir_b1", "L1_attn_a_ir_w2",
                  "L1_attn_a_ir_b2", "L1_attn_a_pw_b", "L1_attn_a_Wo", "L1_attn_a_norm_w"],
}
SUB_CONSTS = {
    "l0_attn_a": ["IDENT_BF", "MASKA"],
    "l0_attn_b": ["IDENT_BF", "IDENT_F32", "ONES_ROW_F32", "MASKB", "COS", "SIN", "PROT_T"],
    "l1_attn_b": ["IDENT_BF", "IDENT_F32", "ONES_ROW", "ONES_COL", "DECAY_TM", "DECAY_BC"],
    "l1_ffn": ["IDENT_BF", "ONES_ROW_F32", "ONES_COL"],
    "l0_ffn": ["IDENT_BF", "IDENT_F32", "I16", "ONES_ROW_F32"],
    "l1_attn_a": ["IDENT_BF", "IDENT_F32", "I16", "ONES_ROW", "ONES_ROW_F32", "ONES_COL_F32", "EPS8_COL"],
}


def build_l0_attn_a():
    kb = KB()
    x_in = kb.din("x_in", [S, D], F32)
    x_out = kb.dout("x_out", [S, D], F32)
    W = {
        "W_K": kb.din("L0_attn_a_W_K", [16, 128, D], F32),
        "W_Q": kb.din("L0_attn_a_W_Q", [16, 128, D], F32),
        "W_V": kb.din("L0_attn_a_W_V", [16, 128, D], F32),
        "W_O": kb.din("L0_attn_a_W_O", [D, D], F32),
    }
    C = emit_consts(kb, SUB_CONSTS["l0_attn_a"])
    emit_l0_attn_a(kb, C, x_in, x_out, W)
    kb.finish()
    kb.close()
    return kb


def build_l0_attn_b():
    kb = KB()
    x_in = kb.din("x_in", [S, D], F32)
    x_out = kb.dout("x_out", [S, D], F32)
    W = {
        "Wq": kb.din("L0_attn_b_Wq", [D, D], F32),
        "Wk": kb.din("L0_attn_b_Wk", [D, D], F32),
        "Wv": kb.din("L0_attn_b_Wv", [D, D], F32),
        "Wo": kb.din("L0_attn_b_Wo", [D, D], F32),
        "lambda_q1": kb.din("L0_attn_b_lambda_q1", [64], F32),
        "lambda_k1": kb.din("L0_attn_b_lambda_k1", [64], F32),
        "lambda_q2": kb.din("L0_attn_b_lambda_q2", [64], F32),
        "lambda_k2": kb.din("L0_attn_b_lambda_k2", [64], F32),
        "ln_w": kb.din("L0_attn_b_ln_w", [128], F32),
        "ln_b": kb.din("L0_attn_b_ln_b", [128], F32),
    }
    C = emit_consts(kb, SUB_CONSTS["l0_attn_b"])
    emit_l0_attn_b(kb, C, x_in, x_out, W)
    kb.finish()
    kb.close()
    return kb


def build_l1_attn_b():
    kb = KB()
    x_in = kb.din("x_in", [S, D], F32)
    x_out = kb.dout("x_out", [S, D], F32)
    W = {
        "Wq": kb.din("L1_attn_b_Wq", [D, D], F32),
        "bq": kb.din("L1_attn_b_bq", [D], F32),
        "Wk": kb.din("L1_attn_b_Wk", [D, D], F32),
        "bk": kb.din("L1_attn_b_bk", [D], F32),
        "Wv": kb.din("L1_attn_b_Wv", [D, D], F32),
        "bv": kb.din("L1_attn_b_bv", [D], F32),
        "Wo": kb.din("L1_attn_b_Wo", [D, D], F32),
        "bo": kb.din("L1_attn_b_bo", [D], F32),
        "DECAY_BC": kb.din("DECAY_BC", [128, S], F32),
    }
    C = emit_consts(kb, SUB_CONSTS["l1_attn_b"][:-1])
    emit_l1_attn_b(kb, C, x_in, x_out, W)
    kb.finish()
    kb.close()
    return kb


def build_l0_ffn():
    kb = KB()
    x_in = kb.din("x_in", [S, D], F32)
    x_out = kb.dout("x_out", [S, D], F32)
    W = {
        "gate_w": kb.din("L0_ffn_gate_w", [8, D], F32),
        "w1": kb.din("L0_ffn_w1", [F0, D], F32),
        "w3": kb.din("L0_ffn_w3", [F0, D], F32),
        "w2": kb.din("L0_ffn_w2", [D, F0], F32),
        "a1": kb.din("L0_ffn_a1", [8, 16, D], F32),
        "b1": kb.din("L0_ffn_b1", [8, F0, 16], F32),
        "a3": kb.din("L0_ffn_a3", [8, 16, D], F32),
        "b3": kb.din("L0_ffn_b3", [8, F0, 16], F32),
        "a2": kb.din("L0_ffn_a2", [8, 16, F0], F32),
        "b2": kb.din("L0_ffn_b2", [8, D, 16], F32),
    }
    C = emit_consts(kb, SUB_CONSTS["l0_ffn"])
    scr = (kb.dtmp("mf_w1bf", [F0, D], BF16), kb.dtmp("mf_w3bf", [F0, D], BF16),
           kb.dtmp("mf_w2bf", [D, F0], BF16), kb.dtmp("mf_w1T", [D, F0], BF16),
           kb.dtmp("mf_w3T", [D, F0], BF16), kb.dtmp("mf_w2T", [F0, D], BF16))
    out_acc = kb.dtmp("mf_outacc", [S, D], F32)
    emit_l0_ffn_prepass(kb, C, W, scr)
    emit_l0_ffn(kb, C, x_in, x_out, W, scr, out_acc)
    kb.finish()
    kb.close()
    return kb


def build_l1_attn_a():
    kb = KB()
    x_in = kb.din("x_in", [S, D], F32)
    x_out = kb.dout("x_out", [S, D], F32)
    W = {}
    for short, name, shape in (
        ("Wk", "L1_attn_a_Wk", [D, 128]), ("Wv", "L1_attn_a_Wv", [D, 128]), ("Wq", "L1_attn_a_Wq", [D, 128]),
        ("fast_lambda", "L1_attn_a_fast_lambda", [128]), ("slow_lambda", "L1_attn_a_slow_lambda", [128]),
        ("Waf", "L1_attn_a_Waf", [D, 128]), ("baf", "L1_attn_a_baf", [128]),
        ("Was", "L1_attn_a_Was", [D, 128]), ("bas", "L1_attn_a_bas", [128]),
        ("nm_w1", "L1_attn_a_nm_w1", [128, 256]), ("nm_b1", "L1_attn_a_nm_b1", [256]),
        ("nm_w2", "L1_attn_a_nm_w2", [256, 128]), ("nm_b2", "L1_attn_a_nm_b2", [128]),
        ("ir_w1", "L1_attn_a_ir_w1", [D, 128]), ("ir_b1", "L1_attn_a_ir_b1", [128]),
        ("ir_w2", "L1_attn_a_ir_w2", [128, 1]), ("ir_b2", "L1_attn_a_ir_b2", [1]),
        ("pw_b", "L1_attn_a_pw_b", [3]), ("Wo", "L1_attn_a_Wo", [128, D]),
        ("norm_w", "L1_attn_a_norm_w", [128]),
    ):
        W[short] = kb.din(name, shape, F32)
    W["pw_wT"] = kb.din("L1_attn_a_pw_wT", [3, D], F32)
    C = emit_consts(kb, SUB_CONSTS["l1_attn_a"])
    emit_l1_attn_a(kb, C, x_in, x_out, W)
    kb.finish()
    kb.close()
    return kb


def build_l1_ffn():
    kb = KB()
    x_in = kb.din("x_in", [S, D], F32)
    x_out = kb.dout("x_out", [S, D], F32)
    W = {
        "w_gate_up": kb.din("L1_ffn_w_gate_up", [2 * H1F, D], F32),
        "w_proj": kb.din("L1_ffn_w_proj", [D, H1F], F32),
    }
    C = emit_consts(kb, SUB_CONSTS["l1_ffn"])
    emit_l1_ffn(kb, C, x_in, x_out, W)
    kb.finish()
    kb.close()
    return kb


BUILDERS = {
    1: (build_l0_attn_a, SUB_WEIGHTS["l0_attn_a"], SUB_CONSTS["l0_attn_a"]),
    2: (build_l0_ffn, SUB_WEIGHTS["l0_ffn"], SUB_CONSTS["l0_ffn"]),
    4: (build_l0_ffn, SUB_WEIGHTS["l0_ffn"], SUB_CONSTS["l0_ffn"]),
    5: (build_l1_attn_a, SUB_WEIGHTS["l1_attn_a"], SUB_CONSTS["l1_attn_a"]),
    3: (build_l0_attn_b, SUB_WEIGHTS["l0_attn_b"], SUB_CONSTS["l0_attn_b"]),
    6: (build_l1_ffn, SUB_WEIGHTS["l1_ffn"], SUB_CONSTS["l1_ffn"]),
    7: (build_l1_attn_b, SUB_WEIGHTS["l1_attn_b"], SUB_CONSTS["l1_attn_b"]),
    8: (build_l1_ffn, SUB_WEIGHTS["l1_ffn"], SUB_CONSTS["l1_ffn"]),
}


ALL_CONSTS = ["IDENT_BF", "IDENT_F32", "I16", "ONES_ROW", "ONES_COL", "ONES_ROW_F32",
              "ONES_COL_F32", "EPS8_COL", "MASKA", "MASKB", "COS", "SIN", "PROT_T",
              "DECAY_TM", "DECAY_BC"]
SMALL_CONSTS = ["IDENT_BF", "IDENT_F32", "I16", "ONES_ROW", "ONES_COL", "ONES_ROW_F32",
                "ONES_COL_F32", "EPS8_COL", "MASKA", "MASKB", "PROT_T", "DECAY_TM"]
BIG_CONSTS = ["COS", "SIN", "DECAY_BC"]


def build_fused():
    kb = KB()
    x_in = kb.din("x_in", [S, D], F32)
    x_out = kb.dout("x_out", [S, D], F32)
    xa = kb.dtmp("x_a", [S, D], F32)
    xb = kb.dtmp("x_b", [S, D], F32)
    C = emit_consts(kb, SMALL_CONSTS)
    CAPS = {nm: kb.din(nm, *CONST_SPECS[nm]) for nm in BIG_CONSTS}

    def with_big(scope, names):
        d = dict(C)
        for nm in names:
            tile = kb.sb_in(scope, "cb_" + nm, *CONST_SPECS[nm])
            kb.load(tile[:], CAPS[nm])
            d[nm] = tile
        for eng in ("tensor", "vector", "scalar", "gpsimd"):
            kb.wait(eng, "dl")
        return d
    W_aa = {
        "W_K": kb.din("L0_attn_a_W_K", [16, 128, D], F32),
        "W_Q": kb.din("L0_attn_a_W_Q", [16, 128, D], F32),
        "W_V": kb.din("L0_attn_a_W_V", [16, 128, D], F32),
        "W_O": kb.din("L0_attn_a_W_O", [D, D], F32),
    }
    W_mf = {
        "gate_w": kb.din("L0_ffn_gate_w", [8, D], F32),
        "w1": kb.din("L0_ffn_w1", [F0, D], F32),
        "w3": kb.din("L0_ffn_w3", [F0, D], F32),
        "w2": kb.din("L0_ffn_w2", [D, F0], F32),
        "a1": kb.din("L0_ffn_a1", [8, 16, D], F32),
        "b1": kb.din("L0_ffn_b1", [8, F0, 16], F32),
        "a3": kb.din("L0_ffn_a3", [8, 16, D], F32),
        "b3": kb.din("L0_ffn_b3", [8, F0, 16], F32),
        "a2": kb.din("L0_ffn_a2", [8, 16, F0], F32),
        "b2": kb.din("L0_ffn_b2", [8, D, 16], F32),
    }
    W_ab = {
        "Wq": kb.din("L0_attn_b_Wq", [D, D], F32),
        "Wk": kb.din("L0_attn_b_Wk", [D, D], F32),
        "Wv": kb.din("L0_attn_b_Wv", [D, D], F32),
        "Wo": kb.din("L0_attn_b_Wo", [D, D], F32),
        "lambda_q1": kb.din("L0_attn_b_lambda_q1", [64], F32),
        "lambda_k1": kb.din("L0_attn_b_lambda_k1", [64], F32),
        "lambda_q2": kb.din("L0_attn_b_lambda_q2", [64], F32),
        "lambda_k2": kb.din("L0_attn_b_lambda_k2", [64], F32),
        "ln_w": kb.din("L0_attn_b_ln_w", [128], F32),
        "ln_b": kb.din("L0_attn_b_ln_b", [128], F32),
    }
    W_sa = {}
    for short, name, shape in (
        ("Wk", "L1_attn_a_Wk", [D, 128]), ("Wv", "L1_attn_a_Wv", [D, 128]), ("Wq", "L1_attn_a_Wq", [D, 128]),
        ("fast_lambda", "L1_attn_a_fast_lambda", [128]), ("slow_lambda", "L1_attn_a_slow_lambda", [128]),
        ("Waf", "L1_attn_a_Waf", [D, 128]), ("baf", "L1_attn_a_baf", [128]),
        ("Was", "L1_attn_a_Was", [D, 128]), ("bas", "L1_attn_a_bas", [128]),
        ("nm_w1", "L1_attn_a_nm_w1", [128, 256]), ("nm_b1", "L1_attn_a_nm_b1", [256]),
        ("nm_w2", "L1_attn_a_nm_w2", [256, 128]), ("nm_b2", "L1_attn_a_nm_b2", [128]),
        ("ir_w1", "L1_attn_a_ir_w1", [D, 128]), ("ir_b1", "L1_attn_a_ir_b1", [128]),
        ("ir_w2", "L1_attn_a_ir_w2", [128, 1]), ("ir_b2", "L1_attn_a_ir_b2", [1]),
        ("pw_b", "L1_attn_a_pw_b", [3]), ("Wo", "L1_attn_a_Wo", [128, D]),
        ("norm_w", "L1_attn_a_norm_w", [128]),
    ):
        W_sa[short] = kb.din(name, shape, F32)
    W_sa["pw_wT"] = kb.din("L1_attn_a_pw_wT", [3, D], F32)
    W_l1f = {
        "w_gate_up": kb.din("L1_ffn_w_gate_up", [2 * H1F, D], F32),
        "w_proj": kb.din("L1_ffn_w_proj", [D, H1F], F32),
    }
    W_lb = {
        "Wq": kb.din("L1_attn_b_Wq", [D, D], F32),
        "bq": kb.din("L1_attn_b_bq", [D], F32),
        "Wk": kb.din("L1_attn_b_Wk", [D, D], F32),
        "bk": kb.din("L1_attn_b_bk", [D], F32),
        "Wv": kb.din("L1_attn_b_Wv", [D, D], F32),
        "bv": kb.din("L1_attn_b_bv", [D], F32),
        "Wo": kb.din("L1_attn_b_Wo", [D, D], F32),
        "bo": kb.din("L1_attn_b_bo", [D], F32),
        "DECAY_BC": CAPS["DECAY_BC"],
    }
    mf_scr = (kb.dtmp("mf_w1bf", [F0, D], BF16), kb.dtmp("mf_w3bf", [F0, D], BF16),
              kb.dtmp("mf_w2bf", [D, F0], BF16), kb.dtmp("mf_w1T", [D, F0], BF16),
              kb.dtmp("mf_w3T", [D, F0], BF16), kb.dtmp("mf_w2T", [F0, D], BF16))
    l1f_scr = (kb.dtmp("l1f_gu", [2 * H1F, D], BF16), kb.dtmp("l1f_p", [D, H1F], BF16),
               kb.dtmp("l1f_guT", [D, 2 * H1F], BF16), kb.dtmp("l1f_pT", [H1F, D], BF16))

    emit_l0_attn_a(kb, C, x_in, xa, W_aa)
    kb.fence()
    mf_oacc = kb.dtmp("mf_outacc", [S, D], F32)
    emit_l0_ffn_prepass(kb, C, W_mf, mf_scr)
    emit_l0_ffn(kb, C, xa, xb, W_mf, mf_scr, mf_oacc)
    kb.fence()
    with kb.scope() as s_ab:
        emit_l0_attn_b(kb, with_big(s_ab, ["COS", "SIN"]), xb, xa, W_ab)
        kb.fence()
    emit_l0_ffn(kb, C, xa, xb, W_mf, mf_scr, mf_oacc)
    kb.fence()
    emit_l1_attn_a(kb, C, xb, xa, W_sa)
    kb.fence()
    emit_l1_ffn(kb, C, xa, xb, W_l1f, quantize=True, scratch=l1f_scr)
    kb.fence()
    emit_l1_attn_b(kb, C, xb, xa, W_lb)
    kb.fence()
    emit_l1_ffn(kb, C, xa, x_out, W_l1f, quantize=False, scratch=l1f_scr)
    kb.finish()
    kb.close()
    return kb


FUSED = True


def extra_inputs(sub, inputs):
    if sub == 5:
        return {"L1_attn_a_pw_wT": np.ascontiguousarray(inputs["L1_attn_a_pw_w"].T)}
    return {}


def kernel(**inputs):
    _install_memtrim()
    consts = make_consts()
    x = np.ascontiguousarray(inputs["L0_attn_a_x"], dtype=np.float32)
    ncores = 8
    if FUSED:
        kb = build_fused()
        wnames = sorted(set(n for ns in SUB_WEIGHTS.values() for n in ns))
        in_maps = []
        extra = extra_inputs(5, inputs)
        for i in range(ncores):
            m = {"x_in": x[i]}
            for n in wnames:
                m[n] = inputs[n]
            for n in ALL_CONSTS:
                m[n] = consts[n]
            m.update(extra)
            in_maps.append(m)
        res = run_bass_kernel_spmd(kb.nc, in_maps, core_ids=list(range(ncores)))
        return np.ascontiguousarray(np.stack([np.asarray(r["x_out"]) for r in res.results]))
    order = [1, 2, 3, 4, 5, 6, 7, 8]
    built = {}
    for sub in order:
        build, wnames, cnames = BUILDERS[sub]
        if build not in built:
            built[build] = build()
        kb = built[build]
        in_maps = []
        extra = extra_inputs(sub, inputs)
        for i in range(ncores):
            m = {"x_in": x[i]}
            for n in wnames:
                m[n] = inputs[n]
            for n in cnames:
                m[n] = consts[n]
            m.update(extra)
            in_maps.append(m)
        res = run_bass_kernel_spmd(kb.nc, in_maps, core_ids=list(range(ncores)))
        x = np.ascontiguousarray(np.stack([np.asarray(r["x_out"]) for r in res.results]))
    return x
